# version 09
# baseline (speedup 1.0000x reference)
import numpy as np
from contextlib import ExitStack

import concourse.bass as bass
import concourse.mybir as mybir
from concourse.bass_utils import run_bass_kernel_spmd

F32 = mybir.dt.float32
BF16 = mybir.dt.bfloat16
AF = mybir.ActivationFunctionType
ALU = mybir.AluOpType
AXX = mybir.AxisListType.X

P = 128
D = 2048
KT = D // P
TL = 2048
NCH = 4
CW = 512
NC0 = 8
CW0 = 256
BIG = 1.0e30

ENGINES = ("sync", "gpsimd", "tensor", "vector", "scalar")


class Sem:
    def __init__(self, handle, name):
        self.h, self.name, self.n = handle, name, 0

    def inc(self, k=1):
        self.n += k
        return self.n


class Rec:
    def __init__(self):
        self.ops = {e: [] for e in ENGINES}
        self.meta = {e: [] for e in ENGINES}

    def em(self, eng, fn, incs=(), tag=""):
        self.ops[eng].append(fn)
        self.meta[eng].append(("op", tuple(incs), tag))

    def wt(self, eng, sem, val):
        if val and val > 0:
            self.ops[eng].append(lambda e, s=sem.h, v=val: e.wait_ge(s, v))
            self.meta[eng].append(("wait", (sem.name, val), ""))

    def check(self):
        heads = {e: 0 for e in ENGINES}
        sems = {}
        import collections
        progress = True
        while progress:
            progress = False
            for e in ENGINES:
                while heads[e] < len(self.meta[e]):
                    kind, data, tag = self.meta[e][heads[e]]
                    if kind == "wait":
                        name, val = data
                        if sems.get(name, 0) < val:
                            break
                    else:
                        for (name, amt) in data:
                            sems[name] = sems.get(name, 0) + amt
                    heads[e] += 1
                    progress = True
        stuck = {e: heads[e] for e in ENGINES if heads[e] < len(self.meta[e])}
        if stuck:
            print("DEADLOCK. engine heads:")
            for e, h in stuck.items():
                kind, data, tag = self.meta[e][h]
                print(f"  {e}@{h}/{len(self.meta[e])}: waiting {data}"
                      f" (sem now {sems.get(data[0], 0)}) {tag}")
            return False
        print("schedule check OK; final sems:", sems)
        return True


class Pool:

    def __init__(self, rec, tensors):
        self.rec, self.t = rec, tensors
        self.nb = len(tensors)
        self.free_at = [None] * self.nb
        self.held = [False] * self.nb
        self.i = 0

    def acquire(self, hold=False):
        for _ in range(self.nb):
            idx = self.i % self.nb
            self.i += 1
            if not self.held[idx]:
                break
        else:
            raise RuntimeError("no free psum bank")
        f = self.free_at[idx]
        if f is not None:
            self.rec.wt("tensor", f[0], f[1])
            self.free_at[idx] = None
        if hold:
            self.held[idx] = True
        return idx, self.t[idx]

    def release(self, idx, sem, val):
        self.held[idx] = False
        self.free_at[idx] = (sem, val)


def pack_lhsT(WT, msz=128):
    K, M = WT.shape
    kt = K // P
    G = M // msz
    a = np.asarray(WT, dtype=np.float32).reshape(kt, P, G, msz).transpose(2, 1, 0, 3)
    return np.ascontiguousarray(a.reshape(G, P, kt * msz))


def build_program(n_layers=2, taps=(), dbg_split=False):
    nc = bass.Bass("TRN2", target_bir_lowering=False, debug=False,
                   detect_race_conditions=False)
    dt = nc.dram_tensor

    xres0 = dt("xres0", [D, TL], F32, kind="ExternalInput").ap()
    out = dt("out", [D, TL], F32, kind="ExternalOutput").ap()

    wi = {}
    def win(name, shape):
        wi[name] = dt(name, list(shape), F32, kind="ExternalInput").ap()
        return wi[name]

    win("wkT", (KT, P, KT * P)); win("wvT", (KT, P, KT * P))
    win("wrT", (KT, P, KT * P)); win("woT", (KT, P, KT * P))
    win("fc1T", (8, P, KT * P)); win("fc2T", (KT, P, 8 * P))
    win("w1T", (32, 8, P, 8 * P)); win("w2T", (32, 8, P, 8 * P))
    win("upT", (4, 8, P, KT * P)); win("dnT", (KT, P, 32 * P))
    win("gate0T", (1, P, KT * 32))
    win("td_T", (P, KT)); win("tf_T", (P, KT))
    win("gb_col", (32, 1)); win("gb_rep", (P, 32))
    if n_layers > 1:
        win("wiT", (KT, P, KT * P)); win("wfT", (KT, P, KT * P))
        win("wgT", (KT, P, KT * P)); win("wo1T", (KT, P, KT * P))
        win("g1fc1", (8, KT, P, KT * P)); win("g1fc2", (8, KT, P, KT * P))
        win("gate1T", (1, P, KT * 8))
        win("b1P", (P, KT * 8)); win("b2raw", (8, D))
        win("gnw", (P, KT)); win("alpha_rep", (P, 8))
    win("ones_col", (P, 1)); win("ones_row", (1, P))
    win("eye128", (P, P)); win("eye32", (32, 32))
    win("sel_rep", (P, 8)); win("fmask", (P, 1)); win("epsv", (P, 4))

    xnext0 = dt("xnext0", [D, TL], F32, kind="Internal").ap()
    xnext1 = dt("xnext1", [D, TL], F32, kind="Internal").ap() if n_layers > 1 else None
    sp_w = dt("sp_w", [D, TL], BF16, kind="Internal").ap()
    sp_u = dt("sp_u", [D, TL], BF16, kind="Internal").ap()
    sp_r = dt("sp_r", [D, TL], BF16, kind="Internal").ap()
    sp_xn = dt("sp_xn", [D, TL], BF16, kind="Internal").ap() if n_layers > 1 else None
    scw_d = dt("scw_d", [32, TL], BF16, kind="Internal").ap()
    comb_d = dt("comb_d", [8, TL], BF16, kind="Internal").ap() if n_layers > 1 else None
    dbg = {}
    if dbg_split:
        for nm in ("dbg_at", "dbg_sh", "dbg_ro"):
            dbg[nm] = dt(nm, [D, TL], F32, kind="Internal").ap()
    cc_in = [dt(f"cc_in{l}", [P, 32], F32, kind="Internal").ap()
             for l in range(n_layers)]
    cc_out = [dt(f"cc_out{l}", [8 * P, 32], F32, kind="Internal").ap()
              for l in range(n_layers)]

    tap_out = {}
    for t in taps:
        tap_out[t] = dt("tap_" + t, [D, TL], F32, kind="ExternalOutput").ap()

    _names = {}

    def _uname(n):
        k = _names.get(n, 0)
        _names[n] = k + 1
        return n if k == 0 else f"{n}_{k}"

    rec = Rec()
    ctx = ExitStack()
    with ctx:
        e = ctx.enter_context
        qW = Sem(e(nc.semaphore("qW")), "qW")
        qX = Sem(e(nc.semaphore("qX")), "qX")
        pe = Sem(e(nc.semaphore("pe")), "pe")
        dv = Sem(e(nc.semaphore("dv")), "dv")
        ac = Sem(e(nc.semaphore("ac")), "ac")
        ccs = [Sem(e(nc.semaphore(f"cc{l}")), f"cc{l}") for l in range(n_layers)]
        sX = Sem(e(nc.semaphore("sX")), "sX")
        sW = Sem(e(nc.semaphore("sW")), "sW")

        psum = [e(nc.psum_tensor(f"ps{i}", [P, CW], F32)) for i in range(7)]
        ps16 = e(nc.psum_tensor("ps16", [P, CW], BF16))
        pool = Pool(rec, psum)

        WSLOT = 32 * P
        wring = e(nc.sbuf_tensor(_uname("wring"), [P, 3 * WSLOT], BF16))
        stage = e(nc.sbuf_tensor(_uname("stage"), [P, 4 * CW], F32))
        cons = e(nc.sbuf_tensor(_uname("cons"), [P, 256], F32))
        C_ONES, C_DEC, C_FF, C_TF, C_TD = 0, 1, 17, 33, 49
        C_GNW, C_SEL, C_CN, C_FIN, C_ALPHA, C_FM = 65, 81, 96, 128, 160, 255
        C_EPS = 168
        cons16 = e(nc.sbuf_tensor(_uname("cons16"), [P, 224], BF16))
        B_ONES, B_EYE128, B_EYE32 = 0, 64, 192
        onesrow = e(nc.sbuf_tensor(_uname("onesrow"), [1, P], F32))
        onesrow16 = e(nc.sbuf_tensor(_uname("onesrow16"), [1, P], BF16))
        gbcol = e(nc.sbuf_tensor(_uname("gbcol"), [32, 1], F32))
        gbrep = e(nc.sbuf_tensor(_uname("gbrep"), [P, 32], F32))
        g0sb = e(nc.sbuf_tensor(_uname("g0sb"), [P, KT * 32], BF16))
        rows = e(nc.sbuf_tensor(_uname("rows"), [1, NCH * CW], F32))
        abc = e(nc.sbuf_tensor(_uname("abc"), [P, 2 * CW0], F32))
        bbc = e(nc.sbuf_tensor(_uname("bbc"), [P, 2 * CW0], F32))
        agsb = e(nc.sbuf_tensor(_uname("agsb"), [P, 8 * 32], F32))
        eye32f = e(nc.sbuf_tensor(_uname("eye32f"), [32, 32], F32))
        if n_layers > 1:
            g1sb = e(nc.sbuf_tensor(_uname("g1sb"), [P, KT * 8], BF16))
            b1sb = e(nc.sbuf_tensor(_uname("b1sb"), [P, KT * 8], F32))

        ws_state = {"i": 0, "free": [0, 0, 0]}

        def ws_load(dram_ap, n_elems):
            s = ws_state["i"] % 3
            ws_state["i"] += 1
            rec.wt("gpsimd", pe, ws_state["free"][s])
            dst = wring[:, s * WSLOT: s * WSLOT + n_elems]
            rec.em("gpsimd", lambda g, d=dst, a=dram_ap:
                   g.dma_start(out=d, in_=a).then_inc(qW.h, 16),
                   incs=(("qW", 16),))
            return s, qW.inc(16)

        def ws_tile(slot, kk, msz=128):
            off = slot * WSLOT + kk * msz
            return wring[:, off: off + msz]

        def ws_done(slot, pe_val):
            ws_state["free"][slot] = pe_val

        class WSweep:

            def __init__(self, groups):
                self.groups = groups
                self.h = []
                self.i = 0

            def get(self, g):
                while self.i <= min(g + 2, len(self.groups) - 1):
                    ap_, ne_ = self.groups[self.i]
                    self.h.append(ws_load(ap_, ne_))
                    self.i += 1
                return self.h[g]

        def mm(bank, lhsT, rhs, start, stop, inc=False):
            if inc:
                rec.em("tensor", lambda t, b=bank, l=lhsT, r=rhs, s=start, p=stop:
                       t.matmul(b, l, r, start=s, stop=p).then_inc(pe.h, 1),
                       incs=(("pe", 1),))
                return pe.inc()
            rec.em("tensor", lambda t, b=bank, l=lhsT, r=rhs, s=start, p=stop:
                   t.matmul(b, l, r, start=s, stop=p))
            return None

        def dve(fn, inc=True):
            rec.wt("vector", dv, dv.n)
            rec.em("vector", lambda v, f=fn: f(v).then_inc(dv.h, 1),
                   incs=(("dv", 1),))
            return dv.inc()

        def act(fn, inc=True):
            rec.wt("scalar", ac, ac.n)
            rec.em("scalar", lambda s, f=fn: f(s).then_inc(ac.h, 1),
                   incs=(("ac", 1),))
            return ac.inc()

        def sdma(out_ap, in_ap, **kw):
            rec.em("sync", lambda s, o=out_ap, i=in_ap, k=kw:
                   s.dma_start(out=o, in_=i, **k).then_inc(qX.h, 16),
                   incs=(("qX", 16),))
            return qX.inc(16)

        def gdma(out_ap, in_ap, **kw):
            rec.em("gpsimd", lambda g, o=out_ap, i=in_ap, k=kw:
                   g.dma_start(out=o, in_=i, **k).then_inc(qW.h, 16),
                   incs=(("qW", 16),))
            return qW.inc(16)

        _sig = {"sync": (qX, sX, []), "gpsimd": (qW, sW, [])}

        def dsig(queue, upto):
            qsem, ssem, lst = _sig[queue]
            for cov, val in lst:
                if cov >= upto:
                    return ssem, val
            rec.wt(queue, qsem, upto)
            rec.em(queue, lambda e, h=ssem.h: e.nop().then_inc(h, 1),
                   incs=((ssem.name, 1),))
            v = ssem.inc()
            lst.append((upto, v))
            return ssem, v

        def wtd(eng, queue, upto):
            if upto and upto > 0:
                s_, v_ = dsig(queue, upto)
                rec.wt(eng, s_, v_)

        def wt_any(eng, s_, v_):
            if s_ == "dma":
                wtd(eng, "gpsimd", v_)
            elif s_ == "dmas":
                wtd(eng, "sync", v_)
            else:
                rec.wt(eng, s_, v_)

        stage_free = [0, 0, 0, 0]
        stage_i = [0]

        def stage_acc(ps_ap, pe_val, dst_dram_ap, accum=True):
            s = stage_i[0] % 4
            stage_i[0] += 1
            wtd("scalar", "gpsimd", stage_free[s])
            rec.wt("scalar", pe, pe_val)
            slot = stage[:, s * CW:(s + 1) * CW]
            v = act(lambda sc, o=slot, p=ps_ap: sc.copy(out=o, in_=p))
            rec.wt("gpsimd", ac, v)
            kw = dict(accum_op=ALU.add) if accum else {}
            q = gdma(dst_dram_ap, slot, **kw)
            stage_free[s] = q
            return v

        def xdst_tile(xdst, g, n, w=CW):
            return bass.AP(xdst.tensor, g * P * TL + n * w, [[TL, P], [1, w]])

        def qfence():
            rec.wt("gpsimd", qW, qW.n)
            rec.wt("sync", pe, pe.n)
            rec.wt("sync", dv, dv.n)
            rec.wt("sync", ac, ac.n)
            rec.wt("gpsimd", pe, pe.n)
            rec.wt("gpsimd", dv, dv.n)
            rec.wt("gpsimd", ac, ac.n)

        sdma(cons[:, C_ONES:C_ONES + 1], wi["ones_col"])
        sdma(cons[:, C_TD:C_TD + KT], wi["td_T"])
        sdma(cons[:, C_TF:C_TF + KT], wi["tf_T"])
        sdma(cons[:, C_SEL:C_SEL + 8], wi["sel_rep"])
        sdma(onesrow[0:1, :], wi["ones_row"])
        sdma(gbcol[0:32, :], wi["gb_col"])
        sdma(gbrep[:, :], wi["gb_rep"])
        sdma(cons[:, C_FM:C_FM + 1], wi["fmask"])
        sdma(cons[:, C_EPS:C_EPS + 4], wi["epsv"])
        sdma(eye32f[0:32, :], wi["eye32"])
        gdma(cons16[:, B_ONES:B_ONES + 1], wi["ones_col"])
        gdma(cons16[:, B_EYE128:B_EYE128 + P], wi["eye128"])
        gdma(cons16[0:32, B_EYE32:B_EYE32 + 32], wi["eye32"])
        gdma(onesrow16[0:1, :], wi["ones_row"])
        g0_q = gdma(g0sb[:, :], wi["gate0T"][0])
        if n_layers > 1:
            sdma(cons[:, C_GNW:C_GNW + KT], wi["gnw"])
            sdma(cons[:, C_ALPHA:C_ALPHA + 8], wi["alpha_rep"])
            sdma(b1sb[:, :], wi["b1P"])
            g1_q = gdma(g1sb[:, :], wi["gate1T"][0])
        cst_q, cst_w = qX.n, qW.n
        wtd("scalar", "sync", cst_q)
        act(lambda s: s.activation(out=cons[:, C_DEC:C_DEC + KT],
                                   in_=cons[:, C_TD:C_TD + KT], func=AF.Exp),
            inc=False)
        dec_ready = act(lambda s: s.activation(out=cons[:, C_DEC:C_DEC + KT],
                                               in_=cons[:, C_DEC:C_DEC + KT],
                                               func=AF.Exp, scale=-1.0))
        wtd("vector", "sync", cst_q)
        ffm = dve(lambda v: v.tensor_scalar(out=cons[:, C_FF:C_FF + KT],
                                            in0=cons[:, C_TF:C_TF + KT],
                                            scalar1=cons[:, C_FM:C_FM + 1],
                                            scalar2=None, op0=ALU.mult))
        rec.wt("scalar", dv, ffm)
        ff_ready = act(lambda s: s.activation(out=cons[:, C_FF:C_FF + KT],
                                              in_=cons[:, C_FF:C_FF + KT],
                                              func=AF.Exp))
        gdma(xnext0, xres0)

        finals = cons[:, C_FIN:C_FIN + 32]
        carry = cons[:, C_CN:C_CN + 32]

        def proj_sweep(wgroups, kt, rhs_fn, evac_fn, m_list=None, ready=None):
            for g in (m_list if m_list is not None else range(len_w(wgroups))):
                sl, lv = ws_load(wgroups(g) if callable(wgroups) else wgroups[g],
                                 kt * P)
                wtd("tensor", "gpsimd", lv)
                if ready is not None:
                    rec.wt("tensor", ready[0], ready[1])
                for n in range(NCH):
                    bi, bk = pool.acquire()
                    for kk in range(kt):
                        mm(bk[:, :], ws_tile(sl, kk), rhs_fn(kk, n),
                           kk == 0, kk == kt - 1, inc=(kk == kt - 1))
                    sem, val = evac_fn(g, n, bk, pe.n)
                    pool.release(bi, sem, val)
                ws_done(sl, pe.n)

        def len_w(x):
            raise RuntimeError

        def emit_layer(l):
            xsrc = xres0 if l == 0 else xnext0
            xdst = xnext0 if l == 0 else xnext1
            cin, cout, cc = cc_in[l], cc_out[l], ccs[l]

            lay = ExitStack()
            with lay:
                le = lay.enter_context
                if l == 0:
                    xlat = le(nc.sbuf_tensor(_uname("xlat"), [P, 8 * TL], BF16))
                else:
                    logits_sb = le(nc.sbuf_tensor(_uname("logits1"), [8, TL], F32))
                    combT = le(nc.sbuf_tensor(_uname("combT"), [8, TL], BF16))

                xn_ctx = ExitStack()
                with xn_ctx:
                    xn = xn_ctx.enter_context(
                        nc.sbuf_tensor(_uname("xn"), [P, KT * TL], BF16))

                    def xn_t(kk, n, w=CW):
                        return xn[:, kk * TL + n * w: kk * TL + n * w + w]

                    p0 = ExitStack()
                    with p0:
                        pe0 = p0.enter_context
                        xch = [pe0(nc.sbuf_tensor(_uname(f"xch{i}"),
                                                  [P, KT * CW0], F32))
                               for i in range(2)]
                        xb16 = pe0(nc.sbuf_tensor(_uname("xb16"), [P, KT * CW0],
                                                  BF16))
                        xs16 = pe0(nc.sbuf_tensor(_uname("xs16"), [P, KT * CW0],
                                                  BF16))
                        if l > 0:
                            qfence()
                            wtd("sync", "gpsimd", qW.n)
                        xn_wr = [0, 0]
                        sq_prev = 0
                        xb_prev = 0
                        stats_prev = 0
                        for n in range(NC0):
                            j = n % 2
                            xc = xch[j]
                            src = bass.AP(xsrc.tensor, n * CW0,
                                          [[TL, P], [P * TL, KT], [1, CW0]])
                            rec.wt("sync", dv, xn_wr[j])
                            rec.wt("sync", ac, sq_prev)
                            lq = sdma(xc[:, :], src)
                            wtd("scalar", "sync", lq)
                            rec.wt("scalar", pe, stats_prev)
                            sq_prev = act(lambda s, x=xc: s.activation(
                                out=xs16[:, :], in_=x[:, :], func=AF.Square))
                            xb_prev = act(lambda s, x=xc: s.copy(out=xb16[:, :],
                                                                 in_=x[:, :]))
                            i1, b1 = pool.acquire()
                            i2, b2 = pool.acquire()
                            rec.wt("tensor", ac, xb_prev)
                            wtd("tensor", "gpsimd", cst_w)
                            for kk in range(KT):
                                mm(b1[0:1, 0:CW0], cons16[:, B_ONES:B_ONES + 1],
                                   xb16[:, kk * CW0:(kk + 1) * CW0],
                                   kk == 0, kk == KT - 1)
                            for kk in range(KT):
                                mm(b2[0:1, 0:CW0], cons16[:, B_ONES:B_ONES + 1],
                                   xs16[:, kk * CW0:(kk + 1) * CW0],
                                   kk == 0, kk == KT - 1,
                                   inc=(kk == KT - 1))
                            stats_prev = pe.n
                            mrow = rows[0:1, 0:CW0]
                            vrow = rows[0:1, CW0:2 * CW0]
                            rrow = rows[0:1, 2 * CW0:3 * CW0]
                            brow = rows[0:1, 3 * CW0:4 * CW0]
                            rec.wt("vector", pe, stats_prev)
                            dve(lambda v, b=b1: v.tensor_scalar(
                                out=mrow, in0=b[0:1, 0:CW0], scalar1=1.0 / D,
                                scalar2=None, op0=ALU.mult), inc=False)
                            dve(lambda v, b=b2: v.tensor_scalar(
                                out=vrow, in0=b[0:1, 0:CW0], scalar1=1.0 / D,
                                scalar2=None, op0=ALU.mult), inc=False)
                            dve(lambda v: v.tensor_tensor(out=brow, in0=mrow,
                                                          in1=mrow,
                                                          op=ALU.mult), inc=False)
                            vv = dve(lambda v: v.tensor_tensor(out=vrow, in0=vrow,
                                                               in1=brow,
                                                               op=ALU.subtract))
                            pool.release(i1, dv, vv)
                            pool.release(i2, dv, vv)
                            rec.wt("scalar", dv, vv)
                            act(lambda s: s.activation(
                                out=rrow, in_=vrow, func=AF.Ln,
                                bias=cons[0:1, C_EPS:C_EPS + 1]), inc=False)
                            rw = act(lambda s: s.activation(out=rrow, in_=rrow,
                                                            func=AF.Exp, scale=-0.5))
                            rec.wt("vector", ac, rw)
                            bw = dve(lambda v: v.tensor_tensor(out=brow, in0=mrow,
                                                               in1=rrow,
                                                               op=ALU.mult))
                            i3, b3 = pool.acquire()
                            i4, b4 = pool.acquire()
                            rec.wt("tensor", ac, rw)
                            av = mm(b3[:, 0:CW0], onesrow[0:1, :], rrow, True, True,
                                    inc=True)
                            rec.wt("tensor", dv, bw)
                            bv = mm(b4[:, 0:CW0], onesrow[0:1, :], brow, True, True,
                                    inc=True)
                            abj = abc[:, (j) * CW0:(j + 1) * CW0]
                            bbj = bbc[:, (j) * CW0:(j + 1) * CW0]
                            rec.wt("scalar", pe, av)
                            rec.wt("scalar", dv, xn_wr[j])
                            a_c = act(lambda s, b=b3, o=abj: s.copy(
                                out=o, in_=b[:, 0:CW0]))
                            rec.wt("scalar", pe, bv)
                            b_c = act(lambda s, b=b4, o=bbj: s.copy(
                                out=o, in_=b[:, 0:CW0]))
                            pool.release(i3, ac, a_c)
                            pool.release(i4, ac, b_c)
                            rec.wt("vector", ac, b_c)
                            for kk in range(KT):
                                dve(lambda v, k=kk, x=xc, o=bbj: v.tensor_tensor(
                                    out=x[:, k * CW0:(k + 1) * CW0],
                                    in0=x[:, k * CW0:(k + 1) * CW0],
                                    in1=o, op=ALU.subtract), inc=False)
                                xn_wr[j] = dve(lambda v, k=kk, nn=n, x=xc, o=abj:
                                               v.tensor_tensor(
                                                   out=xn_t(k, nn, CW0),
                                                   in0=x[:, k * CW0:(k + 1) * CW0],
                                                   in1=o, op=ALU.mult))
                        xn_done = max(xn_wr)
                    if l == 0 and "xn" in tap_out:
                        rec.wt("gpsimd", dv, xn_done)
                        gdma(tap_out["xn"], xn[:, :])
                    if l > 0:
                        rec.wt("sync", dv, xn_done)
                        xn_dump_q = sdma(
                            bass.AP(sp_xn.tensor, 0,
                                    [[TL, P], [P * TL, KT], [1, TL]]),
                            xn[:, :])

                    p1 = ExitStack()
                    with p1:
                        pe1 = p1.enter_context
                        if l == 0:
                            wm = pe1(nc.sbuf_tensor(_uname("wm"), [P, TL], F32))
                            wex = [pe1(nc.sbuf_tensor(_uname(f"wex{i}"), [P, TL], BF16))
                                   for i in range(2)]
                            um = [pe1(nc.sbuf_tensor(_uname(f"um{i}"), [P, TL], BF16))
                                  for i in range(2)]
                            srm = pe1(nc.sbuf_tensor(_uname("srm"), [P, TL], BF16))
                            scr = pe1(nc.sbuf_tensor(_uname("scr"), [P, TL], F32))
                            sw1 = WSweep(
                                [(wi[w][m], KT * P) for m in range(KT)
                                 for w in ("wkT", "wvT", "wrT")])
                            wex_free = [[], []]
                            um_free = [[], []]
                            srm_free = []
                            wm_free = []
                            for m in range(KT):
                                j = m % 2
                                dcol = cons[:, C_DEC + m:C_DEC + m + 1]
                                sl, lv = sw1.get(3 * m)
                                wtd("tensor", "gpsimd", lv)
                                if m == 0:
                                    rec.wt("tensor", dv, xn_done)
                                banks = []
                                for n in range(NCH):
                                    bi, bk = pool.acquire()
                                    for kk in range(KT):
                                        mm(bk[:, :], ws_tile(sl, kk), xn_t(kk, n),
                                           kk == 0, kk == KT - 1, inc=(kk == KT - 1))
                                    banks.append((bi, bk, pe.n))
                                ws_done(sl, pe.n)
                                for s_, v_ in wm_free:
                                    wt_any("vector", s_, v_)
                                wm_free = []
                                for n, (bi, bk, pv) in enumerate(banks):
                                    rec.wt("vector", pe, pv)
                                    cv = dve(lambda v, b=bk, n_=n: v.tensor_scalar(
                                        out=wm[:, n_ * CW:(n_ + 1) * CW], in0=b[:, :],
                                        scalar1=-10.0, scalar2=10.0,
                                        op0=ALU.max, op1=ALU.min))
                                    pool.release(bi, dv, cv)
                                clip_v = dv.n
                                wb = wex[j]
                                for s_, v_ in wex_free[j]:
                                    wt_any("scalar", s_, v_)
                                wex_free[j] = []
                                rec.wt("scalar", dv, clip_v)
                                rec.wt("scalar", ac, ff_ready)
                                ev = act(lambda s, w=wb: s.activation(
                                    out=w[:, :], in_=wm[:, :], func=AF.Exp))
                                wm_free = [(ac, ev)]
                                rec.wt("vector", ac, ev)
                                fv = dve(lambda v, w=wb, m_=m: v.tensor_tensor(
                                    out=w[:, 0:1], in0=w[:, 0:1],
                                    in1=cons[:, C_FF + m_:C_FF + m_ + 1],
                                    op=ALU.mult))
                                rec.wt("sync", dv, fv)
                                wq = sdma(sp_w[m * P:(m + 1) * P, :], wb[:, :])
                                d_bc = bass.broadcast_tensor_aps(wb[:, :], dcol)[1]
                                rec.wt("vector", ac, dec_ready)
                                dve(lambda v, w=wb, db=d_bc: v.tensor_tensor_scan(
                                    out=scr[:, :], data0=db, data1=w[:, :],
                                    initial=0.0, op0=ALU.mult, op1=ALU.add),
                                    inc=False)
                                dsv = dve(lambda v, m_=m: v.tensor_copy(
                                    out=finals[:, 16 + m_:17 + m_],
                                    in_=scr[:, TL - 1:TL]))
                                sl2, lv2 = sw1.get(3 * m + 1)
                                wtd("tensor", "gpsimd", lv2)
                                banks = []
                                for n in range(NCH):
                                    bi, bk = pool.acquire()
                                    for kk in range(KT):
                                        mm(bk[:, :], ws_tile(sl2, kk), xn_t(kk, n),
                                           kk == 0, kk == KT - 1, inc=(kk == KT - 1))
                                    banks.append((bi, bk, pe.n))
                                ws_done(sl2, pe.n)
                                ub = um[j]
                                for s_, v_ in um_free[j]:
                                    wt_any("vector", s_, v_)
                                um_free[j] = []
                                for n, (bi, bk, pv) in enumerate(banks):
                                    rec.wt("vector", pe, pv)
                                    uv = dve(lambda v, b=bk, n_=n, w=wb, u=ub:
                                             v.tensor_tensor(
                                                 out=u[:, n_ * CW:(n_ + 1) * CW],
                                                 in0=w[:, n_ * CW:(n_ + 1) * CW],
                                                 in1=b[:, :], op=ALU.mult))
                                    pool.release(bi, dv, uv)
                                u_v = dv.n
                                rec.wt("sync", dv, u_v)
                                uq = sdma(sp_u[m * P:(m + 1) * P, :], ub[:, :])
                                wex_free[j] = [("dmas", wq), (dv, u_v)]
                                dve(lambda v, u=ub, db=d_bc: v.tensor_tensor_scan(
                                    out=scr[:, :], data0=db, data1=u[:, :],
                                    initial=0.0, op0=ALU.mult, op1=ALU.add),
                                    inc=False)
                                nsv = dve(lambda v, m_=m: v.tensor_copy(
                                    out=finals[:, m_:m_ + 1], in_=scr[:, TL - 1:TL]))
                                um_free[j] = [("dmas", uq), (dv, nsv)]
                                sl3, lv3 = sw1.get(3 * m + 2)
                                wtd("tensor", "gpsimd", lv3)
                                for s_, v_ in srm_free:
                                    wt_any("scalar", s_, v_)
                                srm_free = []
                                for n in range(NCH):
                                    bi, bk = pool.acquire()
                                    for kk in range(KT):
                                        mm(bk[:, :], ws_tile(sl3, kk), xn_t(kk, n),
                                           kk == 0, kk == KT - 1, inc=(kk == KT - 1))
                                    rv = act(lambda s, b=bk, n_=n, pv=pe.n:
                                             (s.wait_ge(pe.h, pv),
                                              s.activation(
                                                  out=srm[:, n_ * CW:(n_ + 1) * CW],
                                                  in_=b[:, :], func=AF.Sigmoid))[1]
                                             ) if False else None
                                    rec.wt("scalar", pe, pe.n)
                                    rv = act(lambda s, b=bk, n_=n: s.activation(
                                        out=srm[:, n_ * CW:(n_ + 1) * CW],
                                        in_=b[:, :], func=AF.Sigmoid))
                                    pool.release(bi, ac, rv)
                                ws_done(sl3, pe.n)
                                rec.wt("sync", ac, ac.n)
                                rq = sdma(sp_r[m * P:(m + 1) * P, :], srm[:, :])
                                srm_free = [("dmas", rq)]
                            scan1_done = dv.n
                        else:
                            am = [pe1(nc.sbuf_tensor(_uname(f"am{i}"), [P, TL], BF16))
                                  for i in range(2)]
                            sgm = pe1(nc.sbuf_tensor(_uname("sgm"), [P, TL], BF16))
                            iim = [pe1(nc.sbuf_tensor(_uname(f"iim{i}"), [P, TL], BF16))
                                   for i in range(2)]
                            scr = pe1(nc.sbuf_tensor(_uname("scr1"), [P, TL], F32))
                            sw1b = WSweep(
                                [(wi[w][m], KT * P) for m in range(KT)
                                 for w in ("wfT", "wiT", "wgT")])
                            am_free = [[], []]
                            iim_free = [[], []]
                            sgm_free = []
                            for m in range(KT):
                                j = m % 2
                                sl, lv = sw1b.get(3 * m)
                                wtd("tensor", "gpsimd", lv)
                                if m == 0:
                                    rec.wt("tensor", dv, xn_done)
                                ab = am[j]
                                for s_, v_ in am_free[j]:
                                    wt_any("scalar", s_, v_)
                                am_free[j] = []
                                for s_, v_ in sgm_free:
                                    wt_any("scalar", s_, v_)
                                sgm_free = []
                                for n in range(NCH):
                                    bi, bk = pool.acquire()
                                    for kk in range(KT):
                                        mm(bk[:, :], ws_tile(sl, kk), xn_t(kk, n),
                                           kk == 0, kk == KT - 1, inc=(kk == KT - 1))
                                    rec.wt("scalar", pe, pe.n)
                                    act(lambda s, b=bk, n_=n, a_=ab: s.activation(
                                        out=a_[:, n_ * CW:(n_ + 1) * CW],
                                        in_=b[:, :], func=AF.Sigmoid), inc=False)
                                    gv = act(lambda s, b=bk, n_=n: s.activation(
                                        out=sgm[:, n_ * CW:(n_ + 1) * CW],
                                        in_=b[:, :], func=AF.Sigmoid, scale=-1.0))
                                    pool.release(bi, ac, gv)
                                a_done = ac.n
                                ws_done(sl, pe.n)
                                rec.wt("sync", ac, a_done)
                                aq = sdma(sp_w[m * P:(m + 1) * P, :], ab[:, :])
                                sl2, lv2 = sw1b.get(3 * m + 1)
                                wtd("tensor", "gpsimd", lv2)
                                ib = iim[j]
                                for s_, v_ in iim_free[j]:
                                    wt_any("scalar", s_, v_)
                                iim_free[j] = []
                                for n in range(NCH):
                                    bi, bk = pool.acquire()
                                    for kk in range(KT):
                                        mm(bk[:, :], ws_tile(sl2, kk), xn_t(kk, n),
                                           kk == 0, kk == KT - 1, inc=(kk == KT - 1))
                                    rec.wt("scalar", pe, pe.n)
                                    sv_ = act(lambda s, b=bk, n_=n, i_=ib:
                                              s.activation(
                                                  out=i_[:, n_ * CW:(n_ + 1) * CW],
                                                  in_=b[:, :], func=AF.Silu))
                                    pool.release(bi, ac, sv_)
                                ws_done(sl2, pe.n)
                                rec.wt("vector", ac, ac.n)
                                iv = dve(lambda v, i_=ib: v.tensor_tensor(
                                    out=i_[:, :], in0=i_[:, :], in1=sgm[:, :],
                                    op=ALU.mult))
                                rec.wt("sync", dv, iv)
                                iq = sdma(sp_u[m * P:(m + 1) * P, :], ib[:, :])
                                dve(lambda v, a_=ab, i_=ib: v.tensor_tensor_scan(
                                    out=scr[:, :], data0=a_[:, :], data1=i_[:, :],
                                    initial=0.0, op0=ALU.mult, op1=ALU.add),
                                    inc=False)
                                fsv = dve(lambda v, m_=m: v.tensor_copy(
                                    out=finals[:, m_:m_ + 1], in_=scr[:, TL - 1:TL]))
                                am_free[j] = [("dmas", aq), (dv, fsv)]
                                iim_free[j] = [("dmas", iq), (dv, fsv)]
                                sl3, lv3 = sw1b.get(3 * m + 2)
                                wtd("tensor", "gpsimd", lv3)
                                rec.wt("scalar", dv, iv)
                                for n in range(NCH):
                                    bi, bk = pool.acquire()
                                    for kk in range(KT):
                                        mm(bk[:, :], ws_tile(sl3, kk), xn_t(kk, n),
                                           kk == 0, kk == KT - 1, inc=(kk == KT - 1))
                                    rec.wt("scalar", pe, pe.n)
                                    gv2 = act(lambda s, b=bk, n_=n: s.activation(
                                        out=sgm[:, n_ * CW:(n_ + 1) * CW],
                                        in_=b[:, :], func=AF.Silu))
                                    pool.release(bi, ac, gv2)
                                ws_done(sl3, pe.n)
                                rec.wt("sync", ac, ac.n)
                                gq = sdma(sp_r[m * P:(m + 1) * P, :], sgm[:, :])
                                sgm_free = [("dmas", gq)]
                            scan1_done = dv.n
                            wtd("tensor", "gpsimd", g1_q)
                            for n in range(NCH):
                                bi, bk = pool.acquire()
                                for kk in range(KT):
                                    mm(bk[0:8, :], g1sb[:, kk * 8:(kk + 1) * 8],
                                       xn_t(kk, n), kk == 0, kk == KT - 1,
                                       inc=(kk == KT - 1))
                                rec.wt("scalar", pe, pe.n)
                                lv_ = act(lambda s, b=bk, n_=n: s.copy(
                                    out=logits_sb[0:8, n_ * CW:(n_ + 1) * CW],
                                    in_=b[0:8, :]))
                                pool.release(bi, ac, lv_)
                            logits_done = ac.n
                            emit_l1_routing(logits_sb, logits_done, combT)

                    rec.wt("gpsimd", dv, scan1_done)
                    gdma(cin, finals)
                    rec.wt("gpsimd", qW, qW.n)
                    rec.em("gpsimd", lambda g, i=cin, o=cout, c=cc.h:
                           g.collective_compute(
                               "AllGather", ALU.bypass,
                               replica_groups=[list(range(8))],
                               ins=[i.opt()], outs=[o.opt()]).then_inc(c),
                           incs=((cc.name, 1),))
                    cc.inc()

                    def consume_carry():
                        rec.wt("gpsimd", cc, 1)
                        ag_ap = bass.AP(cout.tensor, 0,
                                        [[32, P], [P * 32, 8], [1, 32]])
                        agq = gdma(agsb[:, :], ag_ap)
                        wtd("vector", "gpsimd", agq)
                        sel3 = cons[:, C_SEL:C_SEL + 8].rearrange(
                            "p (r o) -> p r o", o=1)
                        ag3 = agsb[:, :].rearrange("p (r c) -> p r c", r=8)
                        sel_b = bass.broadcast_tensor_aps(ag3, sel3)[1]
                        dve(lambda v: v.tensor_tensor(out=ag3, in0=ag3, in1=sel_b,
                                                      op=ALU.mult), inc=False)
                        red_in = bass.AP(agsb[:, :].tensor, agsb[:, :].offset,
                                         [agsb[:, :].ap[0], [1, 32], [32, 8]])
                        carry3 = carry.rearrange("p (c o) -> p c o", o=1)
                        return dve(lambda v: v.tensor_reduce(
                            out=carry3, in_=red_in, axis=AXX, op=ALU.add))

                    if l == 0:
                        emit_l0_p3a(xn_t, xn_done, xlat)
                        xn_last_pe = pe.n
                    else:
                        xn_last_pe = pe.n
                if l == 0:
                    carry_v = consume_carry()
                    emit_l0_p3b(xlat, xdst, carry_v, xn_last_pe)
                    emit_l0_p4(xlat, xdst)
                    wtd("sync", "gpsimd", qW.n)
                    x1cp = [sdma(bass.AP(xnext1.tensor, q_ * (D // 4) * TL,
                                         [[TL, D // 4], [1, TL]]),
                                 bass.AP(xnext0.tensor, q_ * (D // 4) * TL,
                                         [[TL, D // 4], [1, TL]]))
                            for q_ in range(4)]
                    state["x1cp_q"] = x1cp[-1]
                    if "x1" in tap_out:
                        rec.wt("gpsimd", qW, qW.n)
                        gdma(tap_out["x1"], xnext0)
                else:
                    emit_l1_p3(logits_sb, logits_done, combT, xdst, consume_carry,
                               xn_last_pe)
                    emit_l1_p4(combT, xdst, xn_dump_q)
            return None

        def emit_l0_p3a(xn_t, xn_done, xlat):
            p3 = ExitStack()
            with p3:
                pe3 = p3.enter_context
                scores = pe3(nc.sbuf_tensor(_uname("scores"), [32, TL], F32))
                sbbf = scores
                tm = pe3(nc.sbuf_tensor(_uname("tm"), [P, KT * 32], F32))
                tms = pe3(nc.sbuf_tensor(_uname("tms"), [P, KT * 32], F32))
                sel8 = pe3(nc.sbuf_tensor(_uname("sel8"), [P, KT * 32], F32))
                t4 = pe3(nc.sbuf_tensor(_uname("t4"), [P, KT * 32], F32))
                red1 = pe3(nc.sbuf_tensor(_uname("red1"), [P, KT * 4], F32))
                red2 = pe3(nc.sbuf_tensor(_uname("red2"), [P, KT * 4], F32))
                scw16 = pe3(nc.sbuf_tensor(_uname("scw16"), [P, KT * 32], BF16))
                scwT = pe3(nc.sbuf_tensor(_uname("scwT"), [32, TL], BF16))
                up16 = pe3(nc.sbuf_tensor(_uname("up16"), [P, 8 * TL], BF16))
                relu_t = pe3(nc.sbuf_tensor(_uname("relu_t"), [P, 2 * CW], F32))

                wtd("tensor", "gpsimd", g0_q)
                rec.wt("tensor", dv, xn_done)
                for n in range(NCH):
                    bi, bk = pool.acquire()
                    for kk in range(KT):
                        mm(bk[0:32, :], g0sb[:, kk * 32:(kk + 1) * 32],
                           xn_t(kk, n), kk == 0, kk == KT - 1, inc=(kk == KT - 1))
                    rec.wt("scalar", pe, pe.n)
                    sv_ = act(lambda s, b=bk, n_=n: s.activation(
                        out=scores[0:32, n_ * CW:(n_ + 1) * CW],
                        in_=b[0:32, :], func=AF.Sigmoid))
                    pool.release(bi, ac, sv_)
                rec.wt("vector", ac, ac.n)
                wtd("vector", "sync", cst_q)
                sb16v = dve(lambda v: v.tensor_scalar(
                    out=sbbf[0:32, :], in0=scores[0:32, :],
                    scalar1=gbcol[0:32, 0:1], scalar2=None, op0=ALU.add))
                rec.wt("gpsimd", qW, cst_w)
                prev_copy = 0
                for rnd in range(4):
                    bi_t, bk_t = pool.acquire()
                    rec.wt("tensor", dv, sb16v)
                    rec.wt("tensor", ac, prev_copy)
                    for c in range(4):
                        cg = rnd * 4 + c
                        inc = c == 3
                        rec.em("tensor", lambda t, c_=c, cg_=cg, b=bk_t, i_=inc:
                               (t.transpose(b[:, c_ * 32:(c_ + 1) * 32],
                                            sbbf[0:32, cg_ * P:(cg_ + 1) * P],
                                            eye32f[0:32, 0:32]).then_inc(pe.h, 1)
                                if i_ else
                                t.transpose(b[:, c_ * 32:(c_ + 1) * 32],
                                            sbbf[0:32, cg_ * P:(cg_ + 1) * P],
                                            eye32f[0:32, 0:32])),
                               incs=((("pe", 1),) if inc else ()))
                    tv = pe.inc()
                    rec.wt("scalar", pe, tv)
                    prev_copy = act(lambda s, r=rnd, b=bk_t: s.copy(
                        out=tm[:, r * 128:(r + 1) * 128], in_=b[:, 0:128]))
                    pool.release(bi_t, ac, prev_copy)
                rec.wt("vector", ac, prev_copy)
                tm3 = tm[:, :].rearrange("p (c e) -> p c e", e=32)
                gb3 = gbrep[:, :].rearrange("p (o e) -> p o e", o=1)
                gb_b = bass.broadcast_tensor_aps(tm3, gb3)[1]
                tms3 = tms[:, :].rearrange("p (c e) -> p c e", e=32)
                dve(lambda v: v.tensor_tensor(out=tms3, in0=tm3, in1=gb_b,
                                              op=ALU.subtract), inc=False)
                tm4 = tm[:, :].rearrange("p (c g e) -> p c g e", g=4, e=8)
                r14 = red1[:, :].rearrange("p (c g o) -> p c g o", g=4, o=1)
                dve(lambda v: v.tensor_reduce(out=r14, in_=tm4, axis=AXX,
                                              op=ALU.max), inc=False)
                r1b = bass.broadcast_tensor_aps(tm4, r14)[1]
                t44 = t4[:, :].rearrange("p (c g e) -> p c g e", g=4, e=8)
                dve(lambda v: v.tensor_tensor(out=t44, in0=tm4, in1=r1b,
                                              op=ALU.is_equal), inc=False)
                dve(lambda v: v.scalar_tensor_tensor(out=t44, in0=t44, scalar=-BIG,
                                                     in1=tm4, op0=ALU.mult,
                                                     op1=ALU.add), inc=False)
                r24 = red2[:, :].rearrange("p (c g o) -> p c g o", g=4, o=1)
                dve(lambda v: v.tensor_reduce(out=r24, in_=t44, axis=AXX,
                                              op=ALU.max), inc=False)
                dve(lambda v: v.tensor_tensor(out=red1[:, :], in0=red1[:, :],
                                              in1=red2[:, :], op=ALU.add), inc=False)
                gs3 = red1[:, :].rearrange("p (c g) -> p c g", g=4)
                g14 = red2[:, 0:KT].rearrange("p (c o) -> p c o", o=1)
                dve(lambda v: v.tensor_reduce(out=g14, in_=gs3, axis=AXX,
                                              op=ALU.max), inc=False)
                g1b = bass.broadcast_tensor_aps(gs3, g14)[1]
                e1g = t4[:, 0:64].rearrange("p (c g) -> p c g", g=4)
                dve(lambda v: v.tensor_tensor(out=e1g, in0=gs3, in1=g1b,
                                              op=ALU.is_equal), inc=False)
                dve(lambda v: v.scalar_tensor_tensor(out=gs3, in0=e1g, scalar=-BIG,
                                                     in1=gs3, op0=ALU.mult,
                                                     op1=ALU.add), inc=False)
                g24 = red2[:, KT:2 * KT].rearrange("p (c o) -> p c o", o=1)
                dve(lambda v: v.tensor_reduce(out=g24, in_=gs3, axis=AXX,
                                              op=ALU.max), inc=False)
                g2b = bass.broadcast_tensor_aps(gs3, g24)[1]
                dve(lambda v: v.tensor_tensor(out=gs3, in0=gs3, in1=g2b,
                                              op=ALU.is_equal), inc=False)
                dve(lambda v: v.tensor_tensor(out=gs3, in0=gs3, in1=e1g,
                                              op=ALU.add), inc=False)
                gm4 = red1[:, :].rearrange("p (c g o) -> p c g o", g=4, o=1)
                gm_b = bass.broadcast_tensor_aps(tm4, gm4)[1]
                dve(lambda v: v.tensor_scalar(out=t44, in0=gm_b, scalar1=BIG,
                                              scalar2=-BIG, op0=ALU.mult,
                                              op1=ALU.add), inc=False)
                dve(lambda v: v.tensor_tensor(out=t44, in0=t44, in1=tm4,
                                              op=ALU.add), inc=False)
                tmm = t4[:, :].rearrange("p (c e) -> p c e", e=32)
                mi = red2[:, 0:KT].rearrange("p (c o) -> p c o", o=1)
                sel3_ = sel8[:, :].rearrange("p (c e) -> p c e", e=32)
                eq3 = tm[:, :].rearrange("p (c e) -> p c e", e=32)
                for it in range(8):
                    dve(lambda v: v.tensor_reduce(out=mi, in_=tmm, axis=AXX,
                                                  op=ALU.max), inc=False)
                    mib = bass.broadcast_tensor_aps(tmm, mi)[1]
                    if it == 0:
                        dve(lambda v: v.tensor_tensor(out=sel3_, in0=tmm, in1=mib,
                                                      op=ALU.is_equal), inc=False)
                        dve(lambda v: v.scalar_tensor_tensor(
                            out=tmm, in0=sel3_, scalar=-BIG, in1=tmm,
                            op0=ALU.mult, op1=ALU.add), inc=False)
                    else:
                        dve(lambda v: v.tensor_tensor(out=eq3, in0=tmm, in1=mib,
                                                      op=ALU.is_equal), inc=False)
                        dve(lambda v: v.tensor_tensor(out=sel3_, in0=sel3_, in1=eq3,
                                                      op=ALU.add), inc=False)
                        dve(lambda v: v.scalar_tensor_tensor(
                            out=tmm, in0=eq3, scalar=-BIG, in1=tmm,
                            op0=ALU.mult, op1=ALU.add), inc=False)
                dve(lambda v: v.tensor_tensor(out=tms3, in0=tms3, in1=sel3_,
                                              op=ALU.mult), inc=False)
                ssum = red2[:, 0:KT].rearrange("p (c o) -> p c o", o=1)
                dve(lambda v: v.tensor_reduce(out=ssum, in_=tms3, axis=AXX,
                                              op=ALU.add), inc=False)
                dve(lambda v: v.reciprocal(out=red2[:, 0:KT], in_=red2[:, 0:KT]),
                    inc=False)
                sb_b = bass.broadcast_tensor_aps(tms3, ssum)[1]
                dve(lambda v: v.tensor_tensor(out=tms3, in0=tms3, in1=sb_b,
                                              op=ALU.mult), inc=False)
                cwv = dve(lambda v: v.tensor_scalar(out=tms[:, :], in0=tms[:, :],
                                                    scalar1=2.5, scalar2=None,
                                                    op0=ALU.mult))
                rec.wt("scalar", dv, cwv)
                act(lambda s: s.activation(out=tms[:, :], in_=tms[:, :], func=AF.Ln,
                                           bias=cons[:, C_EPS + 2:C_EPS + 3]), inc=False)
                scv = act(lambda s: s.activation(out=tms[:, :], in_=tms[:, :],
                                                 func=AF.Exp, scale=0.5))
                rec.wt("vector", ac, scv)
                s16v = dve(lambda v: v.tensor_copy(out=scw16[:, :], in_=tms[:, :]))
                prev_copy2 = 0
                for rnd in range(4):
                    rec.wt("tensor", dv, s16v)
                    rec.wt("tensor", ac, prev_copy2)
                    for c in range(4):
                        cg = rnd * 4 + c
                        src = bass.AP(scw16[:, :].tensor,
                                      scw16[:, :].offset + cg * 32,
                                      [scw16[:, :].ap[0], [1, 32]])
                        if c < 3:
                            rec.em("tensor", lambda t, c_=c, s_=src:
                                   t.transpose(ps16[0:32, c_ * 128:(c_ + 1) * 128],
                                               s_, cons16[:, B_EYE128:B_EYE128 + P]))
                        else:
                            rec.em("tensor", lambda t, c_=c, s_=src:
                                   t.transpose(ps16[0:32, c_ * 128:(c_ + 1) * 128],
                                               s_, cons16[:, B_EYE128:B_EYE128 + P])
                                   .then_inc(pe.h, 1), incs=(("pe", 1),))
                            tv = pe.inc()
                    rec.wt("scalar", pe, tv)
                    prev_copy2 = act(lambda s, r=rnd: s.copy(
                        out=scwT[0:32, r * CW:(r + 1) * CW], in_=ps16[0:32, 0:CW]))
                rec.wt("sync", ac, prev_copy2)
                scw_q = sdma(scw_d, scwT[0:32, :])
                state["scw_q"] = scw_q

                swf = WSweep([(wi["fc1T"][g], KT * P) for g in range(8)])
                for g in range(8):
                    sl, lv = swf.get(g)
                    wtd("tensor", "gpsimd", lv)
                    for n in range(NCH):
                        bi, bk = pool.acquire()
                        for kk in range(KT):
                            mm(bk[:, :], ws_tile(sl, kk), xn_t(kk, n),
                               kk == 0, kk == KT - 1, inc=(kk == KT - 1))
                        rec.wt("scalar", pe, pe.n)
                        xv = act(lambda s, b=bk, g_=g, n_=n: s.copy(
                            out=xlat[:, g_ * TL + n_ * CW:g_ * TL + (n_ + 1) * CW],
                            in_=b[:, :]))
                        pool.release(bi, ac, xv)
                    ws_done(sl, pe.n)
                state["xlat_done"] = ac.n

                shg = []
                for ih_ in range(4):
                    for g_ in range(8):
                        shg.append((wi["upT"][ih_, g_], KT * P))
                    for g2_ in range(KT):
                        shg.append((bass.AP(wi["dnT"].tensor,
                                            g2_ * P * 32 * P + ih_ * 8 * P,
                                            [[32 * P, P], [1, 8 * P]]), 8 * P))
                swsh = WSweep(shg)
                for ih in range(4):
                    up_last = 0
                    for g in range(8):
                        sl, lv = swsh.get(ih * 24 + g)
                        wtd("tensor", "gpsimd", lv)
                        for n in range(NCH):
                            bi, bk = pool.acquire()
                            for kk in range(KT):
                                mm(bk[:, :], ws_tile(sl, kk), xn_t(kk, n),
                                   kk == 0, kk == KT - 1, inc=(kk == KT - 1))
                            rec.wt("vector", pe, pe.n)
                            rec.wt("vector", ac, up_last)
                            rt = relu_t[:, (n % 2) * CW:(n % 2 + 1) * CW]
                            rv_ = dve(lambda v, b=bk, r=rt: v.tensor_scalar(
                                out=r, in0=b[:, :], scalar1=0.0, scalar2=None,
                                op0=ALU.max))
                            pool.release(bi, dv, rv_)
                            rec.wt("scalar", dv, rv_)
                            up_last = act(lambda s, g_=g, n_=n, r=rt: s.activation(
                                out=up16[:, g_ * TL + n_ * CW:
                                         g_ * TL + (n_ + 1) * CW],
                                in_=r, func=AF.Square))
                        ws_done(sl, pe.n)
                    rec.wt("tensor", ac, up_last)
                    for g2 in range(KT):
                        sl, lv = swsh.get(ih * 24 + 8 + g2)
                        wtd("tensor", "gpsimd", lv)
                        for n in range(NCH):
                            bi, bk = pool.acquire()
                            for kk in range(8):
                                mm(bk[:, :], ws_tile(sl, kk),
                                   up16[:, kk * TL + n * CW:kk * TL + (n + 1) * CW],
                                   kk == 0, kk == 7, inc=(kk == 7))
                            if dbg_split:
                                stage_acc(bk[:, :], pe.n,
                                          xdst_tile(dbg["dbg_sh"], g2, n),
                                          accum=(ih > 0))
                            else:
                                stage_acc(bk[:, :], pe.n, xdst_tile(xnext0, g2, n))
                            pool.release(bi, ac, ac.n)
                        ws_done(sl, pe.n)

        def emit_l0_p3b(xlat, xdst, carry_v, xn_last_pe):
            TH = TL // 2
            p3b = ExitStack()
            with p3b:
                pb = p3b.enter_context
                rwkvH = [pb(nc.sbuf_tensor(_uname(f"rwkvH{i}"), [P, KT * TH],
                                           BF16)) for i in range(2)]
                wrb = [pb(nc.sbuf_tensor(_uname(f"wrb{i}"), [P, TH], BF16))
                       for i in range(2)]
                urb = [pb(nc.sbuf_tensor(_uname(f"urb{i}"), [P, TH], BF16))
                       for i in range(2)]
                srb = [pb(nc.sbuf_tensor(_uname(f"srb{i}"), [P, TH], BF16))
                       for i in range(2)]
                numb = pb(nc.sbuf_tensor(_uname("numb"), [P, TH], F32))
                denb = pb(nc.sbuf_tensor(_uname("denb"), [P, TH], F32))
                tb = pb(nc.sbuf_tensor(_uname("tb"), [P, TH], F32))
                stH = pb(nc.sbuf_tensor(_uname("stH"), [P, 32], F32))
                qfence()
                swo = WSweep([(wi["woT"][g], KT * P) for g in range(KT)] * 2)
                rb_free = [[0, 0], [0, 0], [0, 0]]
                for H in range(2):
                    rwkv = rwkvH[H]
                    for kk in range(KT):
                        j = kk % 2
                        rec.wt("sync", dv, rb_free[0][j])
                        wq_ = sdma(wrb[j][:, :],
                                   sp_w[kk * P:(kk + 1) * P, H * TH:(H + 1) * TH])
                        rec.wt("sync", dv, rb_free[1][j])
                        uq_ = sdma(urb[j][:, :],
                                   sp_u[kk * P:(kk + 1) * P, H * TH:(H + 1) * TH])
                        rec.wt("sync", dv, rb_free[2][j])
                        rq_ = sdma(srb[j][:, :],
                                   sp_r[kk * P:(kk + 1) * P, H * TH:(H + 1) * TH])
                        dcol = cons[:, C_DEC + kk:C_DEC + kk + 1]
                        d_bc = bass.broadcast_tensor_aps(numb[:, :], dcol)[1]
                        wtd("vector", "sync", max(wq_, uq_))
                        rec.wt("vector", dv, carry_v)
                        rec.wt("vector", pe, xn_last_pe)
                        nini = (cons[:, C_CN + kk:C_CN + kk + 1] if H == 0
                                else stH[:, kk:kk + 1])
                        dini = (cons[:, C_CN + 16 + kk:C_CN + 17 + kk] if H == 0
                                else stH[:, 16 + kk:17 + kk])
                        nv = dve(lambda v, u=urb[j], db=d_bc, ini=nini:
                                 v.tensor_tensor_scan(
                                     out=numb[:, :], data0=db, data1=u[:, :],
                                     initial=ini, op0=ALU.mult, op1=ALU.add))
                        if H == 0:
                            dve(lambda v, k=kk: v.tensor_copy(
                                out=stH[:, k:k + 1], in_=numb[:, TH - 1:TH]),
                                inc=False)
                        dnv = dve(lambda v, w=wrb[j], db=d_bc, ini=dini:
                                  v.tensor_tensor_scan(
                                      out=denb[:, :], data0=db, data1=w[:, :],
                                      initial=ini, op0=ALU.mult, op1=ALU.add))
                        rb_free[0][j] = dnv
                        rb_free[1][j] = dnv
                        if H == 0:
                            dve(lambda v, k=kk: v.tensor_copy(
                                out=stH[:, 16 + k:17 + k], in_=denb[:, TH - 1:TH]),
                                inc=False)
                        rec.wt("scalar", dv, dnv)
                        act(lambda s: s.activation(
                            out=tb[:, :], in_=denb[:, :], func=AF.Ln,
                            bias=cons[:, C_EPS + 1:C_EPS + 2]), inc=False)
                        rcv = act(lambda s: s.activation(out=tb[:, :], in_=tb[:, :],
                                                         func=AF.Exp, scale=-1.0))
                        rec.wt("vector", ac, rcv)
                        dve(lambda v: v.tensor_tensor(out=numb[:, :], in0=numb[:, :],
                                                      in1=tb[:, :], op=ALU.mult),
                            inc=False)
                        wtd("vector", "sync", rq_)
                        rv2 = dve(lambda v, s_=srb[j], k=kk, r=rwkv:
                                  v.tensor_tensor(
                                      out=r[:, k * TH:(k + 1) * TH],
                                      in0=numb[:, :], in1=s_[:, :], op=ALU.mult))
                        rb_free[2][j] = rv2
                    rwkv_done = dv.n
                    rec.wt("tensor", dv, rwkv_done)
                    for g in range(KT):
                        sl, lv = swo.get(H * KT + g)
                        wtd("tensor", "gpsimd", lv)
                        for n in range(2):
                            bi, bk = pool.acquire()
                            for kk in range(KT):
                                mm(bk[:, :], ws_tile(sl, kk),
                                   rwkv[:, kk * TH + n * CW:kk * TH + (n + 1) * CW],
                                   kk == 0, kk == KT - 1, inc=(kk == KT - 1))
                            stage_acc(bk[:, :], pe.n,
                                      xdst_tile(xdst, g, H * 2 + n))
                            pool.release(bi, ac, ac.n)
                        ws_done(sl, pe.n)

        def emit_l0_p4(xlat, xdst):
            p4 = ExitStack()
            with p4:
                p4e = p4.enter_context
                ylat = p4e(nc.sbuf_tensor(_uname("ylat"), [P, 8 * TL], F32))
                h16 = p4e(nc.sbuf_tensor(_uname("h16"), [P, 8 * TL], BF16))
                scwbc = [p4e(nc.sbuf_tensor(_uname(f"scwbc{i}"), [P, TL], F32))
                         for i in range(2)]
                scwrow = [p4e(nc.sbuf_tensor(_uname(f"scwrow{i}"), [1, TL], BF16))
                          for i in range(2)]
                relu2 = p4e(nc.sbuf_tensor(_uname("relu2"), [P, 2 * CW], F32))
                qfence()
                swe = WSweep([(wi[w][ee_, m_], 8 * P) for ee_ in range(32)
                              for w in ("w1T", "w2T") for m_ in range(8)])
                swc = WSweep([(wi["fc2T"][g], 8 * P) for g in range(KT)])
                scwrow_pe = [0, 0]
                scwbc_dv = [0, 0]
                h_free_pe = 0
                y_done = {}
                sq_last = 0
                for ee in range(32):
                    j = ee % 2
                    rec.wt("sync", qX, state["scw_q"])
                    rec.wt("sync", pe, scwrow_pe[j])
                    rq_ = sdma(scwrow[j][0:1, :], scw_d[ee:ee + 1, :])
                    for n in range(NCH):
                        bi, bk = pool.acquire()
                        wtd("tensor", "sync", rq_)
                        v_ = mm(bk[:, :], onesrow16[0:1, :],
                                scwrow[j][0:1, n * CW:(n + 1) * CW], True, True,
                                inc=True)
                        rec.wt("scalar", pe, v_)
                        rec.wt("scalar", dv, scwbc_dv[j])
                        cv_ = act(lambda s, b=bk, jj=j, n_=n: s.copy(
                            out=scwbc[jj][:, n_ * CW:(n_ + 1) * CW], in_=b[:, :]))
                        pool.release(bi, ac, cv_)
                    scwrow_pe[j] = pe.n
                    scw_ready = ac.n
                    for m in range(8):
                        sl, lv = swe.get(ee * 16 + m)
                        wtd("tensor", "gpsimd", lv)
                        if ee == 0 and m == 0:
                            rec.wt("tensor", ac, state["xlat_done"])
                        for n in range(NCH):
                            bi, bk = pool.acquire()
                            for kk in range(8):
                                mm(bk[:, :], ws_tile(sl, kk),
                                   xlat[:, kk * TL + n * CW:kk * TL + (n + 1) * CW],
                                   kk == 0, kk == 7, inc=(kk == 7))
                            rt = relu2[:, (n % 2) * CW:(n % 2 + 1) * CW]
                            rec.wt("vector", pe, pe.n)
                            rec.wt("vector", ac, scw_ready)
                            rec.wt("vector", ac, sq_last)
                            hv = dve(lambda v, b=bk, jj=j, n_=n, r=rt:
                                     v.scalar_tensor_tensor(
                                         out=r, in0=b[:, :], scalar=0.0,
                                         in1=scwbc[jj][:, n_ * CW:(n_ + 1) * CW],
                                         op0=ALU.max, op1=ALU.mult))
                            pool.release(bi, dv, hv)
                            rec.wt("scalar", dv, hv)
                            rec.wt("scalar", pe, h_free_pe)
                            sq_last = act(lambda s, m_=m, n_=n, r=rt: s.activation(
                                out=h16[:, m_ * TL + n_ * CW:m_ * TL + (n_ + 1) * CW],
                                in_=r, func=AF.Square))
                        ws_done(sl, pe.n)
                    h_done = sq_last
                    scwbc_dv[j] = dv.n
                    rec.wt("tensor", ac, h_done)
                    for m2 in range(8):
                        sl, lv = swe.get(ee * 16 + 8 + m2)
                        wtd("tensor", "gpsimd", lv)
                        for n in range(NCH):
                            bi, bk = pool.acquire()
                            for kk in range(8):
                                mm(bk[:, :], ws_tile(sl, kk),
                                   h16[:, kk * TL + n * CW:kk * TL + (n + 1) * CW],
                                   kk == 0, kk == 7, inc=(kk == 7))
                            rec.wt("vector", pe, pe.n)
                            if ee == 0:
                                yv = dve(lambda v, b=bk, m_=m2, n_=n: v.tensor_copy(
                                    out=ylat[:, m_ * TL + n_ * CW:
                                             m_ * TL + (n_ + 1) * CW],
                                    in_=b[:, :]))
                            else:
                                yv = dve(lambda v, b=bk, m_=m2, n_=n: v.tensor_tensor(
                                    out=ylat[:, m_ * TL + n_ * CW:
                                             m_ * TL + (n_ + 1) * CW],
                                    in0=ylat[:, m_ * TL + n_ * CW:
                                             m_ * TL + (n_ + 1) * CW],
                                    in1=b[:, :], op=ALU.add))
                            y_done[(m2, n)] = yv
                            pool.release(bi, dv, yv)
                        ws_done(sl, pe.n)
                    h_free_pe = pe.n
                rec.wt("scalar", dv, dv.n)
                y16 = h16
                y16v = act(lambda s: s.copy(out=y16[:, :], in_=ylat[:, :]))
                rec.wt("tensor", ac, y16v)
                for g in range(KT):
                    sl, lv = swc.get(g)
                    wtd("tensor", "gpsimd", lv)
                    for n in range(NCH):
                        bi, bk = pool.acquire()
                        for kk in range(8):
                            mm(bk[:, :], ws_tile(sl, kk),
                               y16[:, kk * TL + n * CW:kk * TL + (n + 1) * CW],
                               kk == 0, kk == 7, inc=(kk == 7))
                        if dbg_split:
                            stage_acc(bk[:, :], pe.n,
                                      xdst_tile(dbg["dbg_ro"], g, n), accum=False)
                        else:
                            stage_acc(bk[:, :], pe.n, xdst_tile(xdst, g, n))
                        pool.release(bi, ac, ac.n)
                    ws_done(sl, pe.n)


        def emit_l1_routing(logits_sb, logits_done, combT):
            p3r = ExitStack()
            with p3r:
                pl = p3r.enter_context
                tmE = pl(nc.sbuf_tensor(_uname("tmE"), [P, KT * 8], F32))
                tmS = pl(nc.sbuf_tensor(_uname("tmS"), [P, KT * 8], F32))
                tmM = pl(nc.sbuf_tensor(_uname("tmM"), [P, KT * 8], F32))
                redE = pl(nc.sbuf_tensor(_uname("redE"), [P, 2 * KT], F32))
                comb16 = pl(nc.sbuf_tensor(_uname("comb16"), [P, KT * 8], BF16))
                rec.wt("tensor", ac, logits_done)
                prev_copy = 0
                for rnd in range(4):
                    bi_t, bk_t = pool.acquire()
                    rec.wt("tensor", ac, prev_copy)
                    for c in range(4):
                        cg = rnd * 4 + c
                        inc = c == 3
                        rec.em("tensor", lambda t, c_=c, cg_=cg, b=bk_t, i_=inc:
                               (t.transpose(b[:, c_ * 8:(c_ + 1) * 8],
                                            logits_sb[0:8, cg_ * P:(cg_ + 1) * P],
                                            eye32f[0:8, 0:8]).then_inc(pe.h, 1)
                                if i_ else
                                t.transpose(b[:, c_ * 8:(c_ + 1) * 8],
                                            logits_sb[0:8, cg_ * P:(cg_ + 1) * P],
                                            eye32f[0:8, 0:8])),
                               incs=((("pe", 1),) if inc else ()))
                    tv = pe.inc()
                    rec.wt("scalar", pe, tv)
                    prev_copy = act(lambda s, r=rnd, b=bk_t: s.copy(
                        out=tmE[:, r * 32:(r + 1) * 32], in_=b[:, 0:32]))
                    pool.release(bi_t, ac, prev_copy)
                rec.wt("vector", ac, prev_copy)
                tm38 = tmE[:, :].rearrange("p (c e) -> p c e", e=8)
                mE = redE[:, 0:KT].rearrange("p (c o) -> p c o", o=1)
                dve(lambda v: v.tensor_reduce(out=mE, in_=tm38, axis=AXX,
                                              op=ALU.max), inc=False)
                mEb = bass.broadcast_tensor_aps(tm38, mE)[1]
                e38 = tmS[:, :].rearrange("p (c e) -> p c e", e=8)
                sv = dve(lambda v: v.tensor_tensor(out=e38, in0=tm38, in1=mEb,
                                                   op=ALU.subtract))
                rec.wt("scalar", dv, sv)
                ev_ = act(lambda s: s.activation(out=tmS[:, :], in_=tmS[:, :],
                                                 func=AF.Exp))
                rec.wt("vector", ac, ev_)
                dve(lambda v: v.tensor_reduce(out=mE, in_=e38, axis=AXX,
                                              op=ALU.max), inc=False)
                m1b = bass.broadcast_tensor_aps(e38, mE)[1]
                msk = tmM[:, :].rearrange("p (c e) -> p c e", e=8)
                dve(lambda v: v.tensor_tensor(out=msk, in0=e38, in1=m1b,
                                              op=ALU.is_equal), inc=False)
                dve(lambda v: v.tensor_tensor(out=msk, in0=e38, in1=msk,
                                              op=ALU.mult), inc=False)
                dve(lambda v: v.tensor_tensor(out=msk, in0=e38, in1=msk,
                                              op=ALU.subtract), inc=False)
                e2 = redE[:, KT:2 * KT].rearrange("p (c o) -> p c o", o=1)
                dve(lambda v: v.tensor_reduce(out=e2, in_=msk, axis=AXX,
                                              op=ALU.max), inc=False)
                dve(lambda v: v.tensor_tensor(out=redE[:, 0:KT], in0=redE[:, 0:KT],
                                              in1=redE[:, KT:2 * KT], op=ALU.add),
                    inc=False)
                m2b = bass.broadcast_tensor_aps(e38, e2)[1]
                dve(lambda v: v.tensor_tensor(out=msk, in0=e38, in1=m2b,
                                              op=ALU.is_ge), inc=False)
                dve(lambda v: v.tensor_tensor(out=msk, in0=e38, in1=msk,
                                              op=ALU.mult), inc=False)
                dve(lambda v: v.reciprocal(out=redE[:, 0:KT], in_=redE[:, 0:KT]),
                    inc=False)
                dnb = bass.broadcast_tensor_aps(msk, mE)[1]
                dve(lambda v: v.tensor_tensor(out=msk, in0=msk, in1=dnb,
                                              op=ALU.mult), inc=False)
                al3 = cons[:, C_ALPHA:C_ALPHA + 8].rearrange("p (o e) -> p o e", o=1)
                alb = bass.broadcast_tensor_aps(msk, al3)[1]
                cbv = dve(lambda v: v.tensor_tensor(out=msk, in0=msk, in1=alb,
                                                    op=ALU.mult))
                rec.wt("vector", dv, cbv)
                c16v = dve(lambda v: v.tensor_copy(out=comb16[:, :], in_=tmM[:, :]))
                prev_copy = 0
                for rnd in range(4):
                    rec.wt("tensor", dv, c16v)
                    rec.wt("tensor", ac, prev_copy)
                    for c in range(4):
                        cg = rnd * 4 + c
                        src = bass.AP(comb16[:, :].tensor,
                                      comb16[:, :].offset + cg * 8,
                                      [comb16[:, :].ap[0], [1, 8]])
                        if c < 3:
                            rec.em("tensor", lambda t, c_=c, s_=src:
                                   t.transpose(ps16[0:8, c_ * 128:(c_ + 1) * 128],
                                               s_, cons16[:, B_EYE128:B_EYE128 + P]))
                        else:
                            rec.em("tensor", lambda t, c_=c, s_=src:
                                   t.transpose(ps16[0:8, c_ * 128:(c_ + 1) * 128],
                                               s_, cons16[:, B_EYE128:B_EYE128 + P])
                                   .then_inc(pe.h, 1), incs=(("pe", 1),))
                            tv = pe.inc()
                    rec.wt("scalar", pe, tv)
                    prev_copy = act(lambda s, r=rnd: s.copy(
                        out=combT[0:8, r * CW:(r + 1) * CW], in_=ps16[0:8, 0:CW]))
                cT_done = prev_copy
                state["cT_done"] = cT_done
                rec.wt("sync", ac, cT_done)
                state["comb_q"] = sdma(comb_d, combT[0:8, :])

        def emit_l1_p3(logits_sb, logits_done, combT, xdst, consume_carry,
                       xn_last_pe):
            TH = TL // 2
            p3L = ExitStack()
            with p3L:
                pl = p3L.enter_context
                o16H = [pl(nc.sbuf_tensor(_uname(f"o16H{i}"), [P, KT * TH], BF16))
                        for i in range(2)]
                arb = [pl(nc.sbuf_tensor(_uname(f"arb{i}"), [P, TH], BF16))
                       for i in range(2)]
                irb = [pl(nc.sbuf_tensor(_uname(f"irb{i}"), [P, TH], BF16))
                       for i in range(2)]
                grb = pl(nc.sbuf_tensor(_uname("grb"), [P, TH], BF16))
                hbuf = pl(nc.sbuf_tensor(_uname("hbuf"), [P, TH], F32))
                hs16 = pl(nc.sbuf_tensor(_uname("hs16"), [P, TH], BF16))
                rstdbc = pl(nc.sbuf_tensor(_uname("rstdbc"), [P, TH], F32))
                tt = pl(nc.sbuf_tensor(_uname("tt"), [P, TH], F32))
                stI = pl(nc.sbuf_tensor(_uname("stI"), [P, KT], F32))
                stO = pl(nc.sbuf_tensor(_uname("stO"), [P, KT], F32))
                qfence()
                carry_v = consume_carry()
                wtd("gpsimd", "sync", state["x1cp_q"])
                swo1 = WSweep([(wi["wo1T"][g], KT * P) for g in range(KT)] * 2)
                for H in range(2):
                    o16 = o16H[H]
                    ssq = []
                    for n in range(2):
                        ssq.append(pool.acquire(hold=True))
                    rb2 = [[0, 0], [0, 0]]
                    for kk in range(KT):
                        j = kk % 2
                        rec.wt("sync", dv, rb2[0][j])
                        aq_ = sdma(arb[j][:, :],
                                   sp_w[kk * P:(kk + 1) * P, H * TH:(H + 1) * TH])
                        rec.wt("sync", dv, rb2[1][j])
                        iq_ = sdma(irb[j][:, :],
                                   sp_u[kk * P:(kk + 1) * P, H * TH:(H + 1) * TH])
                        wtd("vector", "sync", max(aq_, iq_))
                        rec.wt("vector", dv, carry_v)
                        rec.wt("vector", pe, xn_last_pe)
                        ini = (cons[:, C_CN + kk:C_CN + kk + 1] if H == 0
                               else stI[:, kk:kk + 1])
                        hv = dve(lambda v, a_=arb[j], i_=irb[j], ini_=ini:
                                 v.tensor_tensor_scan(
                                     out=hbuf[:, :], data0=a_[:, :], data1=i_[:, :],
                                     initial=ini_, op0=ALU.mult, op1=ALU.add))
                        if H == 0:
                            dve(lambda v, k=kk: v.tensor_copy(
                                out=stO[:, k:k + 1], in_=hbuf[:, TH - 1:TH]),
                                inc=False)
                        rb2[0][j] = hv
                        rb2[1][j] = hv
                        rec.wt("scalar", dv, hv)
                        sqv = act(lambda s: s.activation(out=hs16[:, :],
                                                         in_=hbuf[:, :],
                                                         func=AF.Square))
                        rec.wt("tensor", ac, sqv)
                        for n in range(2):
                            mm(ssq[n][1][0:1, :], cons16[:, B_ONES:B_ONES + 1],
                               hs16[:, n * CW:(n + 1) * CW], kk == 0, kk == KT - 1,
                               inc=(kk == KT - 1 and n == 1))
                    ssq_pe = pe.n
                    rec.wt("scalar", pe, ssq_pe)
                    for n in range(2):
                        act(lambda s, b=ssq[n][1], n_=n: s.activation(
                            out=rows[0:1, n_ * CW:(n_ + 1) * CW], in_=b[0:1, :],
                            func=AF.Ln, bias=cons[0:1, C_EPS + 3:C_EPS + 4]),
                            inc=False)
                    for n in range(2):
                        act(lambda s, n_=n: s.activation(
                            out=rows[0:1, n_ * CW:(n_ + 1) * CW],
                            in_=rows[0:1, n_ * CW:(n_ + 1) * CW],
                            func=AF.Exp, scale=-0.5), inc=False)
                    rec.em("scalar", lambda s: s.copy(
                        out=rows[0:1, 2 * CW:2 * CW + 1],
                        in_=rows[0:1, 0:1]).then_inc(ac.h, 1),
                        incs=(("ac", 1),))
                    expv = ac.inc()
                    rec.wt("vector", ac, expv)
                    sqrtD = float(np.sqrt(D))
                    rowsv = dve(lambda v: v.tensor_scalar(
                        out=rows[0:1, 0:2 * CW], in0=rows[0:1, 0:2 * CW],
                        scalar1=sqrtD, scalar2=None, op0=ALU.mult))
                    for n in range(2):
                        bi, bk = pool.acquire()
                        rec.wt("tensor", dv, rowsv)
                        bv = mm(bk[:, :], onesrow[0:1, :],
                                rows[0:1, n * CW:(n + 1) * CW], True, True,
                                inc=True)
                        rec.wt("scalar", pe, bv)
                        rbv = act(lambda s, b=bk, n_=n: s.copy(
                            out=rstdbc[:, n_ * CW:(n_ + 1) * CW], in_=b[:, :]))
                        pool.release(bi, ac, rbv)
                    rstd_done = ac.n
                    for n in range(2):
                        pool.release(ssq[n][0], ac, rstd_done)
                    g_free = 0
                    for kk in range(KT):
                        j = kk % 2
                        rec.wt("sync", dv, rb2[0][j])
                        aq_ = sdma(arb[j][:, :],
                                   sp_w[kk * P:(kk + 1) * P, H * TH:(H + 1) * TH])
                        rec.wt("sync", dv, rb2[1][j])
                        iq_ = sdma(irb[j][:, :],
                                   sp_u[kk * P:(kk + 1) * P, H * TH:(H + 1) * TH])
                        rec.wt("sync", dv, g_free)
                        gq_ = sdma(grb[:, :],
                                   sp_r[kk * P:(kk + 1) * P, H * TH:(H + 1) * TH])
                        wtd("vector", "sync", max(aq_, iq_))
                        ini = (cons[:, C_CN + kk:C_CN + kk + 1] if H == 0
                               else stI[:, kk:kk + 1])
                        hv = dve(lambda v, a_=arb[j], i_=irb[j], ini_=ini:
                                 v.tensor_tensor_scan(
                                     out=hbuf[:, :], data0=a_[:, :], data1=i_[:, :],
                                     initial=ini_, op0=ALU.mult, op1=ALU.add))
                        rb2[0][j] = hv
                        rb2[1][j] = hv
                        rec.wt("vector", ac, rstd_done)
                        dve(lambda v, k=kk: v.tensor_scalar(
                            out=tt[:, :], in0=hbuf[:, :],
                            scalar1=cons[:, C_GNW + k:C_GNW + k + 1], scalar2=None,
                            op0=ALU.mult), inc=False)
                        dve(lambda v: v.tensor_tensor(out=tt[:, :], in0=tt[:, :],
                                                      in1=rstdbc[:, :],
                                                      op=ALU.mult), inc=False)
                        wtd("vector", "sync", gq_)
                        ov = dve(lambda v, k=kk, o_=o16: v.tensor_tensor(
                            out=o_[:, k * TH:(k + 1) * TH], in0=tt[:, :],
                            in1=grb[:, :], op=ALU.mult))
                        g_free = ov
                    if H == 0:
                        dve(lambda v: v.tensor_copy(out=stI[:, :], in_=stO[:, :]),
                            inc=False)
                    o_done = dv.n
                    rec.wt("tensor", dv, o_done)
                    for g in range(KT):
                        sl, lv = swo1.get(H * KT + g)
                        wtd("tensor", "gpsimd", lv)
                        for n in range(2):
                            bi, bk = pool.acquire()
                            for kk in range(KT):
                                mm(bk[:, :], ws_tile(sl, kk),
                                   o16[:, kk * TH + n * CW:kk * TH + (n + 1) * CW],
                                   kk == 0, kk == KT - 1, inc=(kk == KT - 1))
                            if dbg_split:
                                stage_acc(bk[:, :], pe.n,
                                          xdst_tile(dbg["dbg_at"], g, H * 2 + n),
                                          accum=False)
                            else:
                                stage_acc(bk[:, :], pe.n,
                                          xdst_tile(xdst, g, H * 2 + n))
                            pool.release(bi, ac, ac.n)
                        ws_done(sl, pe.n)
            state["p3_last_pe"] = pe.n

        def emit_l1_p4(combT, xdst, xn_dump_q):
            p4L = ExitStack()
            with p4L:
                q4 = p4L.enter_context
                xnr = q4(nc.sbuf_tensor(_uname("xnr"), [P, KT * TL], BF16))
                hE = q4(nc.sbuf_tensor(_uname("hE"), [P, KT * TL], BF16))
                cbbc = [q4(nc.sbuf_tensor(_uname("cbbc0"), [P, TL], F32))] * 2
                cbrow = [q4(nc.sbuf_tensor(_uname("cbrow0"), [1, TL], BF16))] * 2
                gel_t = q4(nc.sbuf_tensor(_uname("gel_t"), [P, 2 * CW], F32))
                b2sb = [q4(nc.sbuf_tensor(_uname(f"b2sb{i}"), [8, P], BF16))
                        for i in range(2)]
                qfence()
                swg = WSweep([(wi[w][ee_, m_], KT * P) for ee_ in range(8)
                              for w in ("g1fc1", "g1fc2") for m_ in range(KT)])
                rec.wt("sync", qX, xn_dump_q)
                xnr_src = bass.AP(sp_xn.tensor, 0, [[TL, P], [P * TL, KT], [1, TL]])
                xq = sdma(xnr[:, :], xnr_src)
                cbrow_pe = 0
                cbbc_dv = 0
                hE_free_pe = 0
                hlast = 0
                for ee in range(8):
                    j = ee % 2
                    rec.wt("sync", qX, state["comb_q"])
                    rec.wt("sync", pe, cbrow_pe)
                    rq_ = sdma(cbrow[j][0:1, :], comb_d[ee:ee + 1, :])
                    for n in range(NCH):
                        bi, bk = pool.acquire()
                        wtd("tensor", "sync", rq_)
                        v_ = mm(bk[:, :], onesrow16[0:1, :],
                                cbrow[j][0:1, n * CW:(n + 1) * CW], True, True,
                                inc=True)
                        rec.wt("scalar", pe, v_)
                        rec.wt("scalar", dv, cbbc_dv)
                        cv_ = act(lambda s, b=bk, jj=j, n_=n: s.copy(
                            out=cbbc[jj][:, n_ * CW:(n_ + 1) * CW], in_=b[:, :]))
                        pool.release(bi, ac, cv_)
                    cbrow_pe = pe.n
                    cb_ready = ac.n
                    for m in range(KT):
                        sl, lv = swg.get(ee * 32 + m)
                        wtd("tensor", "gpsimd", lv)
                        if ee == 0 and m == 0:
                            wtd("tensor", "sync", xq)
                        for n in range(NCH):
                            bi, bk = pool.acquire()
                            for kk in range(KT):
                                mm(bk[:, :], ws_tile(sl, kk),
                                   xnr[:, kk * TL + n * CW:kk * TL + (n + 1) * CW],
                                   kk == 0, kk == KT - 1, inc=(kk == KT - 1))
                            gt = gel_t[:, (n % 2) * CW:(n % 2 + 1) * CW]
                            rec.wt("scalar", pe, pe.n)
                            rec.wt("scalar", dv, hlast)
                            wtd("scalar", "sync", cst_q)
                            gv_ = act(lambda s, b=bk, m_=m, e_=ee, g_=gt:
                                      s.activation(out=g_, in_=b[:, :], func=AF.Gelu,
                                                   bias=b1sb[:, m_ * 8 + e_:
                                                             m_ * 8 + e_ + 1]))
                            pool.release(bi, ac, gv_)
                            rec.wt("vector", ac, gv_)
                            rec.wt("vector", ac, cb_ready)
                            rec.wt("vector", pe, hE_free_pe)
                            hlast = dve(lambda v, jj=j, m_=m, n_=n, g_=gt:
                                        v.tensor_tensor(
                                            out=hE[:, m_ * TL + n_ * CW:
                                                   m_ * TL + (n_ + 1) * CW],
                                            in0=g_,
                                            in1=cbbc[jj][:, n_ * CW:(n_ + 1) * CW],
                                            op=ALU.mult))
                        ws_done(sl, pe.n)
                    hE_done = hlast
                    cbbc_dv = hlast
                    rec.wt("tensor", dv, hE_done)
                    for m2 in range(KT):
                        sl, lv = swg.get(ee * 32 + KT + m2)
                        wtd("tensor", "gpsimd", lv)
                        for n in range(NCH):
                            bi, bk = pool.acquire()
                            for kk in range(KT):
                                mm(bk[:, :], ws_tile(sl, kk),
                                   hE[:, kk * TL + n * CW:kk * TL + (n + 1) * CW],
                                   kk == 0, kk == KT - 1, inc=(kk == KT - 1))
                            if dbg_split:
                                stage_acc(bk[:, :], pe.n,
                                          xdst_tile(dbg["dbg_ro"], m2, n),
                                          accum=(ee > 0))
                            else:
                                stage_acc(bk[:, :], pe.n, xdst_tile(xdst, m2, n))
                            pool.release(bi, ac, ac.n)
                        ws_done(sl, pe.n)
                    hE_free_pe = pe.n
                b2_pe = [0, 0]
                for m2 in range(KT):
                    jb = m2 % 2
                    rec.wt("gpsimd", pe, b2_pe[jb])
                    bq_ = gdma(b2sb[jb][0:8, :],
                               wi["b2raw"][0:8, m2 * P:(m2 + 1) * P])
                    wtd("tensor", "gpsimd", bq_)
                    rec.wt("tensor", ac, state["cT_done"])
                    for n in range(NCH):
                        bi, bk = pool.acquire()
                        bv = mm(bk[:, :], b2sb[jb][0:8, :],
                                combT[0:8, n * CW:(n + 1) * CW], True, True,
                                inc=True)
                        if dbg_split:
                            stage_acc(bk[:, :], bv, xdst_tile(dbg["dbg_ro"], m2, n))
                        else:
                            stage_acc(bk[:, :], bv, xdst_tile(xdst, m2, n))
                        pool.release(bi, ac, ac.n)
                    b2_pe[jb] = pe.n
                    if not dbg_split:
                        rowap_o = bass.AP(out.tensor, m2 * P * TL, [[TL, P], [1, TL]])
                        rowap_x = bass.AP(xdst.tensor, m2 * P * TL, [[TL, P], [1, TL]])
                        wtd("sync", "gpsimd", qW.n)
                        sdma(rowap_o, rowap_x)

        state = {}
        emit_layer(0)
        if n_layers > 1:
            emit_layer(1)
            rec.wt("gpsimd", qW, qW.n)
            rec.wt("sync", qX, qX.n)
        else:
            rec.wt("gpsimd", qW, qW.n)
            gdma(out, xnext0)
            rec.wt("gpsimd", qW, qW.n)

        assert rec.check(), "semaphore schedule deadlock"

        with nc.Block() as block:
            @block.sync
            def _(sync):
                for fn in rec.ops["sync"]:
                    fn(sync)

            @block.gpsimd
            def _(g):
                for fn in rec.ops["gpsimd"]:
                    fn(g)

            @block.tensor
            def _(t):
                for fn in rec.ops["tensor"]:
                    fn(t)

            @block.vector
            def _(v):
                for fn in rec.ops["vector"]:
                    fn(v)

            @block.scalar
            def _(s):
                for fn in rec.ops["scalar"]:
                    fn(s)

    return nc


def host_pack(inputs, n_layers=2):
    f32 = np.float32
    g = {}
    g["wkT"] = pack_lhsT(np.asarray(inputs["L0_attn_Wk"]).T)
    g["wvT"] = pack_lhsT(np.asarray(inputs["L0_attn_Wv"]).T)
    g["wrT"] = pack_lhsT(np.asarray(inputs["L0_attn_Wr"]).T)
    g["woT"] = pack_lhsT(np.asarray(inputs["L0_attn_Wo"]).T)
    g["fc1T"] = pack_lhsT(np.asarray(inputs["L0_ffn_fc1_w"]).T)
    g["fc2T"] = pack_lhsT(np.asarray(inputs["L0_ffn_fc2_w"]).T)
    g["w1T"] = np.stack([pack_lhsT(np.asarray(w).T) for w in inputs["L0_ffn_w1"]])
    g["w2T"] = np.stack([pack_lhsT(np.asarray(w).T) for w in inputs["L0_ffn_w2"]])
    upT = pack_lhsT(np.asarray(inputs["L0_ffn_shared_up_w"]).T)
    g["upT"] = np.ascontiguousarray(upT.reshape(4, 8, P, KT * P))
    g["dnT"] = pack_lhsT(np.asarray(inputs["L0_ffn_shared_down_w"]).T)
    g["gate0T"] = pack_lhsT(np.asarray(inputs["L0_ffn_gate_w"]).T, msz=32)
    g["td_T"] = np.ascontiguousarray(
        np.asarray(inputs["L0_attn_time_decay"], dtype=f32).reshape(KT, P).T)
    g["tf_T"] = np.ascontiguousarray(
        np.asarray(inputs["L0_attn_time_first"], dtype=f32).reshape(KT, P).T)
    gb = np.asarray(inputs["L0_ffn_gate_bias"], dtype=f32)
    g["gb_col"] = np.ascontiguousarray(gb.reshape(32, 1))
    g["gb_rep"] = np.ascontiguousarray(np.tile(gb.reshape(1, 32), (P, 1)))
    if n_layers > 1:
        g["wiT"] = pack_lhsT(np.asarray(inputs["L1_attn_Wi"]))
        g["wfT"] = pack_lhsT(np.asarray(inputs["L1_attn_Wf"]))
        g["wgT"] = pack_lhsT(np.asarray(inputs["L1_attn_Wg"]))
        g["wo1T"] = pack_lhsT(np.asarray(inputs["L1_attn_Wo"]))
        g["g1fc1"] = np.stack([pack_lhsT(np.asarray(w))
                               for w in inputs["L1_ffn_fc1_w"]])
        g["g1fc2"] = np.stack([pack_lhsT(np.asarray(w))
                               for w in inputs["L1_ffn_fc2_w"]])
        g["gate1T"] = pack_lhsT(np.asarray(inputs["L1_ffn_gate_w"]), msz=8)
        b1 = np.asarray(inputs["L1_ffn_fc1_b"], dtype=f32)
        g["b1P"] = np.ascontiguousarray(
            b1.T.reshape(KT, P, 8).transpose(1, 0, 2).reshape(P, KT * 8))
        g["b2raw"] = np.ascontiguousarray(np.asarray(inputs["L1_ffn_fc2_b"],
                                                     dtype=f32))
        g["gnw"] = np.ascontiguousarray(
            np.asarray(inputs["L1_attn_g_norm_weight"], dtype=f32).reshape(KT, P).T)
        g["alpha_rep"] = np.ascontiguousarray(
            np.tile(np.asarray(inputs["L1_ffn_alpha"], dtype=f32).reshape(1, 8),
                    (P, 1)))
    g["ones_col"] = np.ones((P, 1), f32)
    g["ones_row"] = np.ones((1, P), f32)
    g["eye128"] = np.eye(P, dtype=f32)
    g["eye32"] = np.eye(32, dtype=f32)
    g["epsv"] = np.tile(np.array([[1.0e-5, 1.0e-6, 1.0e-30, 1.0e-5 * D]], f32), (P, 1))
    return g


def make_in_maps(inputs, n_layers=2):
    shared = host_pack(inputs, n_layers=n_layers)
    x = np.asarray(inputs["L0_attn_x"], dtype=np.float32)
    in_maps = []
    for c in range(8):
        b, h = c // 2, c % 2
        m = dict(shared)
        m["xres0"] = np.ascontiguousarray(
            x[b, h * TL:(h + 1) * TL, :].T)
        sel = np.zeros((P, 8), np.float32)
        if h == 1:
            sel[:, 2 * b] = 1.0
        m["sel_rep"] = sel
        m["fmask"] = np.full((P, 1), 1.0 if h == 0 else 0.0, np.float32)
        in_maps.append(m)
    return in_maps


def kernel(**inputs):
    n_layers = 2
    nc = build_program(n_layers=n_layers)
    in_maps = make_in_maps(inputs, n_layers=n_layers)
    res = run_bass_kernel_spmd(nc, in_maps, core_ids=list(range(8)))
    out = np.empty((4, 4096, D), np.float32)
    for c in range(8):
        b, h = c // 2, c % 2
        out[b, h * TL:(h + 1) * TL, :] = res.results[c]["out"].T
    return out


# version 10
# speedup vs baseline: 1.0143x; 1.0143x over previous
import numpy as np
from contextlib import ExitStack

import concourse.bass as bass
import concourse.mybir as mybir
from concourse.bass_utils import run_bass_kernel_spmd

F32 = mybir.dt.float32
BF16 = mybir.dt.bfloat16
AF = mybir.ActivationFunctionType
ALU = mybir.AluOpType
AXX = mybir.AxisListType.X

P = 128
D = 2048
KT = D // P
TL = 2048
NCH = 4
CW = 512
NC0 = 8
CW0 = 256
BIG = 1.0e30

ENGINES = ("sync", "gpsimd", "tensor", "vector", "scalar")


class Sem:
    def __init__(self, handle, name):
        self.h, self.name, self.n = handle, name, 0

    def inc(self, k=1):
        self.n += k
        return self.n


class Rec:
    def __init__(self):
        self.ops = {e: [] for e in ENGINES}
        self.meta = {e: [] for e in ENGINES}

    def em(self, eng, fn, incs=(), tag=""):
        self.ops[eng].append(fn)
        self.meta[eng].append(("op", tuple(incs), tag))

    def wt(self, eng, sem, val):
        if val and val > 0:
            self.ops[eng].append(lambda e, s=sem.h, v=val: e.wait_ge(s, v))
            self.meta[eng].append(("wait", (sem.name, val), ""))

    def check(self):
        heads = {e: 0 for e in ENGINES}
        sems = {}
        import collections
        progress = True
        while progress:
            progress = False
            for e in ENGINES:
                while heads[e] < len(self.meta[e]):
                    kind, data, tag = self.meta[e][heads[e]]
                    if kind == "wait":
                        name, val = data
                        if sems.get(name, 0) < val:
                            break
                    else:
                        for (name, amt) in data:
                            sems[name] = sems.get(name, 0) + amt
                    heads[e] += 1
                    progress = True
        stuck = {e: heads[e] for e in ENGINES if heads[e] < len(self.meta[e])}
        if stuck:
            print("DEADLOCK. engine heads:")
            for e, h in stuck.items():
                kind, data, tag = self.meta[e][h]
                print(f"  {e}@{h}/{len(self.meta[e])}: waiting {data}"
                      f" (sem now {sems.get(data[0], 0)}) {tag}")
            return False
        print("schedule check OK; final sems:", sems)
        return True


class Pool:

    def __init__(self, rec, tensors):
        self.rec, self.t = rec, tensors
        self.nb = len(tensors)
        self.free_at = [None] * self.nb
        self.held = [False] * self.nb
        self.i = 0

    def acquire(self, hold=False):
        for _ in range(self.nb):
            idx = self.i % self.nb
            self.i += 1
            if not self.held[idx]:
                break
        else:
            raise RuntimeError("no free psum bank")
        f = self.free_at[idx]
        if f is not None:
            self.rec.wt("tensor", f[0], f[1])
            self.free_at[idx] = None
        if hold:
            self.held[idx] = True
        return idx, self.t[idx]

    def release(self, idx, sem, val):
        self.held[idx] = False
        self.free_at[idx] = (sem, val)


def pack_lhsT(WT, msz=128):
    K, M = WT.shape
    kt = K // P
    G = M // msz
    a = np.asarray(WT, dtype=np.float32).reshape(kt, P, G, msz).transpose(2, 1, 0, 3)
    return np.ascontiguousarray(a.reshape(G, P, kt * msz))


def build_program(n_layers=2, taps=(), dbg_split=False):
    nc = bass.Bass("TRN2", target_bir_lowering=False, debug=False,
                   detect_race_conditions=False)
    dt = nc.dram_tensor

    xres0 = dt("xres0", [D, TL], F32, kind="ExternalInput").ap()
    out = dt("out", [D, TL], F32, kind="ExternalOutput").ap()

    wi = {}
    def win(name, shape):
        wi[name] = dt(name, list(shape), F32, kind="ExternalInput").ap()
        return wi[name]

    win("wkT", (KT, P, KT * P)); win("wvT", (KT, P, KT * P))
    win("wrT", (KT, P, KT * P)); win("woT", (KT, P, KT * P))
    win("fc1T", (8, P, KT * P)); win("fc2T", (KT, P, 8 * P))
    win("w1T", (32, 8, P, 8 * P)); win("w2T", (32, 8, P, 8 * P))
    win("upT", (4, 8, P, KT * P)); win("dnT", (KT, P, 32 * P))
    win("gate0T", (1, P, KT * 32))
    win("td_T", (P, KT)); win("tf_T", (P, KT))
    win("gb_col", (32, 1)); win("gb_rep", (P, 32))
    if n_layers > 1:
        win("wiT", (KT, P, KT * P)); win("wfT", (KT, P, KT * P))
        win("wgT", (KT, P, KT * P)); win("wo1T", (KT, P, KT * P))
        win("g1fc1", (8, KT, P, KT * P)); win("g1fc2", (8, KT, P, KT * P))
        win("gate1T", (1, P, KT * 8))
        win("b1P", (P, KT * 8)); win("b2raw", (8, D))
        win("gnw", (P, KT)); win("alpha_rep", (P, 8))
    win("ones_col", (P, 1)); win("ones_row", (1, P))
    win("eye128", (P, P)); win("eye32", (32, 32))
    win("sel_rep", (P, 8)); win("fmask", (P, 1)); win("epsv", (P, 4))

    xnext0 = dt("xnext0", [D, TL], F32, kind="Internal").ap()
    xnext1 = dt("xnext1", [D, TL], F32, kind="Internal").ap() if n_layers > 1 else None
    sp_w = dt("sp_w", [D, TL], BF16, kind="Internal").ap()
    sp_u = dt("sp_u", [D, TL], BF16, kind="Internal").ap()
    sp_r = dt("sp_r", [D, TL], BF16, kind="Internal").ap()
    sp_xn = dt("sp_xn", [D, TL], BF16, kind="Internal").ap() if n_layers > 1 else None
    scw_d = dt("scw_d", [32, TL], BF16, kind="Internal").ap()
    comb_d = dt("comb_d", [8, TL], BF16, kind="Internal").ap() if n_layers > 1 else None
    dbg = {}
    if dbg_split:
        for nm in ("dbg_at", "dbg_sh", "dbg_ro"):
            dbg[nm] = dt(nm, [D, TL], F32, kind="Internal").ap()
    cc_in = [dt(f"cc_in{l}", [P, 32], F32, kind="Internal").ap()
             for l in range(n_layers)]
    cc_out = [dt(f"cc_out{l}", [8 * P, 32], F32, kind="Internal").ap()
              for l in range(n_layers)]

    tap_out = {}
    for t in taps:
        tap_out[t] = dt("tap_" + t, [D, TL], F32, kind="ExternalOutput").ap()

    _names = {}

    def _uname(n):
        k = _names.get(n, 0)
        _names[n] = k + 1
        return n if k == 0 else f"{n}_{k}"

    rec = Rec()
    ctx = ExitStack()
    with ctx:
        e = ctx.enter_context
        qW = Sem(e(nc.semaphore("qW")), "qW")
        qX = Sem(e(nc.semaphore("qX")), "qX")
        pe = Sem(e(nc.semaphore("pe")), "pe")
        dv = Sem(e(nc.semaphore("dv")), "dv")
        ac = Sem(e(nc.semaphore("ac")), "ac")
        ccs = [Sem(e(nc.semaphore(f"cc{l}")), f"cc{l}") for l in range(n_layers)]
        sX = Sem(e(nc.semaphore("sX")), "sX")
        sW = Sem(e(nc.semaphore("sW")), "sW")

        psum = [e(nc.psum_tensor(f"ps{i}", [P, CW], F32)) for i in range(7)]
        ps16 = e(nc.psum_tensor("ps16", [P, CW], BF16))
        pool = Pool(rec, psum)

        WSLOT = 32 * P
        wring = e(nc.sbuf_tensor(_uname("wring"), [P, 3 * WSLOT], BF16))
        stage = e(nc.sbuf_tensor(_uname("stage"), [P, 4 * CW], F32))
        cons = e(nc.sbuf_tensor(_uname("cons"), [P, 256], F32))
        C_ONES, C_DEC, C_FF, C_TF, C_TD = 0, 1, 17, 33, 49
        C_GNW, C_SEL, C_CN, C_FIN, C_ALPHA, C_FM = 65, 81, 96, 128, 160, 255
        C_EPS = 168
        cons16 = e(nc.sbuf_tensor(_uname("cons16"), [P, 224], BF16))
        B_ONES, B_EYE128, B_EYE32 = 0, 64, 192
        onesrow = e(nc.sbuf_tensor(_uname("onesrow"), [1, P], F32))
        onesrow16 = e(nc.sbuf_tensor(_uname("onesrow16"), [1, P], BF16))
        gbcol = e(nc.sbuf_tensor(_uname("gbcol"), [32, 1], F32))
        gbrep = e(nc.sbuf_tensor(_uname("gbrep"), [P, 32], F32))
        g0sb = e(nc.sbuf_tensor(_uname("g0sb"), [P, KT * 32], BF16))
        rows = e(nc.sbuf_tensor(_uname("rows"), [1, NCH * CW], F32))
        abc = e(nc.sbuf_tensor(_uname("abc"), [P, 2 * CW0], F32))
        bbc = e(nc.sbuf_tensor(_uname("bbc"), [P, 2 * CW0], F32))
        agsb = e(nc.sbuf_tensor(_uname("agsb"), [P, 8 * 32], F32))
        eye32f = e(nc.sbuf_tensor(_uname("eye32f"), [32, 32], F32))
        if n_layers > 1:
            g1sb = e(nc.sbuf_tensor(_uname("g1sb"), [P, KT * 8], BF16))
            b1sb = e(nc.sbuf_tensor(_uname("b1sb"), [P, KT * 8], F32))

        ws_state = {"i": 0, "free": [0, 0, 0]}

        def ws_load(dram_ap, n_elems):
            s = ws_state["i"] % 3
            ws_state["i"] += 1
            rec.wt("gpsimd", pe, ws_state["free"][s])
            dst = wring[:, s * WSLOT: s * WSLOT + n_elems]
            rec.em("gpsimd", lambda g, d=dst, a=dram_ap:
                   g.dma_start(out=d, in_=a).then_inc(qW.h, 16),
                   incs=(("qW", 16),))
            return s, qW.inc(16)

        def ws_tile(slot, kk, msz=128):
            off = slot * WSLOT + kk * msz
            return wring[:, off: off + msz]

        def ws_done(slot, pe_val):
            ws_state["free"][slot] = pe_val

        class WSweep:

            def __init__(self, groups):
                self.groups = groups
                self.h = []
                self.i = 0

            def get(self, g):
                while self.i <= min(g + 2, len(self.groups) - 1):
                    ap_, ne_ = self.groups[self.i]
                    self.h.append(ws_load(ap_, ne_))
                    self.i += 1
                return self.h[g]

        def mm(bank, lhsT, rhs, start, stop, inc=False):
            if inc:
                rec.em("tensor", lambda t, b=bank, l=lhsT, r=rhs, s=start, p=stop:
                       t.matmul(b, l, r, start=s, stop=p).then_inc(pe.h, 1),
                       incs=(("pe", 1),))
                return pe.inc()
            rec.em("tensor", lambda t, b=bank, l=lhsT, r=rhs, s=start, p=stop:
                   t.matmul(b, l, r, start=s, stop=p))
            return None

        def dve(fn, inc=True):
            rec.wt("vector", dv, dv.n)
            rec.em("vector", lambda v, f=fn: f(v).then_inc(dv.h, 1),
                   incs=(("dv", 1),))
            return dv.inc()

        def act(fn, inc=True):
            rec.wt("scalar", ac, ac.n)
            rec.em("scalar", lambda s, f=fn: f(s).then_inc(ac.h, 1),
                   incs=(("ac", 1),))
            return ac.inc()

        def sdma(out_ap, in_ap, **kw):
            rec.em("sync", lambda s, o=out_ap, i=in_ap, k=kw:
                   s.dma_start(out=o, in_=i, **k).then_inc(qX.h, 16),
                   incs=(("qX", 16),))
            return qX.inc(16)

        def gdma(out_ap, in_ap, **kw):
            rec.em("gpsimd", lambda g, o=out_ap, i=in_ap, k=kw:
                   g.dma_start(out=o, in_=i, **k).then_inc(qW.h, 16),
                   incs=(("qW", 16),))
            return qW.inc(16)

        _sig = {"sync": (qX, sX, []), "gpsimd": (qW, sW, [])}

        def dsig(queue, upto):
            qsem, ssem, lst = _sig[queue]
            for cov, val in lst:
                if cov >= upto:
                    return ssem, val
            rec.wt(queue, qsem, upto)
            rec.em(queue, lambda e, h=ssem.h: e.nop().then_inc(h, 1),
                   incs=((ssem.name, 1),))
            v = ssem.inc()
            lst.append((upto, v))
            return ssem, v

        def wtd(eng, queue, upto):
            if upto and upto > 0:
                s_, v_ = dsig(queue, upto)
                rec.wt(eng, s_, v_)

        def wt_any(eng, s_, v_):
            if s_ == "dma":
                wtd(eng, "gpsimd", v_)
            elif s_ == "dmas":
                wtd(eng, "sync", v_)
            else:
                rec.wt(eng, s_, v_)

        stage_free = [0, 0, 0, 0]
        stage_i = [0]

        def stage_acc(ps_ap, pe_val, dst_dram_ap, accum=True):
            s = stage_i[0] % 4
            stage_i[0] += 1
            wtd("scalar", "gpsimd", stage_free[s])
            rec.wt("scalar", pe, pe_val)
            slot = stage[:, s * CW:(s + 1) * CW]
            v = act(lambda sc, o=slot, p=ps_ap: sc.copy(out=o, in_=p))
            rec.wt("gpsimd", ac, v)
            kw = dict(accum_op=ALU.add) if accum else {}
            q = gdma(dst_dram_ap, slot, **kw)
            stage_free[s] = q
            return v

        def xdst_tile(xdst, g, n, w=CW):
            return bass.AP(xdst.tensor, g * P * TL + n * w, [[TL, P], [1, w]])

        def qfence():
            rec.wt("gpsimd", qW, qW.n)
            rec.wt("sync", pe, pe.n)
            rec.wt("sync", dv, dv.n)
            rec.wt("sync", ac, ac.n)
            rec.wt("gpsimd", pe, pe.n)
            rec.wt("gpsimd", dv, dv.n)
            rec.wt("gpsimd", ac, ac.n)

        sdma(cons[:, C_ONES:C_ONES + 1], wi["ones_col"])
        sdma(cons[:, C_TD:C_TD + KT], wi["td_T"])
        sdma(cons[:, C_TF:C_TF + KT], wi["tf_T"])
        sdma(cons[:, C_SEL:C_SEL + 8], wi["sel_rep"])
        sdma(onesrow[0:1, :], wi["ones_row"])
        sdma(gbcol[0:32, :], wi["gb_col"])
        sdma(gbrep[:, :], wi["gb_rep"])
        sdma(cons[:, C_FM:C_FM + 1], wi["fmask"])
        sdma(cons[:, C_EPS:C_EPS + 4], wi["epsv"])
        sdma(eye32f[0:32, :], wi["eye32"])
        gdma(cons16[:, B_ONES:B_ONES + 1], wi["ones_col"])
        gdma(cons16[:, B_EYE128:B_EYE128 + P], wi["eye128"])
        gdma(cons16[0:32, B_EYE32:B_EYE32 + 32], wi["eye32"])
        gdma(onesrow16[0:1, :], wi["ones_row"])
        g0_q = gdma(g0sb[:, :], wi["gate0T"][0])
        if n_layers > 1:
            sdma(cons[:, C_GNW:C_GNW + KT], wi["gnw"])
            sdma(cons[:, C_ALPHA:C_ALPHA + 8], wi["alpha_rep"])
            sdma(b1sb[:, :], wi["b1P"])
            g1_q = gdma(g1sb[:, :], wi["gate1T"][0])
        cst_q, cst_w = qX.n, qW.n
        wtd("scalar", "sync", cst_q)
        act(lambda s: s.activation(out=cons[:, C_DEC:C_DEC + KT],
                                   in_=cons[:, C_TD:C_TD + KT], func=AF.Exp),
            inc=False)
        dec_ready = act(lambda s: s.activation(out=cons[:, C_DEC:C_DEC + KT],
                                               in_=cons[:, C_DEC:C_DEC + KT],
                                               func=AF.Exp, scale=-1.0))
        wtd("vector", "sync", cst_q)
        ffm = dve(lambda v: v.tensor_scalar(out=cons[:, C_FF:C_FF + KT],
                                            in0=cons[:, C_TF:C_TF + KT],
                                            scalar1=cons[:, C_FM:C_FM + 1],
                                            scalar2=None, op0=ALU.mult))
        rec.wt("scalar", dv, ffm)
        ff_ready = act(lambda s: s.activation(out=cons[:, C_FF:C_FF + KT],
                                              in_=cons[:, C_FF:C_FF + KT],
                                              func=AF.Exp))
        gdma(xnext0, xres0)

        finals = cons[:, C_FIN:C_FIN + 32]
        carry = cons[:, C_CN:C_CN + 32]

        def proj_sweep(wgroups, kt, rhs_fn, evac_fn, m_list=None, ready=None):
            for g in (m_list if m_list is not None else range(len_w(wgroups))):
                sl, lv = ws_load(wgroups(g) if callable(wgroups) else wgroups[g],
                                 kt * P)
                wtd("tensor", "gpsimd", lv)
                if ready is not None:
                    rec.wt("tensor", ready[0], ready[1])
                for n in range(NCH):
                    bi, bk = pool.acquire()
                    for kk in range(kt):
                        mm(bk[:, :], ws_tile(sl, kk), rhs_fn(kk, n),
                           kk == 0, kk == kt - 1, inc=(kk == kt - 1))
                    sem, val = evac_fn(g, n, bk, pe.n)
                    pool.release(bi, sem, val)
                ws_done(sl, pe.n)

        def len_w(x):
            raise RuntimeError

        def emit_layer(l):
            xsrc = xres0 if l == 0 else xnext0
            xdst = xnext0 if l == 0 else xnext1
            cin, cout, cc = cc_in[l], cc_out[l], ccs[l]

            lay = ExitStack()
            with lay:
                le = lay.enter_context
                if l == 0:
                    xlat = le(nc.sbuf_tensor(_uname("xlat"), [P, 8 * TL], BF16))
                else:
                    logits_sb = le(nc.sbuf_tensor(_uname("logits1"), [8, TL], F32))
                    combT = le(nc.sbuf_tensor(_uname("combT"), [8, TL], BF16))

                xn_ctx = ExitStack()
                with xn_ctx:
                    xn = xn_ctx.enter_context(
                        nc.sbuf_tensor(_uname("xn"), [P, KT * TL], BF16))

                    def xn_t(kk, n, w=CW):
                        return xn[:, kk * TL + n * w: kk * TL + n * w + w]

                    p0 = ExitStack()
                    with p0:
                        pe0 = p0.enter_context
                        xch = [pe0(nc.sbuf_tensor(_uname(f"xch{i}"),
                                                  [P, KT * CW0], F32))
                               for i in range(2)]
                        xb16 = pe0(nc.sbuf_tensor(_uname("xb16"), [P, KT * CW0],
                                                  BF16))
                        xs16 = pe0(nc.sbuf_tensor(_uname("xs16"), [P, KT * CW0],
                                                  BF16))
                        if l > 0:
                            qfence()
                            wtd("sync", "gpsimd", qW.n)
                        xn_wr = [0, 0]
                        sq_prev = 0
                        xb_prev = 0
                        stats_prev = 0
                        for n in range(NC0):
                            j = n % 2
                            xc = xch[j]
                            src = bass.AP(xsrc.tensor, n * CW0,
                                          [[TL, P], [P * TL, KT], [1, CW0]])
                            rec.wt("sync", dv, xn_wr[j])
                            rec.wt("sync", ac, sq_prev)
                            lq = sdma(xc[:, :], src)
                            wtd("scalar", "sync", lq)
                            rec.wt("scalar", pe, stats_prev)
                            sq_prev = act(lambda s, x=xc: s.activation(
                                out=xs16[:, :], in_=x[:, :], func=AF.Square))
                            xb_prev = act(lambda s, x=xc: s.copy(out=xb16[:, :],
                                                                 in_=x[:, :]))
                            i1, b1 = pool.acquire()
                            i2, b2 = pool.acquire()
                            rec.wt("tensor", ac, xb_prev)
                            wtd("tensor", "gpsimd", cst_w)
                            for kk in range(KT):
                                mm(b1[0:1, 0:CW0], cons16[:, B_ONES:B_ONES + 1],
                                   xb16[:, kk * CW0:(kk + 1) * CW0],
                                   kk == 0, kk == KT - 1)
                            for kk in range(KT):
                                mm(b2[0:1, 0:CW0], cons16[:, B_ONES:B_ONES + 1],
                                   xs16[:, kk * CW0:(kk + 1) * CW0],
                                   kk == 0, kk == KT - 1,
                                   inc=(kk == KT - 1))
                            stats_prev = pe.n
                            mrow = rows[0:1, 0:CW0]
                            vrow = rows[0:1, CW0:2 * CW0]
                            rrow = rows[0:1, 2 * CW0:3 * CW0]
                            brow = rows[0:1, 3 * CW0:4 * CW0]
                            rec.wt("vector", pe, stats_prev)
                            dve(lambda v, b=b1: v.tensor_scalar(
                                out=mrow, in0=b[0:1, 0:CW0], scalar1=1.0 / D,
                                scalar2=None, op0=ALU.mult), inc=False)
                            dve(lambda v, b=b2: v.tensor_scalar(
                                out=vrow, in0=b[0:1, 0:CW0], scalar1=1.0 / D,
                                scalar2=None, op0=ALU.mult), inc=False)
                            dve(lambda v: v.tensor_tensor(out=brow, in0=mrow,
                                                          in1=mrow,
                                                          op=ALU.mult), inc=False)
                            vv = dve(lambda v: v.tensor_tensor(out=vrow, in0=vrow,
                                                               in1=brow,
                                                               op=ALU.subtract))
                            pool.release(i1, dv, vv)
                            pool.release(i2, dv, vv)
                            rec.wt("scalar", dv, vv)
                            act(lambda s: s.activation(
                                out=rrow, in_=vrow, func=AF.Ln,
                                bias=cons[0:1, C_EPS:C_EPS + 1]), inc=False)
                            rw = act(lambda s: s.activation(out=rrow, in_=rrow,
                                                            func=AF.Exp, scale=-0.5))
                            rec.wt("vector", ac, rw)
                            bw = dve(lambda v: v.tensor_tensor(out=brow, in0=mrow,
                                                               in1=rrow,
                                                               op=ALU.mult))
                            i3, b3 = pool.acquire()
                            i4, b4 = pool.acquire()
                            rec.wt("tensor", ac, rw)
                            av = mm(b3[:, 0:CW0], onesrow[0:1, :], rrow, True, True,
                                    inc=True)
                            rec.wt("tensor", dv, bw)
                            bv = mm(b4[:, 0:CW0], onesrow[0:1, :], brow, True, True,
                                    inc=True)
                            abj = abc[:, (j) * CW0:(j + 1) * CW0]
                            bbj = bbc[:, (j) * CW0:(j + 1) * CW0]
                            rec.wt("scalar", pe, av)
                            rec.wt("scalar", dv, xn_wr[j])
                            a_c = act(lambda s, b=b3, o=abj: s.copy(
                                out=o, in_=b[:, 0:CW0]))
                            rec.wt("scalar", pe, bv)
                            b_c = act(lambda s, b=b4, o=bbj: s.copy(
                                out=o, in_=b[:, 0:CW0]))
                            pool.release(i3, ac, a_c)
                            pool.release(i4, ac, b_c)
                            rec.wt("vector", ac, b_c)
                            for kk in range(KT):
                                dve(lambda v, k=kk, x=xc, o=bbj: v.tensor_tensor(
                                    out=x[:, k * CW0:(k + 1) * CW0],
                                    in0=x[:, k * CW0:(k + 1) * CW0],
                                    in1=o, op=ALU.subtract), inc=False)
                                xn_wr[j] = dve(lambda v, k=kk, nn=n, x=xc, o=abj:
                                               v.tensor_tensor(
                                                   out=xn_t(k, nn, CW0),
                                                   in0=x[:, k * CW0:(k + 1) * CW0],
                                                   in1=o, op=ALU.mult))
                        xn_done = max(xn_wr)
                    if l == 0 and "xn" in tap_out:
                        rec.wt("gpsimd", dv, xn_done)
                        gdma(tap_out["xn"], xn[:, :])
                    if l > 0:
                        rec.wt("sync", dv, xn_done)
                        xn_dump_q = sdma(
                            bass.AP(sp_xn.tensor, 0,
                                    [[TL, P], [P * TL, KT], [1, TL]]),
                            xn[:, :])
                        wtd("sync", "gpsimd", state["l0_acc_q"])
                        x1cp = [sdma(bass.AP(xnext1.tensor,
                                             q_ * (D // 4) * TL,
                                             [[TL, D // 4], [1, TL]]),
                                     bass.AP(xnext0.tensor,
                                             q_ * (D // 4) * TL,
                                             [[TL, D // 4], [1, TL]]))
                                for q_ in range(4)]
                        state["x1cp_q"] = x1cp[-1]

                    p1 = ExitStack()
                    with p1:
                        pe1 = p1.enter_context
                        if l == 0:
                            wm = pe1(nc.sbuf_tensor(_uname("wm"), [P, TL], F32))
                            wex = [pe1(nc.sbuf_tensor(_uname(f"wex{i}"), [P, TL], BF16))
                                   for i in range(2)]
                            um = [pe1(nc.sbuf_tensor(_uname(f"um{i}"), [P, TL], BF16))
                                  for i in range(2)]
                            srm = pe1(nc.sbuf_tensor(_uname("srm"), [P, TL], BF16))
                            scr = pe1(nc.sbuf_tensor(_uname("scr"), [P, TL], F32))
                            sw1 = WSweep(
                                [(wi[w][m], KT * P) for m in range(KT)
                                 for w in ("wkT", "wvT", "wrT")])
                            wex_free = [[], []]
                            um_free = [[], []]
                            srm_free = []
                            wm_free = []
                            for m in range(KT):
                                j = m % 2
                                dcol = cons[:, C_DEC + m:C_DEC + m + 1]
                                sl, lv = sw1.get(3 * m)
                                wtd("tensor", "gpsimd", lv)
                                if m == 0:
                                    rec.wt("tensor", dv, xn_done)
                                banks = []
                                for n in range(NCH):
                                    bi, bk = pool.acquire()
                                    for kk in range(KT):
                                        mm(bk[:, :], ws_tile(sl, kk), xn_t(kk, n),
                                           kk == 0, kk == KT - 1, inc=(kk == KT - 1))
                                    banks.append((bi, bk, pe.n))
                                ws_done(sl, pe.n)
                                for s_, v_ in wm_free:
                                    wt_any("vector", s_, v_)
                                wm_free = []
                                for n, (bi, bk, pv) in enumerate(banks):
                                    rec.wt("vector", pe, pv)
                                    cv = dve(lambda v, b=bk, n_=n: v.tensor_scalar(
                                        out=wm[:, n_ * CW:(n_ + 1) * CW], in0=b[:, :],
                                        scalar1=-10.0, scalar2=10.0,
                                        op0=ALU.max, op1=ALU.min))
                                    pool.release(bi, dv, cv)
                                clip_v = dv.n
                                wb = wex[j]
                                for s_, v_ in wex_free[j]:
                                    wt_any("scalar", s_, v_)
                                wex_free[j] = []
                                rec.wt("scalar", dv, clip_v)
                                rec.wt("scalar", ac, ff_ready)
                                ev = act(lambda s, w=wb: s.activation(
                                    out=w[:, :], in_=wm[:, :], func=AF.Exp))
                                wm_free = [(ac, ev)]
                                rec.wt("vector", ac, ev)
                                fv = dve(lambda v, w=wb, m_=m: v.tensor_tensor(
                                    out=w[:, 0:1], in0=w[:, 0:1],
                                    in1=cons[:, C_FF + m_:C_FF + m_ + 1],
                                    op=ALU.mult))
                                rec.wt("sync", dv, fv)
                                wq = sdma(sp_w[m * P:(m + 1) * P, :], wb[:, :])
                                d_bc = bass.broadcast_tensor_aps(wb[:, :], dcol)[1]
                                rec.wt("vector", ac, dec_ready)
                                dve(lambda v, w=wb, db=d_bc: v.tensor_tensor_scan(
                                    out=scr[:, :], data0=db, data1=w[:, :],
                                    initial=0.0, op0=ALU.mult, op1=ALU.add),
                                    inc=False)
                                dsv = dve(lambda v, m_=m: v.tensor_copy(
                                    out=finals[:, 16 + m_:17 + m_],
                                    in_=scr[:, TL - 1:TL]))
                                sl2, lv2 = sw1.get(3 * m + 1)
                                wtd("tensor", "gpsimd", lv2)
                                banks = []
                                for n in range(NCH):
                                    bi, bk = pool.acquire()
                                    for kk in range(KT):
                                        mm(bk[:, :], ws_tile(sl2, kk), xn_t(kk, n),
                                           kk == 0, kk == KT - 1, inc=(kk == KT - 1))
                                    banks.append((bi, bk, pe.n))
                                ws_done(sl2, pe.n)
                                ub = um[j]
                                for s_, v_ in um_free[j]:
                                    wt_any("vector", s_, v_)
                                um_free[j] = []
                                for n, (bi, bk, pv) in enumerate(banks):
                                    rec.wt("vector", pe, pv)
                                    uv = dve(lambda v, b=bk, n_=n, w=wb, u=ub:
                                             v.tensor_tensor(
                                                 out=u[:, n_ * CW:(n_ + 1) * CW],
                                                 in0=w[:, n_ * CW:(n_ + 1) * CW],
                                                 in1=b[:, :], op=ALU.mult))
                                    pool.release(bi, dv, uv)
                                u_v = dv.n
                                rec.wt("sync", dv, u_v)
                                uq = sdma(sp_u[m * P:(m + 1) * P, :], ub[:, :])
                                wex_free[j] = [("dmas", wq), (dv, u_v)]
                                dve(lambda v, u=ub, db=d_bc: v.tensor_tensor_scan(
                                    out=scr[:, :], data0=db, data1=u[:, :],
                                    initial=0.0, op0=ALU.mult, op1=ALU.add),
                                    inc=False)
                                nsv = dve(lambda v, m_=m: v.tensor_copy(
                                    out=finals[:, m_:m_ + 1], in_=scr[:, TL - 1:TL]))
                                um_free[j] = [("dmas", uq), (dv, nsv)]
                                sl3, lv3 = sw1.get(3 * m + 2)
                                wtd("tensor", "gpsimd", lv3)
                                for s_, v_ in srm_free:
                                    wt_any("scalar", s_, v_)
                                srm_free = []
                                for n in range(NCH):
                                    bi, bk = pool.acquire()
                                    for kk in range(KT):
                                        mm(bk[:, :], ws_tile(sl3, kk), xn_t(kk, n),
                                           kk == 0, kk == KT - 1, inc=(kk == KT - 1))
                                    rv = act(lambda s, b=bk, n_=n, pv=pe.n:
                                             (s.wait_ge(pe.h, pv),
                                              s.activation(
                                                  out=srm[:, n_ * CW:(n_ + 1) * CW],
                                                  in_=b[:, :], func=AF.Sigmoid))[1]
                                             ) if False else None
                                    rec.wt("scalar", pe, pe.n)
                                    rv = act(lambda s, b=bk, n_=n: s.activation(
                                        out=srm[:, n_ * CW:(n_ + 1) * CW],
                                        in_=b[:, :], func=AF.Sigmoid))
                                    pool.release(bi, ac, rv)
                                ws_done(sl3, pe.n)
                                rec.wt("sync", ac, ac.n)
                                rq = sdma(sp_r[m * P:(m + 1) * P, :], srm[:, :])
                                srm_free = [("dmas", rq)]
                            scan1_done = dv.n
                        else:
                            am = [pe1(nc.sbuf_tensor(_uname(f"am{i}"), [P, TL], BF16))
                                  for i in range(2)]
                            sgm = pe1(nc.sbuf_tensor(_uname("sgm"), [P, TL], BF16))
                            iim = [pe1(nc.sbuf_tensor(_uname(f"iim{i}"), [P, TL], BF16))
                                   for i in range(2)]
                            scr = pe1(nc.sbuf_tensor(_uname("scr1"), [P, TL], F32))
                            sw1b = WSweep(
                                [(wi[w][m], KT * P) for m in range(KT)
                                 for w in ("wfT", "wiT", "wgT")])
                            am_free = [[], []]
                            iim_free = [[], []]
                            sgm_free = []
                            for m in range(KT):
                                j = m % 2
                                sl, lv = sw1b.get(3 * m)
                                wtd("tensor", "gpsimd", lv)
                                if m == 0:
                                    rec.wt("tensor", dv, xn_done)
                                ab = am[j]
                                for s_, v_ in am_free[j]:
                                    wt_any("scalar", s_, v_)
                                am_free[j] = []
                                for s_, v_ in sgm_free:
                                    wt_any("scalar", s_, v_)
                                sgm_free = []
                                for n in range(NCH):
                                    bi, bk = pool.acquire()
                                    for kk in range(KT):
                                        mm(bk[:, :], ws_tile(sl, kk), xn_t(kk, n),
                                           kk == 0, kk == KT - 1, inc=(kk == KT - 1))
                                    rec.wt("scalar", pe, pe.n)
                                    act(lambda s, b=bk, n_=n, a_=ab: s.activation(
                                        out=a_[:, n_ * CW:(n_ + 1) * CW],
                                        in_=b[:, :], func=AF.Sigmoid), inc=False)
                                    gv = act(lambda s, b=bk, n_=n: s.activation(
                                        out=sgm[:, n_ * CW:(n_ + 1) * CW],
                                        in_=b[:, :], func=AF.Sigmoid, scale=-1.0))
                                    pool.release(bi, ac, gv)
                                a_done = ac.n
                                ws_done(sl, pe.n)
                                rec.wt("sync", ac, a_done)
                                aq = sdma(sp_w[m * P:(m + 1) * P, :], ab[:, :])
                                sl2, lv2 = sw1b.get(3 * m + 1)
                                wtd("tensor", "gpsimd", lv2)
                                ib = iim[j]
                                for s_, v_ in iim_free[j]:
                                    wt_any("scalar", s_, v_)
                                iim_free[j] = []
                                for n in range(NCH):
                                    bi, bk = pool.acquire()
                                    for kk in range(KT):
                                        mm(bk[:, :], ws_tile(sl2, kk), xn_t(kk, n),
                                           kk == 0, kk == KT - 1, inc=(kk == KT - 1))
                                    rec.wt("scalar", pe, pe.n)
                                    sv_ = act(lambda s, b=bk, n_=n, i_=ib:
                                              s.activation(
                                                  out=i_[:, n_ * CW:(n_ + 1) * CW],
                                                  in_=b[:, :], func=AF.Silu))
                                    pool.release(bi, ac, sv_)
                                ws_done(sl2, pe.n)
                                rec.wt("vector", ac, ac.n)
                                iv = dve(lambda v, i_=ib: v.tensor_tensor(
                                    out=i_[:, :], in0=i_[:, :], in1=sgm[:, :],
                                    op=ALU.mult))
                                rec.wt("sync", dv, iv)
                                iq = sdma(sp_u[m * P:(m + 1) * P, :], ib[:, :])
                                dve(lambda v, a_=ab, i_=ib: v.tensor_tensor_scan(
                                    out=scr[:, :], data0=a_[:, :], data1=i_[:, :],
                                    initial=0.0, op0=ALU.mult, op1=ALU.add),
                                    inc=False)
                                fsv = dve(lambda v, m_=m: v.tensor_copy(
                                    out=finals[:, m_:m_ + 1], in_=scr[:, TL - 1:TL]))
                                am_free[j] = [("dmas", aq), (dv, fsv)]
                                iim_free[j] = [("dmas", iq), (dv, fsv)]
                                sl3, lv3 = sw1b.get(3 * m + 2)
                                wtd("tensor", "gpsimd", lv3)
                                rec.wt("scalar", dv, iv)
                                for n in range(NCH):
                                    bi, bk = pool.acquire()
                                    for kk in range(KT):
                                        mm(bk[:, :], ws_tile(sl3, kk), xn_t(kk, n),
                                           kk == 0, kk == KT - 1, inc=(kk == KT - 1))
                                    rec.wt("scalar", pe, pe.n)
                                    gv2 = act(lambda s, b=bk, n_=n: s.activation(
                                        out=sgm[:, n_ * CW:(n_ + 1) * CW],
                                        in_=b[:, :], func=AF.Silu))
                                    pool.release(bi, ac, gv2)
                                ws_done(sl3, pe.n)
                                rec.wt("sync", ac, ac.n)
                                gq = sdma(sp_r[m * P:(m + 1) * P, :], sgm[:, :])
                                sgm_free = [("dmas", gq)]
                            scan1_done = dv.n
                            wtd("tensor", "gpsimd", g1_q)
                            for n in range(NCH):
                                bi, bk = pool.acquire()
                                for kk in range(KT):
                                    mm(bk[0:8, :], g1sb[:, kk * 8:(kk + 1) * 8],
                                       xn_t(kk, n), kk == 0, kk == KT - 1,
                                       inc=(kk == KT - 1))
                                rec.wt("scalar", pe, pe.n)
                                lv_ = act(lambda s, b=bk, n_=n: s.copy(
                                    out=logits_sb[0:8, n_ * CW:(n_ + 1) * CW],
                                    in_=b[0:8, :]))
                                pool.release(bi, ac, lv_)
                            logits_done = ac.n
                            emit_l1_routing(logits_sb, logits_done, combT)

                    rec.wt("gpsimd", dv, scan1_done)
                    gdma(cin, finals)
                    rec.wt("gpsimd", qW, qW.n)
                    rec.em("gpsimd", lambda g, i=cin, o=cout, c=cc.h:
                           g.collective_compute(
                               "AllGather", ALU.bypass,
                               replica_groups=[list(range(8))],
                               ins=[i.opt()], outs=[o.opt()]).then_inc(c),
                           incs=((cc.name, 1),))
                    cc.inc()

                    def consume_carry():
                        rec.wt("gpsimd", cc, 1)
                        ag_ap = bass.AP(cout.tensor, 0,
                                        [[32, P], [P * 32, 8], [1, 32]])
                        agq = gdma(agsb[:, :], ag_ap)
                        wtd("vector", "gpsimd", agq)
                        sel3 = cons[:, C_SEL:C_SEL + 8].rearrange(
                            "p (r o) -> p r o", o=1)
                        ag3 = agsb[:, :].rearrange("p (r c) -> p r c", r=8)
                        sel_b = bass.broadcast_tensor_aps(ag3, sel3)[1]
                        dve(lambda v: v.tensor_tensor(out=ag3, in0=ag3, in1=sel_b,
                                                      op=ALU.mult), inc=False)
                        red_in = bass.AP(agsb[:, :].tensor, agsb[:, :].offset,
                                         [agsb[:, :].ap[0], [1, 32], [32, 8]])
                        carry3 = carry.rearrange("p (c o) -> p c o", o=1)
                        return dve(lambda v: v.tensor_reduce(
                            out=carry3, in_=red_in, axis=AXX, op=ALU.add))

                    if l == 0:
                        emit_l0_p3a(xn_t, xn_done, xlat)
                        xn_last_pe = pe.n
                    else:
                        xn_last_pe = pe.n
                if l == 0:
                    carry_v = consume_carry()
                    emit_l0_p3b(xlat, xdst, carry_v, xn_last_pe)
                    emit_l0_p4(xlat, xdst)
                    state["l0_acc_q"] = qW.n
                    if "x1" in tap_out:
                        rec.wt("gpsimd", qW, qW.n)
                        gdma(tap_out["x1"], xnext0)
                else:
                    emit_l1_p3(logits_sb, logits_done, combT, xdst, consume_carry,
                               xn_last_pe)
                    emit_l1_p4(combT, xdst, xn_dump_q)
            return None

        def emit_l0_p3a(xn_t, xn_done, xlat):
            p3 = ExitStack()
            with p3:
                pe3 = p3.enter_context
                scores = pe3(nc.sbuf_tensor(_uname("scores"), [32, TL], F32))
                sbbf = scores
                tm = pe3(nc.sbuf_tensor(_uname("tm"), [P, KT * 32], F32))
                tms = pe3(nc.sbuf_tensor(_uname("tms"), [P, KT * 32], F32))
                sel8 = pe3(nc.sbuf_tensor(_uname("sel8"), [P, KT * 32], F32))
                t4 = pe3(nc.sbuf_tensor(_uname("t4"), [P, KT * 32], F32))
                red1 = pe3(nc.sbuf_tensor(_uname("red1"), [P, KT * 4], F32))
                red2 = pe3(nc.sbuf_tensor(_uname("red2"), [P, KT * 4], F32))
                scw16 = pe3(nc.sbuf_tensor(_uname("scw16"), [P, KT * 32], BF16))
                scwT = pe3(nc.sbuf_tensor(_uname("scwT"), [32, TL], BF16))
                up16 = pe3(nc.sbuf_tensor(_uname("up16"), [P, 8 * TL], BF16))
                relu_t = pe3(nc.sbuf_tensor(_uname("relu_t"), [P, 2 * CW], F32))

                wtd("tensor", "gpsimd", g0_q)
                rec.wt("tensor", dv, xn_done)
                for n in range(NCH):
                    bi, bk = pool.acquire()
                    for kk in range(KT):
                        mm(bk[0:32, :], g0sb[:, kk * 32:(kk + 1) * 32],
                           xn_t(kk, n), kk == 0, kk == KT - 1, inc=(kk == KT - 1))
                    rec.wt("scalar", pe, pe.n)
                    sv_ = act(lambda s, b=bk, n_=n: s.activation(
                        out=scores[0:32, n_ * CW:(n_ + 1) * CW],
                        in_=b[0:32, :], func=AF.Sigmoid))
                    pool.release(bi, ac, sv_)
                rec.wt("vector", ac, ac.n)
                wtd("vector", "sync", cst_q)
                sb16v = dve(lambda v: v.tensor_scalar(
                    out=sbbf[0:32, :], in0=scores[0:32, :],
                    scalar1=gbcol[0:32, 0:1], scalar2=None, op0=ALU.add))
                rec.wt("gpsimd", qW, cst_w)
                prev_copy = 0
                for rnd in range(4):
                    bi_t, bk_t = pool.acquire()
                    rec.wt("tensor", dv, sb16v)
                    rec.wt("tensor", ac, prev_copy)
                    for c in range(4):
                        cg = rnd * 4 + c
                        inc = c == 3
                        rec.em("tensor", lambda t, c_=c, cg_=cg, b=bk_t, i_=inc:
                               (t.transpose(b[:, c_ * 32:(c_ + 1) * 32],
                                            sbbf[0:32, cg_ * P:(cg_ + 1) * P],
                                            eye32f[0:32, 0:32]).then_inc(pe.h, 1)
                                if i_ else
                                t.transpose(b[:, c_ * 32:(c_ + 1) * 32],
                                            sbbf[0:32, cg_ * P:(cg_ + 1) * P],
                                            eye32f[0:32, 0:32])),
                               incs=((("pe", 1),) if inc else ()))
                    tv = pe.inc()
                    rec.wt("scalar", pe, tv)
                    prev_copy = act(lambda s, r=rnd, b=bk_t: s.copy(
                        out=tm[:, r * 128:(r + 1) * 128], in_=b[:, 0:128]))
                    pool.release(bi_t, ac, prev_copy)
                rec.wt("vector", ac, prev_copy)
                tm3 = tm[:, :].rearrange("p (c e) -> p c e", e=32)
                gb3 = gbrep[:, :].rearrange("p (o e) -> p o e", o=1)
                gb_b = bass.broadcast_tensor_aps(tm3, gb3)[1]
                tms3 = tms[:, :].rearrange("p (c e) -> p c e", e=32)
                dve(lambda v: v.tensor_tensor(out=tms3, in0=tm3, in1=gb_b,
                                              op=ALU.subtract), inc=False)
                tm4 = tm[:, :].rearrange("p (c g e) -> p c g e", g=4, e=8)
                r14 = red1[:, :].rearrange("p (c g o) -> p c g o", g=4, o=1)
                dve(lambda v: v.tensor_reduce(out=r14, in_=tm4, axis=AXX,
                                              op=ALU.max), inc=False)
                r1b = bass.broadcast_tensor_aps(tm4, r14)[1]
                t44 = t4[:, :].rearrange("p (c g e) -> p c g e", g=4, e=8)
                dve(lambda v: v.tensor_tensor(out=t44, in0=tm4, in1=r1b,
                                              op=ALU.is_equal), inc=False)
                dve(lambda v: v.scalar_tensor_tensor(out=t44, in0=t44, scalar=-BIG,
                                                     in1=tm4, op0=ALU.mult,
                                                     op1=ALU.add), inc=False)
                r24 = red2[:, :].rearrange("p (c g o) -> p c g o", g=4, o=1)
                dve(lambda v: v.tensor_reduce(out=r24, in_=t44, axis=AXX,
                                              op=ALU.max), inc=False)
                dve(lambda v: v.tensor_tensor(out=red1[:, :], in0=red1[:, :],
                                              in1=red2[:, :], op=ALU.add), inc=False)
                gs3 = red1[:, :].rearrange("p (c g) -> p c g", g=4)
                g14 = red2[:, 0:KT].rearrange("p (c o) -> p c o", o=1)
                dve(lambda v: v.tensor_reduce(out=g14, in_=gs3, axis=AXX,
                                              op=ALU.max), inc=False)
                g1b = bass.broadcast_tensor_aps(gs3, g14)[1]
                e1g = t4[:, 0:64].rearrange("p (c g) -> p c g", g=4)
                dve(lambda v: v.tensor_tensor(out=e1g, in0=gs3, in1=g1b,
                                              op=ALU.is_equal), inc=False)
                dve(lambda v: v.scalar_tensor_tensor(out=gs3, in0=e1g, scalar=-BIG,
                                                     in1=gs3, op0=ALU.mult,
                                                     op1=ALU.add), inc=False)
                g24 = red2[:, KT:2 * KT].rearrange("p (c o) -> p c o", o=1)
                dve(lambda v: v.tensor_reduce(out=g24, in_=gs3, axis=AXX,
                                              op=ALU.max), inc=False)
                g2b = bass.broadcast_tensor_aps(gs3, g24)[1]
                dve(lambda v: v.tensor_tensor(out=gs3, in0=gs3, in1=g2b,
                                              op=ALU.is_equal), inc=False)
                dve(lambda v: v.tensor_tensor(out=gs3, in0=gs3, in1=e1g,
                                              op=ALU.add), inc=False)
                gm4 = red1[:, :].rearrange("p (c g o) -> p c g o", g=4, o=1)
                gm_b = bass.broadcast_tensor_aps(tm4, gm4)[1]
                dve(lambda v: v.tensor_scalar(out=t44, in0=gm_b, scalar1=BIG,
                                              scalar2=-BIG, op0=ALU.mult,
                                              op1=ALU.add), inc=False)
                dve(lambda v: v.tensor_tensor(out=t44, in0=t44, in1=tm4,
                                              op=ALU.add), inc=False)
                tmm = t4[:, :].rearrange("p (c e) -> p c e", e=32)
                mi = red2[:, 0:KT].rearrange("p (c o) -> p c o", o=1)
                sel3_ = sel8[:, :].rearrange("p (c e) -> p c e", e=32)
                eq3 = tm[:, :].rearrange("p (c e) -> p c e", e=32)
                for it in range(8):
                    dve(lambda v: v.tensor_reduce(out=mi, in_=tmm, axis=AXX,
                                                  op=ALU.max), inc=False)
                    mib = bass.broadcast_tensor_aps(tmm, mi)[1]
                    if it == 0:
                        dve(lambda v: v.tensor_tensor(out=sel3_, in0=tmm, in1=mib,
                                                      op=ALU.is_equal), inc=False)
                        dve(lambda v: v.scalar_tensor_tensor(
                            out=tmm, in0=sel3_, scalar=-BIG, in1=tmm,
                            op0=ALU.mult, op1=ALU.add), inc=False)
                    else:
                        dve(lambda v: v.tensor_tensor(out=eq3, in0=tmm, in1=mib,
                                                      op=ALU.is_equal), inc=False)
                        dve(lambda v: v.tensor_tensor(out=sel3_, in0=sel3_, in1=eq3,
                                                      op=ALU.add), inc=False)
                        dve(lambda v: v.scalar_tensor_tensor(
                            out=tmm, in0=eq3, scalar=-BIG, in1=tmm,
                            op0=ALU.mult, op1=ALU.add), inc=False)
                dve(lambda v: v.tensor_tensor(out=tms3, in0=tms3, in1=sel3_,
                                              op=ALU.mult), inc=False)
                ssum = red2[:, 0:KT].rearrange("p (c o) -> p c o", o=1)
                dve(lambda v: v.tensor_reduce(out=ssum, in_=tms3, axis=AXX,
                                              op=ALU.add), inc=False)
                dve(lambda v: v.reciprocal(out=red2[:, 0:KT], in_=red2[:, 0:KT]),
                    inc=False)
                sb_b = bass.broadcast_tensor_aps(tms3, ssum)[1]
                dve(lambda v: v.tensor_tensor(out=tms3, in0=tms3, in1=sb_b,
                                              op=ALU.mult), inc=False)
                cwv = dve(lambda v: v.tensor_scalar(out=tms[:, :], in0=tms[:, :],
                                                    scalar1=2.5, scalar2=None,
                                                    op0=ALU.mult))
                rec.wt("scalar", dv, cwv)
                act(lambda s: s.activation(out=tms[:, :], in_=tms[:, :], func=AF.Ln,
                                           bias=cons[:, C_EPS + 2:C_EPS + 3]), inc=False)
                scv = act(lambda s: s.activation(out=tms[:, :], in_=tms[:, :],
                                                 func=AF.Exp, scale=0.5))
                rec.wt("vector", ac, scv)
                s16v = dve(lambda v: v.tensor_copy(out=scw16[:, :], in_=tms[:, :]))
                prev_copy2 = 0
                for rnd in range(4):
                    rec.wt("tensor", dv, s16v)
                    rec.wt("tensor", ac, prev_copy2)
                    for c in range(4):
                        cg = rnd * 4 + c
                        src = bass.AP(scw16[:, :].tensor,
                                      scw16[:, :].offset + cg * 32,
                                      [scw16[:, :].ap[0], [1, 32]])
                        if c < 3:
                            rec.em("tensor", lambda t, c_=c, s_=src:
                                   t.transpose(ps16[0:32, c_ * 128:(c_ + 1) * 128],
                                               s_, cons16[:, B_EYE128:B_EYE128 + P]))
                        else:
                            rec.em("tensor", lambda t, c_=c, s_=src:
                                   t.transpose(ps16[0:32, c_ * 128:(c_ + 1) * 128],
                                               s_, cons16[:, B_EYE128:B_EYE128 + P])
                                   .then_inc(pe.h, 1), incs=(("pe", 1),))
                            tv = pe.inc()
                    rec.wt("scalar", pe, tv)
                    prev_copy2 = act(lambda s, r=rnd: s.copy(
                        out=scwT[0:32, r * CW:(r + 1) * CW], in_=ps16[0:32, 0:CW]))
                rec.wt("sync", ac, prev_copy2)
                scw_q = sdma(scw_d, scwT[0:32, :])
                state["scw_q"] = scw_q

                swf = WSweep([(wi["fc1T"][g], KT * P) for g in range(8)])
                for g in range(8):
                    sl, lv = swf.get(g)
                    wtd("tensor", "gpsimd", lv)
                    for n in range(NCH):
                        bi, bk = pool.acquire()
                        for kk in range(KT):
                            mm(bk[:, :], ws_tile(sl, kk), xn_t(kk, n),
                               kk == 0, kk == KT - 1, inc=(kk == KT - 1))
                        rec.wt("scalar", pe, pe.n)
                        xv = act(lambda s, b=bk, g_=g, n_=n: s.copy(
                            out=xlat[:, g_ * TL + n_ * CW:g_ * TL + (n_ + 1) * CW],
                            in_=b[:, :]))
                        pool.release(bi, ac, xv)
                    ws_done(sl, pe.n)
                state["xlat_done"] = ac.n

                shg = []
                for ih_ in range(4):
                    for g_ in range(8):
                        shg.append((wi["upT"][ih_, g_], KT * P))
                    for g2_ in range(KT):
                        shg.append((bass.AP(wi["dnT"].tensor,
                                            g2_ * P * 32 * P + ih_ * 8 * P,
                                            [[32 * P, P], [1, 8 * P]]), 8 * P))
                swsh = WSweep(shg)
                for ih in range(4):
                    up_last = 0
                    for g in range(8):
                        sl, lv = swsh.get(ih * 24 + g)
                        wtd("tensor", "gpsimd", lv)
                        for n in range(NCH):
                            bi, bk = pool.acquire()
                            for kk in range(KT):
                                mm(bk[:, :], ws_tile(sl, kk), xn_t(kk, n),
                                   kk == 0, kk == KT - 1, inc=(kk == KT - 1))
                            rec.wt("vector", pe, pe.n)
                            rec.wt("vector", ac, up_last)
                            rt = relu_t[:, (n % 2) * CW:(n % 2 + 1) * CW]
                            rv_ = dve(lambda v, b=bk, r=rt: v.tensor_scalar(
                                out=r, in0=b[:, :], scalar1=0.0, scalar2=None,
                                op0=ALU.max))
                            pool.release(bi, dv, rv_)
                            rec.wt("scalar", dv, rv_)
                            up_last = act(lambda s, g_=g, n_=n, r=rt: s.activation(
                                out=up16[:, g_ * TL + n_ * CW:
                                         g_ * TL + (n_ + 1) * CW],
                                in_=r, func=AF.Square))
                        ws_done(sl, pe.n)
                    rec.wt("tensor", ac, up_last)
                    for g2 in range(KT):
                        sl, lv = swsh.get(ih * 24 + 8 + g2)
                        wtd("tensor", "gpsimd", lv)
                        for n in range(NCH):
                            bi, bk = pool.acquire()
                            for kk in range(8):
                                mm(bk[:, :], ws_tile(sl, kk),
                                   up16[:, kk * TL + n * CW:kk * TL + (n + 1) * CW],
                                   kk == 0, kk == 7, inc=(kk == 7))
                            if dbg_split:
                                stage_acc(bk[:, :], pe.n,
                                          xdst_tile(dbg["dbg_sh"], g2, n),
                                          accum=(ih > 0))
                            else:
                                stage_acc(bk[:, :], pe.n, xdst_tile(xnext0, g2, n))
                            pool.release(bi, ac, ac.n)
                        ws_done(sl, pe.n)

        def emit_l0_p3b(xlat, xdst, carry_v, xn_last_pe):
            TH = TL // 2
            p3b = ExitStack()
            with p3b:
                pb = p3b.enter_context
                rwkvH = [pb(nc.sbuf_tensor(_uname(f"rwkvH{i}"), [P, KT * TH],
                                           BF16)) for i in range(2)]
                wrb = [pb(nc.sbuf_tensor(_uname(f"wrb{i}"), [P, TH], BF16))
                       for i in range(2)]
                urb = [pb(nc.sbuf_tensor(_uname(f"urb{i}"), [P, TH], BF16))
                       for i in range(2)]
                srb = [pb(nc.sbuf_tensor(_uname(f"srb{i}"), [P, TH], BF16))
                       for i in range(2)]
                numb = pb(nc.sbuf_tensor(_uname("numb"), [P, TH], F32))
                denb = pb(nc.sbuf_tensor(_uname("denb"), [P, TH], F32))
                tb = pb(nc.sbuf_tensor(_uname("tb"), [P, TH], F32))
                stH = pb(nc.sbuf_tensor(_uname("stH"), [P, 32], F32))
                qfence()
                swo = WSweep([(wi["woT"][g], KT * P) for g in range(KT)] * 2)
                rb_free = [[0, 0], [0, 0], [0, 0]]
                rwkv_done_H = [0, 0]
                for H in range(2):
                    rwkv = rwkvH[H]
                    for kk in range(KT):
                        j = kk % 2
                        rec.wt("sync", dv, rb_free[0][j])
                        wq_ = sdma(wrb[j][:, :],
                                   sp_w[kk * P:(kk + 1) * P, H * TH:(H + 1) * TH])
                        rec.wt("sync", dv, rb_free[1][j])
                        uq_ = sdma(urb[j][:, :],
                                   sp_u[kk * P:(kk + 1) * P, H * TH:(H + 1) * TH])
                        rec.wt("sync", dv, rb_free[2][j])
                        rq_ = sdma(srb[j][:, :],
                                   sp_r[kk * P:(kk + 1) * P, H * TH:(H + 1) * TH])
                        dcol = cons[:, C_DEC + kk:C_DEC + kk + 1]
                        d_bc = bass.broadcast_tensor_aps(numb[:, :], dcol)[1]
                        wtd("vector", "sync", max(wq_, uq_))
                        rec.wt("vector", dv, carry_v)
                        rec.wt("vector", pe, xn_last_pe)
                        nini = (cons[:, C_CN + kk:C_CN + kk + 1] if H == 0
                                else stH[:, kk:kk + 1])
                        dini = (cons[:, C_CN + 16 + kk:C_CN + 17 + kk] if H == 0
                                else stH[:, 16 + kk:17 + kk])
                        nv = dve(lambda v, u=urb[j], db=d_bc, ini=nini:
                                 v.tensor_tensor_scan(
                                     out=numb[:, :], data0=db, data1=u[:, :],
                                     initial=ini, op0=ALU.mult, op1=ALU.add))
                        if H == 0:
                            dve(lambda v, k=kk: v.tensor_copy(
                                out=stH[:, k:k + 1], in_=numb[:, TH - 1:TH]),
                                inc=False)
                        dnv = dve(lambda v, w=wrb[j], db=d_bc, ini=dini:
                                  v.tensor_tensor_scan(
                                      out=denb[:, :], data0=db, data1=w[:, :],
                                      initial=ini, op0=ALU.mult, op1=ALU.add))
                        rb_free[0][j] = dnv
                        rb_free[1][j] = dnv
                        if H == 0:
                            dve(lambda v, k=kk: v.tensor_copy(
                                out=stH[:, 16 + k:17 + k], in_=denb[:, TH - 1:TH]),
                                inc=False)
                        rec.wt("scalar", dv, dnv)
                        act(lambda s: s.activation(
                            out=tb[:, :], in_=denb[:, :], func=AF.Ln,
                            bias=cons[:, C_EPS + 1:C_EPS + 2]), inc=False)
                        rcv = act(lambda s: s.activation(out=tb[:, :], in_=tb[:, :],
                                                         func=AF.Exp, scale=-1.0))
                        rec.wt("vector", ac, rcv)
                        dve(lambda v: v.tensor_tensor(out=numb[:, :], in0=numb[:, :],
                                                      in1=tb[:, :], op=ALU.mult),
                            inc=False)
                        wtd("vector", "sync", rq_)
                        rv2 = dve(lambda v, s_=srb[j], k=kk, r=rwkv:
                                  v.tensor_tensor(
                                      out=r[:, k * TH:(k + 1) * TH],
                                      in0=numb[:, :], in1=s_[:, :], op=ALU.mult))
                        rb_free[2][j] = rv2
                    rwkv_done_H[H] = dv.n
                for H in range(2):
                    rwkv = rwkvH[H]
                    rec.wt("tensor", dv, rwkv_done_H[H])
                    for g in range(KT):
                        sl, lv = swo.get(H * KT + g)
                        wtd("tensor", "gpsimd", lv)
                        for n in range(2):
                            bi, bk = pool.acquire()
                            for kk in range(KT):
                                mm(bk[:, :], ws_tile(sl, kk),
                                   rwkv[:, kk * TH + n * CW:kk * TH + (n + 1) * CW],
                                   kk == 0, kk == KT - 1, inc=(kk == KT - 1))
                            stage_acc(bk[:, :], pe.n,
                                      xdst_tile(xdst, g, H * 2 + n))
                            pool.release(bi, ac, ac.n)
                        ws_done(sl, pe.n)

        def emit_l0_p4(xlat, xdst):
            p4 = ExitStack()
            with p4:
                p4e = p4.enter_context
                ylat = p4e(nc.sbuf_tensor(_uname("ylat"), [P, 8 * TL], F32))
                h16 = p4e(nc.sbuf_tensor(_uname("h16"), [P, 8 * TL], BF16))
                scwbc = [p4e(nc.sbuf_tensor(_uname(f"scwbc{i}"), [P, TL], F32))
                         for i in range(2)]
                scwrow = [p4e(nc.sbuf_tensor(_uname(f"scwrow{i}"), [1, TL], BF16))
                          for i in range(2)]
                relu2 = p4e(nc.sbuf_tensor(_uname("relu2"), [P, 2 * CW], F32))
                qfence()
                swe = WSweep([(wi[w][ee_, m_], 8 * P) for ee_ in range(32)
                              for w in ("w1T", "w2T") for m_ in range(8)])
                swc = WSweep([(wi["fc2T"][g], 8 * P) for g in range(KT)])
                scwrow_pe = [0, 0]
                scwbc_dv = [0, 0]
                h_free_pe = 0
                y_done = {}
                sq_last = 0
                for ee in range(32):
                    j = ee % 2
                    rec.wt("sync", qX, state["scw_q"])
                    rec.wt("sync", pe, scwrow_pe[j])
                    rq_ = sdma(scwrow[j][0:1, :], scw_d[ee:ee + 1, :])
                    for n in range(NCH):
                        bi, bk = pool.acquire()
                        wtd("tensor", "sync", rq_)
                        v_ = mm(bk[:, :], onesrow16[0:1, :],
                                scwrow[j][0:1, n * CW:(n + 1) * CW], True, True,
                                inc=True)
                        rec.wt("scalar", pe, v_)
                        rec.wt("scalar", dv, scwbc_dv[j])
                        cv_ = act(lambda s, b=bk, jj=j, n_=n: s.copy(
                            out=scwbc[jj][:, n_ * CW:(n_ + 1) * CW], in_=b[:, :]))
                        pool.release(bi, ac, cv_)
                    scwrow_pe[j] = pe.n
                    scw_ready = ac.n
                    for m in range(8):
                        sl, lv = swe.get(ee * 16 + m)
                        wtd("tensor", "gpsimd", lv)
                        if ee == 0 and m == 0:
                            rec.wt("tensor", ac, state["xlat_done"])
                        for n in range(NCH):
                            bi, bk = pool.acquire()
                            for kk in range(8):
                                mm(bk[:, :], ws_tile(sl, kk),
                                   xlat[:, kk * TL + n * CW:kk * TL + (n + 1) * CW],
                                   kk == 0, kk == 7, inc=(kk == 7))
                            rt = relu2[:, (n % 2) * CW:(n % 2 + 1) * CW]
                            rec.wt("vector", pe, pe.n)
                            rec.wt("vector", ac, scw_ready)
                            rec.wt("vector", ac, sq_last)
                            hv = dve(lambda v, b=bk, jj=j, n_=n, r=rt:
                                     v.scalar_tensor_tensor(
                                         out=r, in0=b[:, :], scalar=0.0,
                                         in1=scwbc[jj][:, n_ * CW:(n_ + 1) * CW],
                                         op0=ALU.max, op1=ALU.mult))
                            pool.release(bi, dv, hv)
                            rec.wt("scalar", dv, hv)
                            rec.wt("scalar", pe, h_free_pe)
                            sq_last = act(lambda s, m_=m, n_=n, r=rt: s.activation(
                                out=h16[:, m_ * TL + n_ * CW:m_ * TL + (n_ + 1) * CW],
                                in_=r, func=AF.Square))
                        ws_done(sl, pe.n)
                    h_done = sq_last
                    scwbc_dv[j] = dv.n
                    rec.wt("tensor", ac, h_done)
                    for m2 in range(8):
                        sl, lv = swe.get(ee * 16 + 8 + m2)
                        wtd("tensor", "gpsimd", lv)
                        for n in range(NCH):
                            bi, bk = pool.acquire()
                            for kk in range(8):
                                mm(bk[:, :], ws_tile(sl, kk),
                                   h16[:, kk * TL + n * CW:kk * TL + (n + 1) * CW],
                                   kk == 0, kk == 7, inc=(kk == 7))
                            rec.wt("vector", pe, pe.n)
                            if ee == 0:
                                yv = dve(lambda v, b=bk, m_=m2, n_=n: v.tensor_copy(
                                    out=ylat[:, m_ * TL + n_ * CW:
                                             m_ * TL + (n_ + 1) * CW],
                                    in_=b[:, :]))
                            else:
                                yv = dve(lambda v, b=bk, m_=m2, n_=n: v.tensor_tensor(
                                    out=ylat[:, m_ * TL + n_ * CW:
                                             m_ * TL + (n_ + 1) * CW],
                                    in0=ylat[:, m_ * TL + n_ * CW:
                                             m_ * TL + (n_ + 1) * CW],
                                    in1=b[:, :], op=ALU.add))
                            y_done[(m2, n)] = yv
                            pool.release(bi, dv, yv)
                        ws_done(sl, pe.n)
                    h_free_pe = pe.n
                rec.wt("scalar", dv, dv.n)
                y16 = h16
                y16v = act(lambda s: s.copy(out=y16[:, :], in_=ylat[:, :]))
                rec.wt("tensor", ac, y16v)
                for g in range(KT):
                    sl, lv = swc.get(g)
                    wtd("tensor", "gpsimd", lv)
                    for n in range(NCH):
                        bi, bk = pool.acquire()
                        for kk in range(8):
                            mm(bk[:, :], ws_tile(sl, kk),
                               y16[:, kk * TL + n * CW:kk * TL + (n + 1) * CW],
                               kk == 0, kk == 7, inc=(kk == 7))
                        if dbg_split:
                            stage_acc(bk[:, :], pe.n,
                                      xdst_tile(dbg["dbg_ro"], g, n), accum=False)
                        else:
                            stage_acc(bk[:, :], pe.n, xdst_tile(xdst, g, n))
                        pool.release(bi, ac, ac.n)
                    ws_done(sl, pe.n)


        def emit_l1_routing(logits_sb, logits_done, combT):
            p3r = ExitStack()
            with p3r:
                pl = p3r.enter_context
                tmE = pl(nc.sbuf_tensor(_uname("tmE"), [P, KT * 8], F32))
                tmS = pl(nc.sbuf_tensor(_uname("tmS"), [P, KT * 8], F32))
                tmM = pl(nc.sbuf_tensor(_uname("tmM"), [P, KT * 8], F32))
                redE = pl(nc.sbuf_tensor(_uname("redE"), [P, 2 * KT], F32))
                comb16 = pl(nc.sbuf_tensor(_uname("comb16"), [P, KT * 8], BF16))
                rec.wt("tensor", ac, logits_done)
                prev_copy = 0
                for rnd in range(4):
                    bi_t, bk_t = pool.acquire()
                    rec.wt("tensor", ac, prev_copy)
                    for c in range(4):
                        cg = rnd * 4 + c
                        inc = c == 3
                        rec.em("tensor", lambda t, c_=c, cg_=cg, b=bk_t, i_=inc:
                               (t.transpose(b[:, c_ * 8:(c_ + 1) * 8],
                                            logits_sb[0:8, cg_ * P:(cg_ + 1) * P],
                                            eye32f[0:8, 0:8]).then_inc(pe.h, 1)
                                if i_ else
                                t.transpose(b[:, c_ * 8:(c_ + 1) * 8],
                                            logits_sb[0:8, cg_ * P:(cg_ + 1) * P],
                                            eye32f[0:8, 0:8])),
                               incs=((("pe", 1),) if inc else ()))
                    tv = pe.inc()
                    rec.wt("scalar", pe, tv)
                    prev_copy = act(lambda s, r=rnd, b=bk_t: s.copy(
                        out=tmE[:, r * 32:(r + 1) * 32], in_=b[:, 0:32]))
                    pool.release(bi_t, ac, prev_copy)
                rec.wt("vector", ac, prev_copy)
                tm38 = tmE[:, :].rearrange("p (c e) -> p c e", e=8)
                mE = redE[:, 0:KT].rearrange("p (c o) -> p c o", o=1)
                dve(lambda v: v.tensor_reduce(out=mE, in_=tm38, axis=AXX,
                                              op=ALU.max), inc=False)
                mEb = bass.broadcast_tensor_aps(tm38, mE)[1]
                e38 = tmS[:, :].rearrange("p (c e) -> p c e", e=8)
                sv = dve(lambda v: v.tensor_tensor(out=e38, in0=tm38, in1=mEb,
                                                   op=ALU.subtract))
                rec.wt("scalar", dv, sv)
                ev_ = act(lambda s: s.activation(out=tmS[:, :], in_=tmS[:, :],
                                                 func=AF.Exp))
                rec.wt("vector", ac, ev_)
                dve(lambda v: v.tensor_reduce(out=mE, in_=e38, axis=AXX,
                                              op=ALU.max), inc=False)
                m1b = bass.broadcast_tensor_aps(e38, mE)[1]
                msk = tmM[:, :].rearrange("p (c e) -> p c e", e=8)
                dve(lambda v: v.tensor_tensor(out=msk, in0=e38, in1=m1b,
                                              op=ALU.is_equal), inc=False)
                dve(lambda v: v.tensor_tensor(out=msk, in0=e38, in1=msk,
                                              op=ALU.mult), inc=False)
                dve(lambda v: v.tensor_tensor(out=msk, in0=e38, in1=msk,
                                              op=ALU.subtract), inc=False)
                e2 = redE[:, KT:2 * KT].rearrange("p (c o) -> p c o", o=1)
                dve(lambda v: v.tensor_reduce(out=e2, in_=msk, axis=AXX,
                                              op=ALU.max), inc=False)
                dve(lambda v: v.tensor_tensor(out=redE[:, 0:KT], in0=redE[:, 0:KT],
                                              in1=redE[:, KT:2 * KT], op=ALU.add),
                    inc=False)
                m2b = bass.broadcast_tensor_aps(e38, e2)[1]
                dve(lambda v: v.tensor_tensor(out=msk, in0=e38, in1=m2b,
                                              op=ALU.is_ge), inc=False)
                dve(lambda v: v.tensor_tensor(out=msk, in0=e38, in1=msk,
                                              op=ALU.mult), inc=False)
                dve(lambda v: v.reciprocal(out=redE[:, 0:KT], in_=redE[:, 0:KT]),
                    inc=False)
                dnb = bass.broadcast_tensor_aps(msk, mE)[1]
                dve(lambda v: v.tensor_tensor(out=msk, in0=msk, in1=dnb,
                                              op=ALU.mult), inc=False)
                al3 = cons[:, C_ALPHA:C_ALPHA + 8].rearrange("p (o e) -> p o e", o=1)
                alb = bass.broadcast_tensor_aps(msk, al3)[1]
                cbv = dve(lambda v: v.tensor_tensor(out=msk, in0=msk, in1=alb,
                                                    op=ALU.mult))
                rec.wt("vector", dv, cbv)
                c16v = dve(lambda v: v.tensor_copy(out=comb16[:, :], in_=tmM[:, :]))
                prev_copy = 0
                for rnd in range(4):
                    rec.wt("tensor", dv, c16v)
                    rec.wt("tensor", ac, prev_copy)
                    for c in range(4):
                        cg = rnd * 4 + c
                        src = bass.AP(comb16[:, :].tensor,
                                      comb16[:, :].offset + cg * 8,
                                      [comb16[:, :].ap[0], [1, 8]])
                        if c < 3:
                            rec.em("tensor", lambda t, c_=c, s_=src:
                                   t.transpose(ps16[0:8, c_ * 128:(c_ + 1) * 128],
                                               s_, cons16[:, B_EYE128:B_EYE128 + P]))
                        else:
                            rec.em("tensor", lambda t, c_=c, s_=src:
                                   t.transpose(ps16[0:8, c_ * 128:(c_ + 1) * 128],
                                               s_, cons16[:, B_EYE128:B_EYE128 + P])
                                   .then_inc(pe.h, 1), incs=(("pe", 1),))
                            tv = pe.inc()
                    rec.wt("scalar", pe, tv)
                    prev_copy = act(lambda s, r=rnd: s.copy(
                        out=combT[0:8, r * CW:(r + 1) * CW], in_=ps16[0:8, 0:CW]))
                cT_done = prev_copy
                state["cT_done"] = cT_done
                rec.wt("sync", ac, cT_done)
                state["comb_q"] = sdma(comb_d, combT[0:8, :])

        def emit_l1_p3(logits_sb, logits_done, combT, xdst, consume_carry,
                       xn_last_pe):
            TH = TL // 2
            p3L = ExitStack()
            with p3L:
                pl = p3L.enter_context
                o16H = [pl(nc.sbuf_tensor(_uname(f"o16H{i}"), [P, KT * TH], BF16))
                        for i in range(2)]
                arb = [pl(nc.sbuf_tensor(_uname(f"arb{i}"), [P, TH], BF16))
                       for i in range(2)]
                irb = [pl(nc.sbuf_tensor(_uname(f"irb{i}"), [P, TH], BF16))
                       for i in range(2)]
                grb = pl(nc.sbuf_tensor(_uname("grb"), [P, TH], BF16))
                hbuf = pl(nc.sbuf_tensor(_uname("hbuf"), [P, TH], F32))
                hs16 = pl(nc.sbuf_tensor(_uname("hs16"), [P, TH], BF16))
                rstdbc = [pl(nc.sbuf_tensor(_uname(f"rstdbc{i}"), [P, TH], F32))
                          for i in range(2)]
                tt = pl(nc.sbuf_tensor(_uname("tt"), [P, TH], F32))
                stI = pl(nc.sbuf_tensor(_uname("stI"), [P, KT], F32))
                stO = pl(nc.sbuf_tensor(_uname("stO"), [P, KT], F32))
                qfence()
                carry_v = consume_carry()
                wtd("gpsimd", "sync", state["x1cp_q"])
                swo1 = WSweep([(wi["wo1T"][g], KT * P) for g in range(KT)] * 2)
                o_done_H = [0, 0]
                ssq_pe_prev = 0
                for H in range(2):
                    o16 = o16H[H]
                    ssq = []
                    for n in range(2):
                        ssq.append(pool.acquire(hold=True))
                    rb2 = [[0, 0], [0, 0]]
                    for kk in range(KT):
                        j = kk % 2
                        rec.wt("sync", dv, rb2[0][j])
                        aq_ = sdma(arb[j][:, :],
                                   sp_w[kk * P:(kk + 1) * P, H * TH:(H + 1) * TH])
                        rec.wt("sync", dv, rb2[1][j])
                        iq_ = sdma(irb[j][:, :],
                                   sp_u[kk * P:(kk + 1) * P, H * TH:(H + 1) * TH])
                        wtd("vector", "sync", max(aq_, iq_))
                        rec.wt("vector", dv, carry_v)
                        rec.wt("vector", pe, xn_last_pe)
                        ini = (cons[:, C_CN + kk:C_CN + kk + 1] if H == 0
                               else stI[:, kk:kk + 1])
                        hv = dve(lambda v, a_=arb[j], i_=irb[j], ini_=ini:
                                 v.tensor_tensor_scan(
                                     out=hbuf[:, :], data0=a_[:, :], data1=i_[:, :],
                                     initial=ini_, op0=ALU.mult, op1=ALU.add))
                        if H == 0:
                            dve(lambda v, k=kk: v.tensor_copy(
                                out=stO[:, k:k + 1], in_=hbuf[:, TH - 1:TH]),
                                inc=False)
                        rb2[0][j] = hv
                        rb2[1][j] = hv
                        rec.wt("scalar", dv, hv)
                        rec.wt("scalar", pe, ssq_pe_prev)
                        sqv = act(lambda s: s.activation(out=hs16[:, :],
                                                         in_=hbuf[:, :],
                                                         func=AF.Square))
                        rec.wt("tensor", ac, sqv)
                        for n in range(2):
                            mm(ssq[n][1][0:1, :], cons16[:, B_ONES:B_ONES + 1],
                               hs16[:, n * CW:(n + 1) * CW], kk == 0, kk == KT - 1,
                               inc=(kk == KT - 1 and n == 1))
                    ssq_pe = pe.n
                    ssq_pe_prev = ssq_pe
                    rec.wt("scalar", pe, ssq_pe)
                    for n in range(2):
                        act(lambda s, b=ssq[n][1], n_=n: s.activation(
                            out=rows[0:1, n_ * CW:(n_ + 1) * CW], in_=b[0:1, :],
                            func=AF.Ln, bias=cons[0:1, C_EPS + 3:C_EPS + 4]),
                            inc=False)
                    for n in range(2):
                        act(lambda s, n_=n: s.activation(
                            out=rows[0:1, n_ * CW:(n_ + 1) * CW],
                            in_=rows[0:1, n_ * CW:(n_ + 1) * CW],
                            func=AF.Exp, scale=-0.5), inc=False)
                    rec.em("scalar", lambda s: s.copy(
                        out=rows[0:1, 2 * CW:2 * CW + 1],
                        in_=rows[0:1, 0:1]).then_inc(ac.h, 1),
                        incs=(("ac", 1),))
                    expv = ac.inc()
                    rec.wt("vector", ac, expv)
                    sqrtD = float(np.sqrt(D))
                    rowsv = dve(lambda v: v.tensor_scalar(
                        out=rows[0:1, 0:2 * CW], in0=rows[0:1, 0:2 * CW],
                        scalar1=sqrtD, scalar2=None, op0=ALU.mult))
                    for n in range(2):
                        bi, bk = pool.acquire()
                        rec.wt("tensor", dv, rowsv)
                        bv = mm(bk[:, :], onesrow[0:1, :],
                                rows[0:1, n * CW:(n + 1) * CW], True, True,
                                inc=True)
                        rec.wt("scalar", pe, bv)
                        rbv = act(lambda s, b=bk, n_=n, rb=rstdbc[H]: s.copy(
                            out=rb[:, n_ * CW:(n_ + 1) * CW], in_=b[:, :]))
                        pool.release(bi, ac, rbv)
                    rstd_done = ac.n
                    for n in range(2):
                        pool.release(ssq[n][0], ac, rstd_done)
                    g_free = 0
                    for kk in range(KT):
                        j = kk % 2
                        rec.wt("sync", dv, rb2[0][j])
                        aq_ = sdma(arb[j][:, :],
                                   sp_w[kk * P:(kk + 1) * P, H * TH:(H + 1) * TH])
                        rec.wt("sync", dv, rb2[1][j])
                        iq_ = sdma(irb[j][:, :],
                                   sp_u[kk * P:(kk + 1) * P, H * TH:(H + 1) * TH])
                        rec.wt("sync", dv, g_free)
                        gq_ = sdma(grb[:, :],
                                   sp_r[kk * P:(kk + 1) * P, H * TH:(H + 1) * TH])
                        wtd("vector", "sync", max(aq_, iq_))
                        ini = (cons[:, C_CN + kk:C_CN + kk + 1] if H == 0
                               else stI[:, kk:kk + 1])
                        hv = dve(lambda v, a_=arb[j], i_=irb[j], ini_=ini:
                                 v.tensor_tensor_scan(
                                     out=hbuf[:, :], data0=a_[:, :], data1=i_[:, :],
                                     initial=ini_, op0=ALU.mult, op1=ALU.add))
                        rb2[0][j] = hv
                        rb2[1][j] = hv
                        rec.wt("vector", ac, rstd_done)
                        dve(lambda v, k=kk: v.tensor_scalar(
                            out=tt[:, :], in0=hbuf[:, :],
                            scalar1=cons[:, C_GNW + k:C_GNW + k + 1], scalar2=None,
                            op0=ALU.mult), inc=False)
                        dve(lambda v, rb=rstdbc[H]: v.tensor_tensor(
                            out=tt[:, :], in0=tt[:, :], in1=rb[:, :],
                            op=ALU.mult), inc=False)
                        wtd("vector", "sync", gq_)
                        ov = dve(lambda v, k=kk, o_=o16: v.tensor_tensor(
                            out=o_[:, k * TH:(k + 1) * TH], in0=tt[:, :],
                            in1=grb[:, :], op=ALU.mult))
                        g_free = ov
                    if H == 0:
                        dve(lambda v: v.tensor_copy(out=stI[:, :], in_=stO[:, :]),
                            inc=False)
                    o_done_H[H] = dv.n
                for H in range(2):
                    o16 = o16H[H]
                    rec.wt("tensor", dv, o_done_H[H])
                    for g in range(KT):
                        sl, lv = swo1.get(H * KT + g)
                        wtd("tensor", "gpsimd", lv)
                        for n in range(2):
                            bi, bk = pool.acquire()
                            for kk in range(KT):
                                mm(bk[:, :], ws_tile(sl, kk),
                                   o16[:, kk * TH + n * CW:kk * TH + (n + 1) * CW],
                                   kk == 0, kk == KT - 1, inc=(kk == KT - 1))
                            if dbg_split:
                                stage_acc(bk[:, :], pe.n,
                                          xdst_tile(dbg["dbg_at"], g, H * 2 + n),
                                          accum=False)
                            else:
                                stage_acc(bk[:, :], pe.n,
                                          xdst_tile(xdst, g, H * 2 + n))
                            pool.release(bi, ac, ac.n)
                        ws_done(sl, pe.n)
            state["p3_last_pe"] = pe.n

        def emit_l1_p4(combT, xdst, xn_dump_q):
            p4L = ExitStack()
            with p4L:
                q4 = p4L.enter_context
                xnr = q4(nc.sbuf_tensor(_uname("xnr"), [P, KT * TL], BF16))
                hE = q4(nc.sbuf_tensor(_uname("hE"), [P, KT * TL], BF16))
                cbbc = [q4(nc.sbuf_tensor(_uname("cbbc0"), [P, TL], F32))] * 2
                cbrow = [q4(nc.sbuf_tensor(_uname("cbrow0"), [1, TL], BF16))] * 2
                gel_t = q4(nc.sbuf_tensor(_uname("gel_t"), [P, 2 * CW], F32))
                b2sb = [q4(nc.sbuf_tensor(_uname(f"b2sb{i}"), [8, P], BF16))
                        for i in range(2)]
                qfence()
                swg = WSweep([(wi[w][ee_, m_], KT * P) for ee_ in range(8)
                              for w in ("g1fc1", "g1fc2") for m_ in range(KT)])
                rec.wt("sync", qX, xn_dump_q)
                xnr_src = bass.AP(sp_xn.tensor, 0, [[TL, P], [P * TL, KT], [1, TL]])
                xq = sdma(xnr[:, :], xnr_src)
                cbrow_pe = 0
                cbbc_dv = 0
                hE_free_pe = 0
                hlast = 0
                for ee in range(8):
                    j = ee % 2
                    rec.wt("sync", qX, state["comb_q"])
                    rec.wt("sync", pe, cbrow_pe)
                    rq_ = sdma(cbrow[j][0:1, :], comb_d[ee:ee + 1, :])
                    for n in range(NCH):
                        bi, bk = pool.acquire()
                        wtd("tensor", "sync", rq_)
                        v_ = mm(bk[:, :], onesrow16[0:1, :],
                                cbrow[j][0:1, n * CW:(n + 1) * CW], True, True,
                                inc=True)
                        rec.wt("scalar", pe, v_)
                        rec.wt("scalar", dv, cbbc_dv)
                        cv_ = act(lambda s, b=bk, jj=j, n_=n: s.copy(
                            out=cbbc[jj][:, n_ * CW:(n_ + 1) * CW], in_=b[:, :]))
                        pool.release(bi, ac, cv_)
                    cbrow_pe = pe.n
                    cb_ready = ac.n
                    for m in range(KT):
                        sl, lv = swg.get(ee * 32 + m)
                        wtd("tensor", "gpsimd", lv)
                        if ee == 0 and m == 0:
                            wtd("tensor", "sync", xq)
                        for n in range(NCH):
                            bi, bk = pool.acquire()
                            for kk in range(KT):
                                mm(bk[:, :], ws_tile(sl, kk),
                                   xnr[:, kk * TL + n * CW:kk * TL + (n + 1) * CW],
                                   kk == 0, kk == KT - 1, inc=(kk == KT - 1))
                            gt = gel_t[:, (n % 2) * CW:(n % 2 + 1) * CW]
                            rec.wt("scalar", pe, pe.n)
                            rec.wt("scalar", dv, hlast)
                            wtd("scalar", "sync", cst_q)
                            gv_ = act(lambda s, b=bk, m_=m, e_=ee, g_=gt:
                                      s.activation(out=g_, in_=b[:, :], func=AF.Gelu,
                                                   bias=b1sb[:, m_ * 8 + e_:
                                                             m_ * 8 + e_ + 1]))
                            pool.release(bi, ac, gv_)
                            rec.wt("vector", ac, gv_)
                            rec.wt("vector", ac, cb_ready)
                            rec.wt("vector", pe, hE_free_pe)
                            hlast = dve(lambda v, jj=j, m_=m, n_=n, g_=gt:
                                        v.tensor_tensor(
                                            out=hE[:, m_ * TL + n_ * CW:
                                                   m_ * TL + (n_ + 1) * CW],
                                            in0=g_,
                                            in1=cbbc[jj][:, n_ * CW:(n_ + 1) * CW],
                                            op=ALU.mult))
                        ws_done(sl, pe.n)
                    hE_done = hlast
                    cbbc_dv = hlast
                    rec.wt("tensor", dv, hE_done)
                    for m2 in range(KT):
                        sl, lv = swg.get(ee * 32 + KT + m2)
                        wtd("tensor", "gpsimd", lv)
                        for n in range(NCH):
                            bi, bk = pool.acquire()
                            for kk in range(KT):
                                mm(bk[:, :], ws_tile(sl, kk),
                                   hE[:, kk * TL + n * CW:kk * TL + (n + 1) * CW],
                                   kk == 0, kk == KT - 1, inc=(kk == KT - 1))
                            if dbg_split:
                                stage_acc(bk[:, :], pe.n,
                                          xdst_tile(dbg["dbg_ro"], m2, n),
                                          accum=(ee > 0))
                            else:
                                stage_acc(bk[:, :], pe.n, xdst_tile(xdst, m2, n))
                            pool.release(bi, ac, ac.n)
                        ws_done(sl, pe.n)
                    hE_free_pe = pe.n
                b2_pe = [0, 0]
                for m2 in range(KT):
                    jb = m2 % 2
                    rec.wt("gpsimd", pe, b2_pe[jb])
                    bq_ = gdma(b2sb[jb][0:8, :],
                               wi["b2raw"][0:8, m2 * P:(m2 + 1) * P])
                    wtd("tensor", "gpsimd", bq_)
                    rec.wt("tensor", ac, state["cT_done"])
                    for n in range(NCH):
                        bi, bk = pool.acquire()
                        bv = mm(bk[:, :], b2sb[jb][0:8, :],
                                combT[0:8, n * CW:(n + 1) * CW], True, True,
                                inc=True)
                        if dbg_split:
                            stage_acc(bk[:, :], bv, xdst_tile(dbg["dbg_ro"], m2, n))
                        else:
                            stage_acc(bk[:, :], bv, xdst_tile(xdst, m2, n))
                        pool.release(bi, ac, ac.n)
                    b2_pe[jb] = pe.n
                    if not dbg_split:
                        rowap_o = bass.AP(out.tensor, m2 * P * TL, [[TL, P], [1, TL]])
                        rowap_x = bass.AP(xdst.tensor, m2 * P * TL, [[TL, P], [1, TL]])
                        wtd("sync", "gpsimd", qW.n)
                        sdma(rowap_o, rowap_x)

        state = {}
        emit_layer(0)
        if n_layers > 1:
            emit_layer(1)
            rec.wt("gpsimd", qW, qW.n)
            rec.wt("sync", qX, qX.n)
        else:
            rec.wt("gpsimd", qW, qW.n)
            gdma(out, xnext0)
            rec.wt("gpsimd", qW, qW.n)

        assert rec.check(), "semaphore schedule deadlock"

        with nc.Block() as block:
            @block.sync
            def _(sync):
                for fn in rec.ops["sync"]:
                    fn(sync)

            @block.gpsimd
            def _(g):
                for fn in rec.ops["gpsimd"]:
                    fn(g)

            @block.tensor
            def _(t):
                for fn in rec.ops["tensor"]:
                    fn(t)

            @block.vector
            def _(v):
                for fn in rec.ops["vector"]:
                    fn(v)

            @block.scalar
            def _(s):
                for fn in rec.ops["scalar"]:
                    fn(s)

    return nc


def host_pack(inputs, n_layers=2):
    f32 = np.float32
    g = {}
    g["wkT"] = pack_lhsT(np.asarray(inputs["L0_attn_Wk"]).T)
    g["wvT"] = pack_lhsT(np.asarray(inputs["L0_attn_Wv"]).T)
    g["wrT"] = pack_lhsT(np.asarray(inputs["L0_attn_Wr"]).T)
    g["woT"] = pack_lhsT(np.asarray(inputs["L0_attn_Wo"]).T)
    g["fc1T"] = pack_lhsT(np.asarray(inputs["L0_ffn_fc1_w"]).T)
    g["fc2T"] = pack_lhsT(np.asarray(inputs["L0_ffn_fc2_w"]).T)
    g["w1T"] = np.stack([pack_lhsT(np.asarray(w).T) for w in inputs["L0_ffn_w1"]])
    g["w2T"] = np.stack([pack_lhsT(np.asarray(w).T) for w in inputs["L0_ffn_w2"]])
    upT = pack_lhsT(np.asarray(inputs["L0_ffn_shared_up_w"]).T)
    g["upT"] = np.ascontiguousarray(upT.reshape(4, 8, P, KT * P))
    g["dnT"] = pack_lhsT(np.asarray(inputs["L0_ffn_shared_down_w"]).T)
    g["gate0T"] = pack_lhsT(np.asarray(inputs["L0_ffn_gate_w"]).T, msz=32)
    g["td_T"] = np.ascontiguousarray(
        np.asarray(inputs["L0_attn_time_decay"], dtype=f32).reshape(KT, P).T)
    g["tf_T"] = np.ascontiguousarray(
        np.asarray(inputs["L0_attn_time_first"], dtype=f32).reshape(KT, P).T)
    gb = np.asarray(inputs["L0_ffn_gate_bias"], dtype=f32)
    g["gb_col"] = np.ascontiguousarray(gb.reshape(32, 1))
    g["gb_rep"] = np.ascontiguousarray(np.tile(gb.reshape(1, 32), (P, 1)))
    if n_layers > 1:
        g["wiT"] = pack_lhsT(np.asarray(inputs["L1_attn_Wi"]))
        g["wfT"] = pack_lhsT(np.asarray(inputs["L1_attn_Wf"]))
        g["wgT"] = pack_lhsT(np.asarray(inputs["L1_attn_Wg"]))
        g["wo1T"] = pack_lhsT(np.asarray(inputs["L1_attn_Wo"]))
        g["g1fc1"] = np.stack([pack_lhsT(np.asarray(w))
                               for w in inputs["L1_ffn_fc1_w"]])
        g["g1fc2"] = np.stack([pack_lhsT(np.asarray(w))
                               for w in inputs["L1_ffn_fc2_w"]])
        g["gate1T"] = pack_lhsT(np.asarray(inputs["L1_ffn_gate_w"]), msz=8)
        b1 = np.asarray(inputs["L1_ffn_fc1_b"], dtype=f32)
        g["b1P"] = np.ascontiguousarray(
            b1.T.reshape(KT, P, 8).transpose(1, 0, 2).reshape(P, KT * 8))
        g["b2raw"] = np.ascontiguousarray(np.asarray(inputs["L1_ffn_fc2_b"],
                                                     dtype=f32))
        g["gnw"] = np.ascontiguousarray(
            np.asarray(inputs["L1_attn_g_norm_weight"], dtype=f32).reshape(KT, P).T)
        g["alpha_rep"] = np.ascontiguousarray(
            np.tile(np.asarray(inputs["L1_ffn_alpha"], dtype=f32).reshape(1, 8),
                    (P, 1)))
    g["ones_col"] = np.ones((P, 1), f32)
    g["ones_row"] = np.ones((1, P), f32)
    g["eye128"] = np.eye(P, dtype=f32)
    g["eye32"] = np.eye(32, dtype=f32)
    g["epsv"] = np.tile(np.array([[1.0e-5, 1.0e-6, 1.0e-30, 1.0e-5 * D]], f32), (P, 1))
    return g


def make_in_maps(inputs, n_layers=2):
    shared = host_pack(inputs, n_layers=n_layers)
    x = np.asarray(inputs["L0_attn_x"], dtype=np.float32)
    in_maps = []
    for c in range(8):
        b, h = c // 2, c % 2
        m = dict(shared)
        m["xres0"] = np.ascontiguousarray(
            x[b, h * TL:(h + 1) * TL, :].T)
        sel = np.zeros((P, 8), np.float32)
        if h == 1:
            sel[:, 2 * b] = 1.0
        m["sel_rep"] = sel
        m["fmask"] = np.full((P, 1), 1.0 if h == 0 else 0.0, np.float32)
        in_maps.append(m)
    return in_maps


def kernel(**inputs):
    n_layers = 2
    nc = build_program(n_layers=n_layers)
    in_maps = make_in_maps(inputs, n_layers=n_layers)
    res = run_bass_kernel_spmd(nc, in_maps, core_ids=list(range(8)))
    out = np.empty((4, 4096, D), np.float32)
    for c in range(8):
        b, h = c // 2, c % 2
        out[b, h * TL:(h + 1) * TL, :] = res.results[c]["out"].T
    return out


# version 11
# speedup vs baseline: 1.0173x; 1.0029x over previous
import numpy as np
from contextlib import ExitStack

import concourse.bass as bass
import concourse.mybir as mybir
from concourse.bass_utils import run_bass_kernel_spmd

F32 = mybir.dt.float32
BF16 = mybir.dt.bfloat16
AF = mybir.ActivationFunctionType
ALU = mybir.AluOpType
AXX = mybir.AxisListType.X

P = 128
D = 2048
KT = D // P
TL = 2048
NCH = 4
CW = 512
NC0 = 8
CW0 = 256
BIG = 1.0e30

ENGINES = ("sync", "gpsimd", "tensor", "vector", "scalar")


class Sem:
    def __init__(self, handle, name):
        self.h, self.name, self.n = handle, name, 0

    def inc(self, k=1):
        self.n += k
        return self.n


class Rec:
    def __init__(self):
        self.ops = {e: [] for e in ENGINES}
        self.meta = {e: [] for e in ENGINES}

    def em(self, eng, fn, incs=(), tag=""):
        self.ops[eng].append(fn)
        self.meta[eng].append(("op", tuple(incs), tag))

    def wt(self, eng, sem, val):
        if val and val > 0:
            self.ops[eng].append(lambda e, s=sem.h, v=val: e.wait_ge(s, v))
            self.meta[eng].append(("wait", (sem.name, val), ""))

    def check(self):
        heads = {e: 0 for e in ENGINES}
        sems = {}
        import collections
        progress = True
        while progress:
            progress = False
            for e in ENGINES:
                while heads[e] < len(self.meta[e]):
                    kind, data, tag = self.meta[e][heads[e]]
                    if kind == "wait":
                        name, val = data
                        if sems.get(name, 0) < val:
                            break
                    else:
                        for (name, amt) in data:
                            sems[name] = sems.get(name, 0) + amt
                    heads[e] += 1
                    progress = True
        stuck = {e: heads[e] for e in ENGINES if heads[e] < len(self.meta[e])}
        if stuck:
            print("DEADLOCK. engine heads:")
            for e, h in stuck.items():
                kind, data, tag = self.meta[e][h]
                print(f"  {e}@{h}/{len(self.meta[e])}: waiting {data}"
                      f" (sem now {sems.get(data[0], 0)}) {tag}")
            return False
        print("schedule check OK; final sems:", sems)
        return True


class Pool:

    def __init__(self, rec, tensors):
        self.rec, self.t = rec, tensors
        self.nb = len(tensors)
        self.free_at = [None] * self.nb
        self.held = [False] * self.nb
        self.i = 0

    def acquire(self, hold=False):
        for _ in range(self.nb):
            idx = self.i % self.nb
            self.i += 1
            if not self.held[idx]:
                break
        else:
            raise RuntimeError("no free psum bank")
        f = self.free_at[idx]
        if f is not None:
            self.rec.wt("tensor", f[0], f[1])
            self.free_at[idx] = None
        if hold:
            self.held[idx] = True
        return idx, self.t[idx]

    def release(self, idx, sem, val):
        self.held[idx] = False
        self.free_at[idx] = (sem, val)


def pack_lhsT(WT, msz=128):
    K, M = WT.shape
    kt = K // P
    G = M // msz
    a = np.asarray(WT, dtype=np.float32).reshape(kt, P, G, msz).transpose(2, 1, 0, 3)
    return np.ascontiguousarray(a.reshape(G, P, kt * msz))


def build_program(n_layers=2, taps=(), dbg_split=False):
    nc = bass.Bass("TRN2", target_bir_lowering=False, debug=False,
                   detect_race_conditions=False)
    dt = nc.dram_tensor

    xres0 = dt("xres0", [D, TL], F32, kind="ExternalInput").ap()
    out = dt("out", [D, TL], F32, kind="ExternalOutput").ap()

    wi = {}
    def win(name, shape):
        wi[name] = dt(name, list(shape), F32, kind="ExternalInput").ap()
        return wi[name]

    win("wkT", (KT, P, KT * P)); win("wvT", (KT, P, KT * P))
    win("wrT", (KT, P, KT * P)); win("woT", (KT, P, KT * P))
    win("fc1T", (8, P, KT * P)); win("fc2T", (KT, P, 8 * P))
    win("w1T", (32, 8, P, 8 * P)); win("w2T", (32, 8, P, 8 * P))
    win("upT", (4, 8, P, KT * P)); win("dnT", (KT, P, 32 * P))
    win("gate0T", (1, P, KT * 32))
    win("td_T", (P, KT)); win("tf_T", (P, KT))
    win("gb_col", (32, 1)); win("gb_rep", (P, 32))
    if n_layers > 1:
        win("wiT", (KT, P, KT * P)); win("wfT", (KT, P, KT * P))
        win("wgT", (KT, P, KT * P)); win("wo1T", (KT, P, KT * P))
        win("g1fc1", (8, KT, P, KT * P)); win("g1fc2", (8, KT, P, KT * P))
        win("gate1T", (1, P, KT * 8))
        win("b1P", (P, KT * 8)); win("b2raw", (8, D))
        win("gnw", (P, KT)); win("alpha_rep", (P, 8))
    win("ones_col", (P, 1)); win("ones_row", (1, P))
    win("eye128", (P, P)); win("eye32", (32, 32))
    win("sel_rep", (P, 8)); win("fmask", (P, 1)); win("epsv", (P, 4))

    xnext0 = dt("xnext0", [D, TL], F32, kind="Internal").ap()
    xnext1 = dt("xnext1", [D, TL], F32, kind="Internal").ap() if n_layers > 1 else None
    sp_w = dt("sp_w", [D, TL], BF16, kind="Internal").ap()
    sp_u = dt("sp_u", [D, TL], BF16, kind="Internal").ap()
    sp_r = dt("sp_r", [D, TL], BF16, kind="Internal").ap()
    sp_xn = dt("sp_xn", [D, TL], BF16, kind="Internal").ap() if n_layers > 1 else None
    scw_d = dt("scw_d", [32, TL], BF16, kind="Internal").ap()
    comb_d = dt("comb_d", [8, TL], BF16, kind="Internal").ap() if n_layers > 1 else None
    dbg = {}
    if dbg_split:
        for nm in ("dbg_at", "dbg_sh", "dbg_ro"):
            dbg[nm] = dt(nm, [D, TL], F32, kind="Internal").ap()
    cc_in = [dt(f"cc_in{l}", [P, 32], F32, kind="Internal").ap()
             for l in range(n_layers)]
    cc_out = [dt(f"cc_out{l}", [8 * P, 32], F32, kind="Internal").ap()
              for l in range(n_layers)]

    tap_out = {}
    for t in taps:
        tap_out[t] = dt("tap_" + t, [D, TL], F32, kind="ExternalOutput").ap()

    _names = {}

    def _uname(n):
        k = _names.get(n, 0)
        _names[n] = k + 1
        return n if k == 0 else f"{n}_{k}"

    rec = Rec()
    ctx = ExitStack()
    with ctx:
        e = ctx.enter_context
        qW = Sem(e(nc.semaphore("qW")), "qW")
        qX = Sem(e(nc.semaphore("qX")), "qX")
        pe = Sem(e(nc.semaphore("pe")), "pe")
        dv = Sem(e(nc.semaphore("dv")), "dv")
        ac = Sem(e(nc.semaphore("ac")), "ac")
        ccs = [Sem(e(nc.semaphore(f"cc{l}")), f"cc{l}") for l in range(n_layers)]
        sX = Sem(e(nc.semaphore("sX")), "sX")
        sW = Sem(e(nc.semaphore("sW")), "sW")

        psum = [e(nc.psum_tensor(f"ps{i}", [P, CW], F32)) for i in range(7)]
        ps16 = e(nc.psum_tensor("ps16", [P, CW], BF16))
        pool = Pool(rec, psum)

        WSLOT = 32 * P
        wring = e(nc.sbuf_tensor(_uname("wring"), [P, 3 * WSLOT], BF16))
        stage = e(nc.sbuf_tensor(_uname("stage"), [P, 4 * CW], F32))
        cons = e(nc.sbuf_tensor(_uname("cons"), [P, 256], F32))
        C_ONES, C_DEC, C_FF, C_TF, C_TD = 0, 1, 17, 33, 49
        C_GNW, C_SEL, C_CN, C_FIN, C_ALPHA, C_FM = 65, 81, 96, 128, 160, 255
        C_EPS = 168
        cons16 = e(nc.sbuf_tensor(_uname("cons16"), [P, 224], BF16))
        B_ONES, B_EYE128, B_EYE32 = 0, 64, 192
        onesrow = e(nc.sbuf_tensor(_uname("onesrow"), [1, P], F32))
        onesrow16 = e(nc.sbuf_tensor(_uname("onesrow16"), [1, P], BF16))
        gbcol = e(nc.sbuf_tensor(_uname("gbcol"), [32, 1], F32))
        gbrep = e(nc.sbuf_tensor(_uname("gbrep"), [P, 32], F32))
        g0sb = e(nc.sbuf_tensor(_uname("g0sb"), [P, KT * 32], BF16))
        rows = e(nc.sbuf_tensor(_uname("rows"), [1, NCH * CW], F32))
        abc = e(nc.sbuf_tensor(_uname("abc"), [P, 2 * CW0], F32))
        bbc = e(nc.sbuf_tensor(_uname("bbc"), [P, 2 * CW0], F32))
        agsb = e(nc.sbuf_tensor(_uname("agsb"), [P, 8 * 32], F32))
        eye32f = e(nc.sbuf_tensor(_uname("eye32f"), [32, 32], F32))
        if n_layers > 1:
            g1sb = e(nc.sbuf_tensor(_uname("g1sb"), [P, KT * 8], BF16))
            b1sb = e(nc.sbuf_tensor(_uname("b1sb"), [P, KT * 8], F32))

        ws_state = {"i": 0, "free": [0, 0, 0]}

        def ws_load(dram_ap, n_elems):
            s = ws_state["i"] % 3
            ws_state["i"] += 1
            rec.wt("gpsimd", pe, ws_state["free"][s])
            dst = wring[:, s * WSLOT: s * WSLOT + n_elems]
            rec.em("gpsimd", lambda g, d=dst, a=dram_ap:
                   g.dma_start(out=d, in_=a).then_inc(qW.h, 16),
                   incs=(("qW", 16),))
            return s, qW.inc(16)

        def ws_tile(slot, kk, msz=128):
            off = slot * WSLOT + kk * msz
            return wring[:, off: off + msz]

        def ws_done(slot, pe_val):
            ws_state["free"][slot] = pe_val

        class WSweep:

            def __init__(self, groups):
                self.groups = groups
                self.h = []
                self.i = 0

            def get(self, g):
                while self.i <= min(g + 2, len(self.groups) - 1):
                    ap_, ne_ = self.groups[self.i]
                    self.h.append(ws_load(ap_, ne_))
                    self.i += 1
                return self.h[g]

        def mm(bank, lhsT, rhs, start, stop, inc=False):
            if inc:
                rec.em("tensor", lambda t, b=bank, l=lhsT, r=rhs, s=start, p=stop:
                       t.matmul(b, l, r, start=s, stop=p).then_inc(pe.h, 1),
                       incs=(("pe", 1),))
                return pe.inc()
            rec.em("tensor", lambda t, b=bank, l=lhsT, r=rhs, s=start, p=stop:
                   t.matmul(b, l, r, start=s, stop=p))
            return None

        def dve(fn, inc=True):
            rec.wt("vector", dv, dv.n)
            rec.em("vector", lambda v, f=fn: f(v).then_inc(dv.h, 1),
                   incs=(("dv", 1),))
            return dv.inc()

        def act(fn, inc=True):
            rec.wt("scalar", ac, ac.n)
            rec.em("scalar", lambda s, f=fn: f(s).then_inc(ac.h, 1),
                   incs=(("ac", 1),))
            return ac.inc()

        def sdma(out_ap, in_ap, **kw):
            rec.em("sync", lambda s, o=out_ap, i=in_ap, k=kw:
                   s.dma_start(out=o, in_=i, **k).then_inc(qX.h, 16),
                   incs=(("qX", 16),))
            return qX.inc(16)

        def gdma(out_ap, in_ap, **kw):
            rec.em("gpsimd", lambda g, o=out_ap, i=in_ap, k=kw:
                   g.dma_start(out=o, in_=i, **k).then_inc(qW.h, 16),
                   incs=(("qW", 16),))
            return qW.inc(16)

        _sig = {"sync": (qX, sX, []), "gpsimd": (qW, sW, [])}

        def dsig(queue, upto):
            qsem, ssem, lst = _sig[queue]
            for cov, val in lst:
                if cov >= upto:
                    return ssem, val
            rec.wt(queue, qsem, upto)
            rec.em(queue, lambda e, h=ssem.h: e.nop().then_inc(h, 1),
                   incs=((ssem.name, 1),))
            v = ssem.inc()
            lst.append((upto, v))
            return ssem, v

        def wtd(eng, queue, upto):
            if upto and upto > 0:
                s_, v_ = dsig(queue, upto)
                rec.wt(eng, s_, v_)

        def wt_any(eng, s_, v_):
            if s_ == "dma":
                wtd(eng, "gpsimd", v_)
            elif s_ == "dmas":
                wtd(eng, "sync", v_)
            else:
                rec.wt(eng, s_, v_)

        stage_free = [0, 0, 0, 0]
        stage_i = [0]

        def stage_acc(ps_ap, pe_val, dst_dram_ap, accum=True):
            s = stage_i[0] % 4
            stage_i[0] += 1
            wtd("scalar", "gpsimd", stage_free[s])
            rec.wt("scalar", pe, pe_val)
            slot = stage[:, s * CW:(s + 1) * CW]
            v = act(lambda sc, o=slot, p=ps_ap: sc.copy(out=o, in_=p))
            rec.wt("gpsimd", ac, v)
            kw = dict(accum_op=ALU.add) if accum else {}
            q = gdma(dst_dram_ap, slot, **kw)
            stage_free[s] = q
            return v

        def xdst_tile(xdst, g, n, w=CW):
            return bass.AP(xdst.tensor, g * P * TL + n * w, [[TL, P], [1, w]])

        def qfence():
            rec.wt("gpsimd", qW, qW.n)
            rec.wt("sync", pe, pe.n)
            rec.wt("sync", dv, dv.n)
            rec.wt("sync", ac, ac.n)
            rec.wt("gpsimd", pe, pe.n)
            rec.wt("gpsimd", dv, dv.n)
            rec.wt("gpsimd", ac, ac.n)

        sdma(cons[:, C_ONES:C_ONES + 1], wi["ones_col"])
        sdma(cons[:, C_TD:C_TD + KT], wi["td_T"])
        sdma(cons[:, C_TF:C_TF + KT], wi["tf_T"])
        sdma(cons[:, C_SEL:C_SEL + 8], wi["sel_rep"])
        sdma(onesrow[0:1, :], wi["ones_row"])
        sdma(gbcol[0:32, :], wi["gb_col"])
        sdma(gbrep[:, :], wi["gb_rep"])
        sdma(cons[:, C_FM:C_FM + 1], wi["fmask"])
        sdma(cons[:, C_EPS:C_EPS + 4], wi["epsv"])
        sdma(eye32f[0:32, :], wi["eye32"])
        gdma(cons16[:, B_ONES:B_ONES + 1], wi["ones_col"])
        gdma(cons16[:, B_EYE128:B_EYE128 + P], wi["eye128"])
        gdma(cons16[0:32, B_EYE32:B_EYE32 + 32], wi["eye32"])
        gdma(onesrow16[0:1, :], wi["ones_row"])
        g0_q = gdma(g0sb[:, :], wi["gate0T"][0])
        if n_layers > 1:
            sdma(cons[:, C_GNW:C_GNW + KT], wi["gnw"])
            sdma(cons[:, C_ALPHA:C_ALPHA + 8], wi["alpha_rep"])
            sdma(b1sb[:, :], wi["b1P"])
            g1_q = gdma(g1sb[:, :], wi["gate1T"][0])
        cst_q, cst_w = qX.n, qW.n
        wtd("scalar", "sync", cst_q)
        act(lambda s: s.activation(out=cons[:, C_DEC:C_DEC + KT],
                                   in_=cons[:, C_TD:C_TD + KT], func=AF.Exp),
            inc=False)
        dec_ready = act(lambda s: s.activation(out=cons[:, C_DEC:C_DEC + KT],
                                               in_=cons[:, C_DEC:C_DEC + KT],
                                               func=AF.Exp, scale=-1.0))
        wtd("vector", "sync", cst_q)
        ffm = dve(lambda v: v.tensor_scalar(out=cons[:, C_FF:C_FF + KT],
                                            in0=cons[:, C_TF:C_TF + KT],
                                            scalar1=cons[:, C_FM:C_FM + 1],
                                            scalar2=None, op0=ALU.mult))
        rec.wt("scalar", dv, ffm)
        ff_ready = act(lambda s: s.activation(out=cons[:, C_FF:C_FF + KT],
                                              in_=cons[:, C_FF:C_FF + KT],
                                              func=AF.Exp))
        gdma(xnext0, xres0)

        finals = cons[:, C_FIN:C_FIN + 32]
        carry = cons[:, C_CN:C_CN + 32]

        def proj_sweep(wgroups, kt, rhs_fn, evac_fn, m_list=None, ready=None):
            for g in (m_list if m_list is not None else range(len_w(wgroups))):
                sl, lv = ws_load(wgroups(g) if callable(wgroups) else wgroups[g],
                                 kt * P)
                wtd("tensor", "gpsimd", lv)
                if ready is not None:
                    rec.wt("tensor", ready[0], ready[1])
                for n in range(NCH):
                    bi, bk = pool.acquire()
                    for kk in range(kt):
                        mm(bk[:, :], ws_tile(sl, kk), rhs_fn(kk, n),
                           kk == 0, kk == kt - 1, inc=(kk == kt - 1))
                    sem, val = evac_fn(g, n, bk, pe.n)
                    pool.release(bi, sem, val)
                ws_done(sl, pe.n)

        def len_w(x):
            raise RuntimeError

        def emit_layer(l):
            xsrc = xres0 if l == 0 else xnext0
            xdst = xnext0 if l == 0 else xnext1
            cin, cout, cc = cc_in[l], cc_out[l], ccs[l]

            lay = ExitStack()
            with lay:
                le = lay.enter_context
                if l == 0:
                    xlat = le(nc.sbuf_tensor(_uname("xlat"), [P, 8 * TL], BF16))
                else:
                    logits_sb = le(nc.sbuf_tensor(_uname("logits1"), [8, TL], F32))
                    combT = le(nc.sbuf_tensor(_uname("combT"), [8, TL], BF16))

                xn_ctx = ExitStack()
                with xn_ctx:
                    xn = xn_ctx.enter_context(
                        nc.sbuf_tensor(_uname("xn"), [P, KT * TL], BF16))

                    def xn_t(kk, n, w=CW):
                        return xn[:, kk * TL + n * w: kk * TL + n * w + w]

                    p0 = ExitStack()
                    with p0:
                        pe0 = p0.enter_context
                        xch = [pe0(nc.sbuf_tensor(_uname(f"xch{i}"),
                                                  [P, KT * CW0], F32))
                               for i in range(2)]
                        xb16 = pe0(nc.sbuf_tensor(_uname("xb16"), [P, KT * CW0],
                                                  BF16))
                        xs16 = pe0(nc.sbuf_tensor(_uname("xs16"), [P, KT * CW0],
                                                  BF16))
                        if l > 0:
                            qfence()
                            wtd("sync", "gpsimd", qW.n)
                        xn_wr = [0, 0]
                        sq_prev = 0
                        xb_prev = 0
                        stats_prev = 0
                        for n in range(NC0):
                            j = n % 2
                            xc = xch[j]
                            src = bass.AP(xsrc.tensor, n * CW0,
                                          [[TL, P], [P * TL, KT], [1, CW0]])
                            rec.wt("sync", dv, xn_wr[j])
                            rec.wt("sync", ac, sq_prev)
                            lq = sdma(xc[:, :], src)
                            wtd("scalar", "sync", lq)
                            rec.wt("scalar", pe, stats_prev)
                            sq_prev = act(lambda s, x=xc: s.activation(
                                out=xs16[:, :], in_=x[:, :], func=AF.Square))
                            xb_prev = act(lambda s, x=xc: s.copy(out=xb16[:, :],
                                                                 in_=x[:, :]))
                            i1, b1 = pool.acquire()
                            i2, b2 = pool.acquire()
                            rec.wt("tensor", ac, xb_prev)
                            wtd("tensor", "gpsimd", cst_w)
                            for kk in range(KT):
                                mm(b1[0:1, 0:CW0], cons16[:, B_ONES:B_ONES + 1],
                                   xb16[:, kk * CW0:(kk + 1) * CW0],
                                   kk == 0, kk == KT - 1)
                            for kk in range(KT):
                                mm(b2[0:1, 0:CW0], cons16[:, B_ONES:B_ONES + 1],
                                   xs16[:, kk * CW0:(kk + 1) * CW0],
                                   kk == 0, kk == KT - 1,
                                   inc=(kk == KT - 1))
                            stats_prev = pe.n
                            mrow = rows[0:1, 0:CW0]
                            vrow = rows[0:1, CW0:2 * CW0]
                            rrow = rows[0:1, 2 * CW0:3 * CW0]
                            brow = rows[0:1, 3 * CW0:4 * CW0]
                            rec.wt("vector", pe, stats_prev)
                            dve(lambda v, b=b1: v.tensor_scalar(
                                out=mrow, in0=b[0:1, 0:CW0], scalar1=1.0 / D,
                                scalar2=None, op0=ALU.mult), inc=False)
                            dve(lambda v, b=b2: v.tensor_scalar(
                                out=vrow, in0=b[0:1, 0:CW0], scalar1=1.0 / D,
                                scalar2=None, op0=ALU.mult), inc=False)
                            dve(lambda v: v.tensor_tensor(out=brow, in0=mrow,
                                                          in1=mrow,
                                                          op=ALU.mult), inc=False)
                            vv = dve(lambda v: v.tensor_tensor(out=vrow, in0=vrow,
                                                               in1=brow,
                                                               op=ALU.subtract))
                            pool.release(i1, dv, vv)
                            pool.release(i2, dv, vv)
                            rec.wt("scalar", dv, vv)
                            act(lambda s: s.activation(
                                out=rrow, in_=vrow, func=AF.Ln,
                                bias=cons[0:1, C_EPS:C_EPS + 1]), inc=False)
                            rw = act(lambda s: s.activation(out=rrow, in_=rrow,
                                                            func=AF.Exp, scale=-0.5))
                            rec.wt("vector", ac, rw)
                            bw = dve(lambda v: v.tensor_tensor(out=brow, in0=mrow,
                                                               in1=rrow,
                                                               op=ALU.mult))
                            i3, b3 = pool.acquire()
                            i4, b4 = pool.acquire()
                            rec.wt("tensor", ac, rw)
                            av = mm(b3[:, 0:CW0], onesrow[0:1, :], rrow, True, True,
                                    inc=True)
                            rec.wt("tensor", dv, bw)
                            bv = mm(b4[:, 0:CW0], onesrow[0:1, :], brow, True, True,
                                    inc=True)
                            abj = abc[:, (j) * CW0:(j + 1) * CW0]
                            bbj = bbc[:, (j) * CW0:(j + 1) * CW0]
                            rec.wt("scalar", pe, av)
                            rec.wt("scalar", dv, xn_wr[j])
                            a_c = act(lambda s, b=b3, o=abj: s.copy(
                                out=o, in_=b[:, 0:CW0]))
                            rec.wt("scalar", pe, bv)
                            b_c = act(lambda s, b=b4, o=bbj: s.copy(
                                out=o, in_=b[:, 0:CW0]))
                            pool.release(i3, ac, a_c)
                            pool.release(i4, ac, b_c)
                            rec.wt("vector", ac, b_c)
                            for kk in range(KT):
                                dve(lambda v, k=kk, x=xc, o=bbj: v.tensor_tensor(
                                    out=x[:, k * CW0:(k + 1) * CW0],
                                    in0=x[:, k * CW0:(k + 1) * CW0],
                                    in1=o, op=ALU.subtract), inc=False)
                                xn_wr[j] = dve(lambda v, k=kk, nn=n, x=xc, o=abj:
                                               v.tensor_tensor(
                                                   out=xn_t(k, nn, CW0),
                                                   in0=x[:, k * CW0:(k + 1) * CW0],
                                                   in1=o, op=ALU.mult))
                        xn_done = max(xn_wr)
                    if l == 0 and "xn" in tap_out:
                        rec.wt("gpsimd", dv, xn_done)
                        gdma(tap_out["xn"], xn[:, :])
                    if l > 0:
                        rec.wt("sync", dv, xn_done)
                        xn_dump_q = sdma(
                            bass.AP(sp_xn.tensor, 0,
                                    [[TL, P], [P * TL, KT], [1, TL]]),
                            xn[:, :])
                        wtd("sync", "gpsimd", state["l0_acc_q"])
                        x1cp = [sdma(bass.AP(xnext1.tensor,
                                             q_ * (D // 4) * TL,
                                             [[TL, D // 4], [1, TL]]),
                                     bass.AP(xnext0.tensor,
                                             q_ * (D // 4) * TL,
                                             [[TL, D // 4], [1, TL]]))
                                for q_ in range(4)]
                        state["x1cp_q"] = x1cp[-1]

                    p1 = ExitStack()
                    with p1:
                        pe1 = p1.enter_context
                        if l == 0:
                            wm = pe1(nc.sbuf_tensor(_uname("wm"), [P, TL], F32))
                            wex = [pe1(nc.sbuf_tensor(_uname(f"wex{i}"), [P, TL], BF16))
                                   for i in range(2)]
                            um = [pe1(nc.sbuf_tensor(_uname(f"um{i}"), [P, TL], BF16))
                                  for i in range(2)]
                            srm = pe1(nc.sbuf_tensor(_uname("srm"), [P, TL], BF16))
                            scr = pe1(nc.sbuf_tensor(_uname("scr"), [P, TL], F32))
                            sw1 = WSweep(
                                [(wi[w][m], KT * P) for m in range(KT)
                                 for w in ("wkT", "wvT", "wrT")])
                            wex_free = [[], []]
                            um_free = [[], []]
                            srm_free = []
                            wm_free = []
                            for m in range(KT):
                                j = m % 2
                                dcol = cons[:, C_DEC + m:C_DEC + m + 1]
                                sl, lv = sw1.get(3 * m)
                                wtd("tensor", "gpsimd", lv)
                                if m == 0:
                                    rec.wt("tensor", dv, xn_done)
                                banks = []
                                for n in range(NCH):
                                    bi, bk = pool.acquire()
                                    for kk in range(KT):
                                        mm(bk[:, :], ws_tile(sl, kk), xn_t(kk, n),
                                           kk == 0, kk == KT - 1, inc=(kk == KT - 1))
                                    banks.append((bi, bk, pe.n))
                                ws_done(sl, pe.n)
                                for s_, v_ in wm_free:
                                    wt_any("vector", s_, v_)
                                wm_free = []
                                for n, (bi, bk, pv) in enumerate(banks):
                                    rec.wt("vector", pe, pv)
                                    cv = dve(lambda v, b=bk, n_=n: v.tensor_scalar(
                                        out=wm[:, n_ * CW:(n_ + 1) * CW], in0=b[:, :],
                                        scalar1=-10.0, scalar2=10.0,
                                        op0=ALU.max, op1=ALU.min))
                                    pool.release(bi, dv, cv)
                                clip_v = dv.n
                                wb = wex[j]
                                for s_, v_ in wex_free[j]:
                                    wt_any("scalar", s_, v_)
                                wex_free[j] = []
                                rec.wt("scalar", dv, clip_v)
                                rec.wt("scalar", ac, ff_ready)
                                ev = act(lambda s, w=wb: s.activation(
                                    out=w[:, :], in_=wm[:, :], func=AF.Exp))
                                wm_free = [(ac, ev)]
                                rec.wt("vector", ac, ev)
                                fv = dve(lambda v, w=wb, m_=m: v.tensor_tensor(
                                    out=w[:, 0:1], in0=w[:, 0:1],
                                    in1=cons[:, C_FF + m_:C_FF + m_ + 1],
                                    op=ALU.mult))
                                rec.wt("sync", dv, fv)
                                wq = sdma(sp_w[m * P:(m + 1) * P, :], wb[:, :])
                                d_bc = bass.broadcast_tensor_aps(wb[:, :], dcol)[1]
                                rec.wt("vector", ac, dec_ready)
                                dve(lambda v, w=wb, db=d_bc: v.tensor_tensor_scan(
                                    out=scr[:, :], data0=db, data1=w[:, :],
                                    initial=0.0, op0=ALU.mult, op1=ALU.add),
                                    inc=False)
                                dsv = dve(lambda v, m_=m: v.tensor_copy(
                                    out=finals[:, 16 + m_:17 + m_],
                                    in_=scr[:, TL - 1:TL]))
                                sl2, lv2 = sw1.get(3 * m + 1)
                                wtd("tensor", "gpsimd", lv2)
                                banks = []
                                for n in range(NCH):
                                    bi, bk = pool.acquire()
                                    for kk in range(KT):
                                        mm(bk[:, :], ws_tile(sl2, kk), xn_t(kk, n),
                                           kk == 0, kk == KT - 1, inc=(kk == KT - 1))
                                    banks.append((bi, bk, pe.n))
                                ws_done(sl2, pe.n)
                                ub = um[j]
                                for s_, v_ in um_free[j]:
                                    wt_any("vector", s_, v_)
                                um_free[j] = []
                                for n, (bi, bk, pv) in enumerate(banks):
                                    rec.wt("vector", pe, pv)
                                    uv = dve(lambda v, b=bk, n_=n, w=wb, u=ub:
                                             v.tensor_tensor(
                                                 out=u[:, n_ * CW:(n_ + 1) * CW],
                                                 in0=w[:, n_ * CW:(n_ + 1) * CW],
                                                 in1=b[:, :], op=ALU.mult))
                                    pool.release(bi, dv, uv)
                                u_v = dv.n
                                rec.wt("sync", dv, u_v)
                                uq = sdma(sp_u[m * P:(m + 1) * P, :], ub[:, :])
                                wex_free[j] = [("dmas", wq), (dv, u_v)]
                                dve(lambda v, u=ub, db=d_bc: v.tensor_tensor_scan(
                                    out=scr[:, :], data0=db, data1=u[:, :],
                                    initial=0.0, op0=ALU.mult, op1=ALU.add),
                                    inc=False)
                                nsv = dve(lambda v, m_=m: v.tensor_copy(
                                    out=finals[:, m_:m_ + 1], in_=scr[:, TL - 1:TL]))
                                um_free[j] = [("dmas", uq), (dv, nsv)]
                                sl3, lv3 = sw1.get(3 * m + 2)
                                wtd("tensor", "gpsimd", lv3)
                                for s_, v_ in srm_free:
                                    wt_any("scalar", s_, v_)
                                srm_free = []
                                for n in range(NCH):
                                    bi, bk = pool.acquire()
                                    for kk in range(KT):
                                        mm(bk[:, :], ws_tile(sl3, kk), xn_t(kk, n),
                                           kk == 0, kk == KT - 1, inc=(kk == KT - 1))
                                    rv = act(lambda s, b=bk, n_=n, pv=pe.n:
                                             (s.wait_ge(pe.h, pv),
                                              s.activation(
                                                  out=srm[:, n_ * CW:(n_ + 1) * CW],
                                                  in_=b[:, :], func=AF.Sigmoid))[1]
                                             ) if False else None
                                    rec.wt("scalar", pe, pe.n)
                                    rv = act(lambda s, b=bk, n_=n: s.activation(
                                        out=srm[:, n_ * CW:(n_ + 1) * CW],
                                        in_=b[:, :], func=AF.Sigmoid))
                                    pool.release(bi, ac, rv)
                                ws_done(sl3, pe.n)
                                rec.wt("sync", ac, ac.n)
                                rq = sdma(sp_r[m * P:(m + 1) * P, :], srm[:, :])
                                srm_free = [("dmas", rq)]
                            scan1_done = dv.n
                        else:
                            am = [pe1(nc.sbuf_tensor(_uname(f"am{i}"), [P, TL], BF16))
                                  for i in range(2)]
                            sgm = pe1(nc.sbuf_tensor(_uname("sgm"), [P, TL], BF16))
                            iim = [pe1(nc.sbuf_tensor(_uname(f"iim{i}"), [P, TL], BF16))
                                   for i in range(2)]
                            scr = pe1(nc.sbuf_tensor(_uname("scr1"), [P, TL], F32))
                            sw1b = WSweep(
                                [(wi[w][m], KT * P) for m in range(KT)
                                 for w in ("wfT", "wiT", "wgT")])
                            am_free = [[], []]
                            iim_free = [[], []]
                            sgm_free = []
                            for m in range(KT):
                                j = m % 2
                                sl, lv = sw1b.get(3 * m)
                                wtd("tensor", "gpsimd", lv)
                                if m == 0:
                                    rec.wt("tensor", dv, xn_done)
                                ab = am[j]
                                for s_, v_ in am_free[j]:
                                    wt_any("scalar", s_, v_)
                                am_free[j] = []
                                for s_, v_ in sgm_free:
                                    wt_any("scalar", s_, v_)
                                sgm_free = []
                                for n in range(NCH):
                                    bi, bk = pool.acquire()
                                    for kk in range(KT):
                                        mm(bk[:, :], ws_tile(sl, kk), xn_t(kk, n),
                                           kk == 0, kk == KT - 1, inc=(kk == KT - 1))
                                    rec.wt("scalar", pe, pe.n)
                                    act(lambda s, b=bk, n_=n, a_=ab: s.activation(
                                        out=a_[:, n_ * CW:(n_ + 1) * CW],
                                        in_=b[:, :], func=AF.Sigmoid), inc=False)
                                    gv = act(lambda s, b=bk, n_=n: s.activation(
                                        out=sgm[:, n_ * CW:(n_ + 1) * CW],
                                        in_=b[:, :], func=AF.Sigmoid, scale=-1.0))
                                    pool.release(bi, ac, gv)
                                a_done = ac.n
                                ws_done(sl, pe.n)
                                rec.wt("sync", ac, a_done)
                                aq = sdma(sp_w[m * P:(m + 1) * P, :], ab[:, :])
                                sl2, lv2 = sw1b.get(3 * m + 1)
                                wtd("tensor", "gpsimd", lv2)
                                ib = iim[j]
                                for s_, v_ in iim_free[j]:
                                    wt_any("scalar", s_, v_)
                                iim_free[j] = []
                                for n in range(NCH):
                                    bi, bk = pool.acquire()
                                    for kk in range(KT):
                                        mm(bk[:, :], ws_tile(sl2, kk), xn_t(kk, n),
                                           kk == 0, kk == KT - 1, inc=(kk == KT - 1))
                                    rec.wt("scalar", pe, pe.n)
                                    sv_ = act(lambda s, b=bk, n_=n, i_=ib:
                                              s.activation(
                                                  out=i_[:, n_ * CW:(n_ + 1) * CW],
                                                  in_=b[:, :], func=AF.Silu))
                                    pool.release(bi, ac, sv_)
                                ws_done(sl2, pe.n)
                                rec.wt("vector", ac, ac.n)
                                iv = dve(lambda v, i_=ib: v.tensor_tensor(
                                    out=i_[:, :], in0=i_[:, :], in1=sgm[:, :],
                                    op=ALU.mult))
                                rec.wt("sync", dv, iv)
                                iq = sdma(sp_u[m * P:(m + 1) * P, :], ib[:, :])
                                dve(lambda v, a_=ab, i_=ib: v.tensor_tensor_scan(
                                    out=scr[:, :], data0=a_[:, :], data1=i_[:, :],
                                    initial=0.0, op0=ALU.mult, op1=ALU.add),
                                    inc=False)
                                fsv = dve(lambda v, m_=m: v.tensor_copy(
                                    out=finals[:, m_:m_ + 1], in_=scr[:, TL - 1:TL]))
                                am_free[j] = [("dmas", aq), (dv, fsv)]
                                iim_free[j] = [("dmas", iq), (dv, fsv)]
                                sl3, lv3 = sw1b.get(3 * m + 2)
                                wtd("tensor", "gpsimd", lv3)
                                rec.wt("scalar", dv, iv)
                                for n in range(NCH):
                                    bi, bk = pool.acquire()
                                    for kk in range(KT):
                                        mm(bk[:, :], ws_tile(sl3, kk), xn_t(kk, n),
                                           kk == 0, kk == KT - 1, inc=(kk == KT - 1))
                                    rec.wt("scalar", pe, pe.n)
                                    gv2 = act(lambda s, b=bk, n_=n: s.activation(
                                        out=sgm[:, n_ * CW:(n_ + 1) * CW],
                                        in_=b[:, :], func=AF.Silu))
                                    pool.release(bi, ac, gv2)
                                ws_done(sl3, pe.n)
                                rec.wt("sync", ac, ac.n)
                                gq = sdma(sp_r[m * P:(m + 1) * P, :], sgm[:, :])
                                sgm_free = [("dmas", gq)]
                            scan1_done = dv.n
                            wtd("tensor", "gpsimd", g1_q)
                            for n in range(NCH):
                                bi, bk = pool.acquire()
                                for kk in range(KT):
                                    mm(bk[0:8, :], g1sb[:, kk * 8:(kk + 1) * 8],
                                       xn_t(kk, n), kk == 0, kk == KT - 1,
                                       inc=(kk == KT - 1))
                                rec.wt("scalar", pe, pe.n)
                                lv_ = act(lambda s, b=bk, n_=n: s.copy(
                                    out=logits_sb[0:8, n_ * CW:(n_ + 1) * CW],
                                    in_=b[0:8, :]))
                                pool.release(bi, ac, lv_)
                            logits_done = ac.n
                            emit_l1_routing(logits_sb, logits_done, combT)

                    rec.wt("gpsimd", dv, scan1_done)
                    gdma(cin, finals)
                    rec.wt("gpsimd", qW, qW.n)
                    rec.em("gpsimd", lambda g, i=cin, o=cout, c=cc.h:
                           g.collective_compute(
                               "AllGather", ALU.bypass,
                               replica_groups=[list(range(8))],
                               ins=[i.opt()], outs=[o.opt()]).then_inc(c),
                           incs=((cc.name, 1),))
                    cc.inc()

                    def consume_carry():
                        rec.wt("gpsimd", cc, 1)
                        ag_ap = bass.AP(cout.tensor, 0,
                                        [[32, P], [P * 32, 8], [1, 32]])
                        agq = gdma(agsb[:, :], ag_ap)
                        wtd("vector", "gpsimd", agq)
                        sel3 = cons[:, C_SEL:C_SEL + 8].rearrange(
                            "p (r o) -> p r o", o=1)
                        ag3 = agsb[:, :].rearrange("p (r c) -> p r c", r=8)
                        sel_b = bass.broadcast_tensor_aps(ag3, sel3)[1]
                        dve(lambda v: v.tensor_tensor(out=ag3, in0=ag3, in1=sel_b,
                                                      op=ALU.mult), inc=False)
                        red_in = bass.AP(agsb[:, :].tensor, agsb[:, :].offset,
                                         [agsb[:, :].ap[0], [1, 32], [32, 8]])
                        carry3 = carry.rearrange("p (c o) -> p c o", o=1)
                        return dve(lambda v: v.tensor_reduce(
                            out=carry3, in_=red_in, axis=AXX, op=ALU.add))

                    if l == 0:
                        emit_l0_p3a(xn_t, xn_done, xlat)
                        xn_last_pe = pe.n
                    else:
                        xn_last_pe = pe.n
                if l == 0:
                    carry_v = consume_carry()
                    emit_l0_p3b(xlat, xdst, carry_v, xn_last_pe)
                    emit_l0_p4(xlat, xdst)
                    state["l0_acc_q"] = qW.n
                    if "x1" in tap_out:
                        rec.wt("gpsimd", qW, qW.n)
                        gdma(tap_out["x1"], xnext0)
                else:
                    emit_l1_p3(logits_sb, logits_done, combT, xdst, consume_carry,
                               xn_last_pe)
                    emit_l1_p4(combT, xdst, xn_dump_q)
            return None

        def emit_l0_p3a(xn_t, xn_done, xlat):
            p3 = ExitStack()
            with p3:
                pe3 = p3.enter_context
                scores = pe3(nc.sbuf_tensor(_uname("scores"), [32, TL], F32))
                sbbf = scores
                tm = pe3(nc.sbuf_tensor(_uname("tm"), [P, KT * 32], F32))
                tms = pe3(nc.sbuf_tensor(_uname("tms"), [P, KT * 32], F32))
                sel8 = pe3(nc.sbuf_tensor(_uname("sel8"), [P, KT * 32], F32))
                t4 = pe3(nc.sbuf_tensor(_uname("t4"), [P, KT * 32], F32))
                red1 = pe3(nc.sbuf_tensor(_uname("red1"), [P, KT * 4], F32))
                red2 = pe3(nc.sbuf_tensor(_uname("red2"), [P, KT * 4], F32))
                scw16 = pe3(nc.sbuf_tensor(_uname("scw16"), [P, KT * 32], BF16))
                scwT = pe3(nc.sbuf_tensor(_uname("scwT"), [32, TL], BF16))
                up16 = pe3(nc.sbuf_tensor(_uname("up16"), [P, 8 * TL], BF16))
                relu_t = pe3(nc.sbuf_tensor(_uname("relu_t"), [P, 2 * CW], F32))

                wtd("tensor", "gpsimd", g0_q)
                rec.wt("tensor", dv, xn_done)
                for n in range(NCH):
                    bi, bk = pool.acquire()
                    for kk in range(KT):
                        mm(bk[0:32, :], g0sb[:, kk * 32:(kk + 1) * 32],
                           xn_t(kk, n), kk == 0, kk == KT - 1, inc=(kk == KT - 1))
                    rec.wt("scalar", pe, pe.n)
                    sv_ = act(lambda s, b=bk, n_=n: s.activation(
                        out=scores[0:32, n_ * CW:(n_ + 1) * CW],
                        in_=b[0:32, :], func=AF.Sigmoid))
                    pool.release(bi, ac, sv_)
                rec.wt("vector", ac, ac.n)
                wtd("vector", "sync", cst_q)
                sb16v = dve(lambda v: v.tensor_scalar(
                    out=sbbf[0:32, :], in0=scores[0:32, :],
                    scalar1=gbcol[0:32, 0:1], scalar2=None, op0=ALU.add))
                rec.wt("gpsimd", qW, cst_w)
                prev_copy = 0
                for rnd in range(4):
                    bi_t, bk_t = pool.acquire()
                    rec.wt("tensor", dv, sb16v)
                    rec.wt("tensor", ac, prev_copy)
                    for c in range(4):
                        cg = rnd * 4 + c
                        inc = c == 3
                        rec.em("tensor", lambda t, c_=c, cg_=cg, b=bk_t, i_=inc:
                               (t.transpose(b[:, c_ * 32:(c_ + 1) * 32],
                                            sbbf[0:32, cg_ * P:(cg_ + 1) * P],
                                            eye32f[0:32, 0:32]).then_inc(pe.h, 1)
                                if i_ else
                                t.transpose(b[:, c_ * 32:(c_ + 1) * 32],
                                            sbbf[0:32, cg_ * P:(cg_ + 1) * P],
                                            eye32f[0:32, 0:32])),
                               incs=((("pe", 1),) if inc else ()))
                    tv = pe.inc()
                    rec.wt("scalar", pe, tv)
                    prev_copy = act(lambda s, r=rnd, b=bk_t: s.copy(
                        out=tm[:, r * 128:(r + 1) * 128], in_=b[:, 0:128]))
                    pool.release(bi_t, ac, prev_copy)
                rec.wt("vector", ac, prev_copy)
                tm3 = tm[:, :].rearrange("p (c e) -> p c e", e=32)
                gb3 = gbrep[:, :].rearrange("p (o e) -> p o e", o=1)
                gb_b = bass.broadcast_tensor_aps(tm3, gb3)[1]
                tms3 = tms[:, :].rearrange("p (c e) -> p c e", e=32)
                dve(lambda v: v.tensor_tensor(out=tms3, in0=tm3, in1=gb_b,
                                              op=ALU.subtract), inc=False)
                tm4 = tm[:, :].rearrange("p (c g e) -> p c g e", g=4, e=8)
                r14 = red1[:, :].rearrange("p (c g o) -> p c g o", g=4, o=1)
                dve(lambda v: v.tensor_reduce(out=r14, in_=tm4, axis=AXX,
                                              op=ALU.max), inc=False)
                r1b = bass.broadcast_tensor_aps(tm4, r14)[1]
                t44 = t4[:, :].rearrange("p (c g e) -> p c g e", g=4, e=8)
                dve(lambda v: v.tensor_tensor(out=t44, in0=tm4, in1=r1b,
                                              op=ALU.is_equal), inc=False)
                dve(lambda v: v.scalar_tensor_tensor(out=t44, in0=t44, scalar=-BIG,
                                                     in1=tm4, op0=ALU.mult,
                                                     op1=ALU.add), inc=False)
                r24 = red2[:, :].rearrange("p (c g o) -> p c g o", g=4, o=1)
                dve(lambda v: v.tensor_reduce(out=r24, in_=t44, axis=AXX,
                                              op=ALU.max), inc=False)
                dve(lambda v: v.tensor_tensor(out=red1[:, :], in0=red1[:, :],
                                              in1=red2[:, :], op=ALU.add), inc=False)
                gs3 = red1[:, :].rearrange("p (c g) -> p c g", g=4)
                g14 = red2[:, 0:KT].rearrange("p (c o) -> p c o", o=1)
                dve(lambda v: v.tensor_reduce(out=g14, in_=gs3, axis=AXX,
                                              op=ALU.max), inc=False)
                g1b = bass.broadcast_tensor_aps(gs3, g14)[1]
                e1g = t4[:, 0:64].rearrange("p (c g) -> p c g", g=4)
                dve(lambda v: v.tensor_tensor(out=e1g, in0=gs3, in1=g1b,
                                              op=ALU.is_equal), inc=False)
                dve(lambda v: v.scalar_tensor_tensor(out=gs3, in0=e1g, scalar=-BIG,
                                                     in1=gs3, op0=ALU.mult,
                                                     op1=ALU.add), inc=False)
                g24 = red2[:, KT:2 * KT].rearrange("p (c o) -> p c o", o=1)
                dve(lambda v: v.tensor_reduce(out=g24, in_=gs3, axis=AXX,
                                              op=ALU.max), inc=False)
                g2b = bass.broadcast_tensor_aps(gs3, g24)[1]
                dve(lambda v: v.tensor_tensor(out=gs3, in0=gs3, in1=g2b,
                                              op=ALU.is_equal), inc=False)
                dve(lambda v: v.tensor_tensor(out=gs3, in0=gs3, in1=e1g,
                                              op=ALU.add), inc=False)
                gm4 = red1[:, :].rearrange("p (c g o) -> p c g o", g=4, o=1)
                gm_b = bass.broadcast_tensor_aps(tm4, gm4)[1]
                dve(lambda v: v.tensor_scalar(out=t44, in0=gm_b, scalar1=BIG,
                                              scalar2=-BIG, op0=ALU.mult,
                                              op1=ALU.add), inc=False)
                dve(lambda v: v.tensor_tensor(out=t44, in0=t44, in1=tm4,
                                              op=ALU.add), inc=False)
                tmm = t4[:, :].rearrange("p (c e) -> p c e", e=32)
                mi = red2[:, 0:KT].rearrange("p (c o) -> p c o", o=1)
                sel3_ = sel8[:, :].rearrange("p (c e) -> p c e", e=32)
                eq3 = tm[:, :].rearrange("p (c e) -> p c e", e=32)
                for it in range(8):
                    dve(lambda v: v.tensor_reduce(out=mi, in_=tmm, axis=AXX,
                                                  op=ALU.max), inc=False)
                    mib = bass.broadcast_tensor_aps(tmm, mi)[1]
                    if it == 0:
                        dve(lambda v: v.tensor_tensor(out=sel3_, in0=tmm, in1=mib,
                                                      op=ALU.is_equal), inc=False)
                        dve(lambda v: v.scalar_tensor_tensor(
                            out=tmm, in0=sel3_, scalar=-BIG, in1=tmm,
                            op0=ALU.mult, op1=ALU.add), inc=False)
                    else:
                        dve(lambda v: v.tensor_tensor(out=eq3, in0=tmm, in1=mib,
                                                      op=ALU.is_equal), inc=False)
                        dve(lambda v: v.tensor_tensor(out=sel3_, in0=sel3_, in1=eq3,
                                                      op=ALU.add), inc=False)
                        dve(lambda v: v.scalar_tensor_tensor(
                            out=tmm, in0=eq3, scalar=-BIG, in1=tmm,
                            op0=ALU.mult, op1=ALU.add), inc=False)
                dve(lambda v: v.tensor_tensor(out=tms3, in0=tms3, in1=sel3_,
                                              op=ALU.mult), inc=False)
                ssum = red2[:, 0:KT].rearrange("p (c o) -> p c o", o=1)
                dve(lambda v: v.tensor_reduce(out=ssum, in_=tms3, axis=AXX,
                                              op=ALU.add), inc=False)
                dve(lambda v: v.reciprocal(out=red2[:, 0:KT], in_=red2[:, 0:KT]),
                    inc=False)
                sb_b = bass.broadcast_tensor_aps(tms3, ssum)[1]
                dve(lambda v: v.tensor_tensor(out=tms3, in0=tms3, in1=sb_b,
                                              op=ALU.mult), inc=False)
                cwv = dve(lambda v: v.tensor_scalar(out=tms[:, :], in0=tms[:, :],
                                                    scalar1=2.5, scalar2=None,
                                                    op0=ALU.mult))
                rec.wt("scalar", dv, cwv)
                act(lambda s: s.activation(out=tms[:, :], in_=tms[:, :], func=AF.Ln,
                                           bias=cons[:, C_EPS + 2:C_EPS + 3]), inc=False)
                scv = act(lambda s: s.activation(out=tms[:, :], in_=tms[:, :],
                                                 func=AF.Exp, scale=0.5))
                rec.wt("vector", ac, scv)
                s16v = dve(lambda v: v.tensor_copy(out=scw16[:, :], in_=tms[:, :]))
                prev_copy2 = 0
                for rnd in range(4):
                    rec.wt("tensor", dv, s16v)
                    rec.wt("tensor", ac, prev_copy2)
                    for c in range(4):
                        cg = rnd * 4 + c
                        src = bass.AP(scw16[:, :].tensor,
                                      scw16[:, :].offset + cg * 32,
                                      [scw16[:, :].ap[0], [1, 32]])
                        if c < 3:
                            rec.em("tensor", lambda t, c_=c, s_=src:
                                   t.transpose(ps16[0:32, c_ * 128:(c_ + 1) * 128],
                                               s_, cons16[:, B_EYE128:B_EYE128 + P]))
                        else:
                            rec.em("tensor", lambda t, c_=c, s_=src:
                                   t.transpose(ps16[0:32, c_ * 128:(c_ + 1) * 128],
                                               s_, cons16[:, B_EYE128:B_EYE128 + P])
                                   .then_inc(pe.h, 1), incs=(("pe", 1),))
                            tv = pe.inc()
                    rec.wt("scalar", pe, tv)
                    prev_copy2 = act(lambda s, r=rnd: s.copy(
                        out=scwT[0:32, r * CW:(r + 1) * CW], in_=ps16[0:32, 0:CW]))
                rec.wt("sync", ac, prev_copy2)
                scw_q = sdma(scw_d, scwT[0:32, :])
                state["scw_q"] = scw_q

                swf = WSweep([(wi["fc1T"][g], KT * P) for g in range(8)])
                for g in range(8):
                    sl, lv = swf.get(g)
                    wtd("tensor", "gpsimd", lv)
                    for n in range(NCH):
                        bi, bk = pool.acquire()
                        for kk in range(KT):
                            mm(bk[:, :], ws_tile(sl, kk), xn_t(kk, n),
                               kk == 0, kk == KT - 1, inc=(kk == KT - 1))
                        rec.wt("scalar", pe, pe.n)
                        xv = act(lambda s, b=bk, g_=g, n_=n: s.copy(
                            out=xlat[:, g_ * TL + n_ * CW:g_ * TL + (n_ + 1) * CW],
                            in_=b[:, :]))
                        pool.release(bi, ac, xv)
                    ws_done(sl, pe.n)
                state["xlat_done"] = ac.n

                shg = []
                for ih_ in range(4):
                    for g_ in range(8):
                        shg.append((wi["upT"][ih_, g_], KT * P))
                    for g2_ in range(KT):
                        shg.append((bass.AP(wi["dnT"].tensor,
                                            g2_ * P * 32 * P + ih_ * 8 * P,
                                            [[32 * P, P], [1, 8 * P]]), 8 * P))
                swsh = WSweep(shg)
                for ih in range(4):
                    up_last = 0
                    for g in range(8):
                        sl, lv = swsh.get(ih * 24 + g)
                        wtd("tensor", "gpsimd", lv)
                        for n in range(NCH):
                            bi, bk = pool.acquire()
                            for kk in range(KT):
                                mm(bk[:, :], ws_tile(sl, kk), xn_t(kk, n),
                                   kk == 0, kk == KT - 1, inc=(kk == KT - 1))
                            rec.wt("vector", pe, pe.n)
                            rec.wt("vector", ac, up_last)
                            rt = relu_t[:, (n % 2) * CW:(n % 2 + 1) * CW]
                            rv_ = dve(lambda v, b=bk, r=rt: v.tensor_scalar(
                                out=r, in0=b[:, :], scalar1=0.0, scalar2=None,
                                op0=ALU.max))
                            pool.release(bi, dv, rv_)
                            rec.wt("scalar", dv, rv_)
                            up_last = act(lambda s, g_=g, n_=n, r=rt: s.activation(
                                out=up16[:, g_ * TL + n_ * CW:
                                         g_ * TL + (n_ + 1) * CW],
                                in_=r, func=AF.Square))
                        ws_done(sl, pe.n)
                    rec.wt("tensor", ac, up_last)
                    for g2 in range(KT):
                        sl, lv = swsh.get(ih * 24 + 8 + g2)
                        wtd("tensor", "gpsimd", lv)
                        for n in range(NCH):
                            bi, bk = pool.acquire()
                            for kk in range(8):
                                mm(bk[:, :], ws_tile(sl, kk),
                                   up16[:, kk * TL + n * CW:kk * TL + (n + 1) * CW],
                                   kk == 0, kk == 7, inc=(kk == 7))
                            if dbg_split:
                                stage_acc(bk[:, :], pe.n,
                                          xdst_tile(dbg["dbg_sh"], g2, n),
                                          accum=(ih > 0))
                            else:
                                stage_acc(bk[:, :], pe.n, xdst_tile(xnext0, g2, n))
                            pool.release(bi, ac, ac.n)
                        ws_done(sl, pe.n)

        def emit_l0_p3b(xlat, xdst, carry_v, xn_last_pe):
            TH = TL // 2
            p3b = ExitStack()
            with p3b:
                pb = p3b.enter_context
                rwkvH = [pb(nc.sbuf_tensor(_uname(f"rwkvH{i}"), [P, KT * TH],
                                           BF16)) for i in range(2)]
                wrb = [pb(nc.sbuf_tensor(_uname(f"wrb{i}"), [P, TH], BF16))
                       for i in range(2)]
                urb = [pb(nc.sbuf_tensor(_uname(f"urb{i}"), [P, TH], BF16))
                       for i in range(2)]
                srb = [pb(nc.sbuf_tensor(_uname(f"srb{i}"), [P, TH], BF16))
                       for i in range(2)]
                numb = pb(nc.sbuf_tensor(_uname("numb"), [P, TH], F32))
                denb = pb(nc.sbuf_tensor(_uname("denb"), [P, TH], F32))
                tb = pb(nc.sbuf_tensor(_uname("tb"), [P, TH], F32))
                stH = pb(nc.sbuf_tensor(_uname("stH"), [P, 32], F32))
                qfence()
                swo = WSweep([(wi["woT"][g], KT * P) for g in range(KT)] * 2)
                rb_free = [[0, 0], [0, 0], [0, 0]]

                def pre_kk(H, kk):
                    rwkv = rwkvH[H]
                    j = kk % 2
                    rec.wt("sync", dv, rb_free[0][j])
                    wq_ = sdma(wrb[j][:, :],
                               sp_w[kk * P:(kk + 1) * P, H * TH:(H + 1) * TH])
                    rec.wt("sync", dv, rb_free[1][j])
                    uq_ = sdma(urb[j][:, :],
                               sp_u[kk * P:(kk + 1) * P, H * TH:(H + 1) * TH])
                    rec.wt("sync", dv, rb_free[2][j])
                    rq_ = sdma(srb[j][:, :],
                               sp_r[kk * P:(kk + 1) * P, H * TH:(H + 1) * TH])
                    dcol = cons[:, C_DEC + kk:C_DEC + kk + 1]
                    d_bc = bass.broadcast_tensor_aps(numb[:, :], dcol)[1]
                    wtd("vector", "sync", max(wq_, uq_))
                    rec.wt("vector", dv, carry_v)
                    rec.wt("vector", pe, xn_last_pe)
                    nini = (cons[:, C_CN + kk:C_CN + kk + 1] if H == 0
                            else stH[:, kk:kk + 1])
                    dini = (cons[:, C_CN + 16 + kk:C_CN + 17 + kk] if H == 0
                            else stH[:, 16 + kk:17 + kk])
                    nv = dve(lambda v, u=urb[j], db=d_bc, ini=nini:
                             v.tensor_tensor_scan(
                                 out=numb[:, :], data0=db, data1=u[:, :],
                                 initial=ini, op0=ALU.mult, op1=ALU.add))
                    if H == 0:
                        dve(lambda v, k=kk: v.tensor_copy(
                            out=stH[:, k:k + 1], in_=numb[:, TH - 1:TH]),
                            inc=False)
                    dnv = dve(lambda v, w=wrb[j], db=d_bc, ini=dini:
                              v.tensor_tensor_scan(
                                  out=denb[:, :], data0=db, data1=w[:, :],
                                  initial=ini, op0=ALU.mult, op1=ALU.add))
                    rb_free[0][j] = dnv
                    rb_free[1][j] = dnv
                    if H == 0:
                        dve(lambda v, k=kk: v.tensor_copy(
                            out=stH[:, 16 + k:17 + k], in_=denb[:, TH - 1:TH]),
                            inc=False)
                    rec.wt("scalar", dv, dnv)
                    act(lambda s: s.activation(
                        out=tb[:, :], in_=denb[:, :], func=AF.Ln,
                        bias=cons[:, C_EPS + 1:C_EPS + 2]), inc=False)
                    rcv = act(lambda s: s.activation(out=tb[:, :], in_=tb[:, :],
                                                     func=AF.Exp, scale=-1.0))
                    rec.wt("vector", ac, rcv)
                    dve(lambda v: v.tensor_tensor(out=numb[:, :], in0=numb[:, :],
                                                  in1=tb[:, :], op=ALU.mult),
                        inc=False)
                    wtd("vector", "sync", rq_)
                    rv2 = dve(lambda v, s_=srb[j], k=kk, r=rwkv:
                              v.tensor_tensor(
                                  out=r[:, k * TH:(k + 1) * TH],
                                  in0=numb[:, :], in1=s_[:, :], op=ALU.mult))
                    rb_free[2][j] = rv2

                for kk in range(KT):
                    pre_kk(0, kk)
                rwkv_done_A = dv.n
                rec.wt("tensor", dv, rwkv_done_A)
                for g in range(KT):
                    sl, lv = swo.get(g)
                    wtd("tensor", "gpsimd", lv)
                    for n in range(2):
                        bi, bk = pool.acquire()
                        for kk in range(KT):
                            mm(bk[:, :], ws_tile(sl, kk),
                               rwkvH[0][:, kk * TH + n * CW:
                                         kk * TH + (n + 1) * CW],
                               kk == 0, kk == KT - 1, inc=(kk == KT - 1))
                        stage_acc(bk[:, :], pe.n, xdst_tile(xdst, g, n))
                        pool.release(bi, ac, ac.n)
                    ws_done(sl, pe.n)
                    pre_kk(1, g)
                rwkv_done_B = dv.n
                rec.wt("tensor", dv, rwkv_done_B)
                for g in range(KT):
                    sl, lv = swo.get(KT + g)
                    wtd("tensor", "gpsimd", lv)
                    for n in range(2):
                        bi, bk = pool.acquire()
                        for kk in range(KT):
                            mm(bk[:, :], ws_tile(sl, kk),
                               rwkvH[1][:, kk * TH + n * CW:
                                         kk * TH + (n + 1) * CW],
                               kk == 0, kk == KT - 1, inc=(kk == KT - 1))
                        stage_acc(bk[:, :], pe.n, xdst_tile(xdst, g, 2 + n))
                        pool.release(bi, ac, ac.n)
                    ws_done(sl, pe.n)

        def emit_l0_p4(xlat, xdst):
            p4 = ExitStack()
            with p4:
                p4e = p4.enter_context
                ylat = p4e(nc.sbuf_tensor(_uname("ylat"), [P, 8 * TL], F32))
                h16 = p4e(nc.sbuf_tensor(_uname("h16"), [P, 8 * TL], BF16))
                scwbc = [p4e(nc.sbuf_tensor(_uname(f"scwbc{i}"), [P, TL], F32))
                         for i in range(2)]
                scwrow = [p4e(nc.sbuf_tensor(_uname(f"scwrow{i}"), [1, TL], BF16))
                          for i in range(2)]
                relu2 = p4e(nc.sbuf_tensor(_uname("relu2"), [P, 2 * CW], F32))
                qfence()
                swe = WSweep([(wi[w][ee_, m_], 8 * P) for ee_ in range(32)
                              for w in ("w1T", "w2T") for m_ in range(8)])
                swc = WSweep([(wi["fc2T"][g], 8 * P) for g in range(KT)])
                scwrow_pe = [0, 0]
                scwbc_dv = [0, 0]
                h_free_pe = 0
                y_done = {}
                sq_last = 0
                for ee in range(32):
                    j = ee % 2
                    rec.wt("sync", qX, state["scw_q"])
                    rec.wt("sync", pe, scwrow_pe[j])
                    rq_ = sdma(scwrow[j][0:1, :], scw_d[ee:ee + 1, :])
                    for n in range(NCH):
                        bi, bk = pool.acquire()
                        wtd("tensor", "sync", rq_)
                        v_ = mm(bk[:, :], onesrow16[0:1, :],
                                scwrow[j][0:1, n * CW:(n + 1) * CW], True, True,
                                inc=True)
                        rec.wt("scalar", pe, v_)
                        rec.wt("scalar", dv, scwbc_dv[j])
                        cv_ = act(lambda s, b=bk, jj=j, n_=n: s.copy(
                            out=scwbc[jj][:, n_ * CW:(n_ + 1) * CW], in_=b[:, :]))
                        pool.release(bi, ac, cv_)
                    scwrow_pe[j] = pe.n
                    scw_ready = ac.n
                    for m in range(8):
                        sl, lv = swe.get(ee * 16 + m)
                        wtd("tensor", "gpsimd", lv)
                        if ee == 0 and m == 0:
                            rec.wt("tensor", ac, state["xlat_done"])
                        for n in range(NCH):
                            bi, bk = pool.acquire()
                            for kk in range(8):
                                mm(bk[:, :], ws_tile(sl, kk),
                                   xlat[:, kk * TL + n * CW:kk * TL + (n + 1) * CW],
                                   kk == 0, kk == 7, inc=(kk == 7))
                            rt = relu2[:, (n % 2) * CW:(n % 2 + 1) * CW]
                            rec.wt("vector", pe, pe.n)
                            rec.wt("vector", ac, scw_ready)
                            rec.wt("vector", ac, sq_last)
                            hv = dve(lambda v, b=bk, jj=j, n_=n, r=rt:
                                     v.scalar_tensor_tensor(
                                         out=r, in0=b[:, :], scalar=0.0,
                                         in1=scwbc[jj][:, n_ * CW:(n_ + 1) * CW],
                                         op0=ALU.max, op1=ALU.mult))
                            pool.release(bi, dv, hv)
                            rec.wt("scalar", dv, hv)
                            rec.wt("scalar", pe, h_free_pe)
                            sq_last = act(lambda s, m_=m, n_=n, r=rt: s.activation(
                                out=h16[:, m_ * TL + n_ * CW:m_ * TL + (n_ + 1) * CW],
                                in_=r, func=AF.Square))
                        ws_done(sl, pe.n)
                    h_done = sq_last
                    scwbc_dv[j] = dv.n
                    rec.wt("tensor", ac, h_done)
                    for m2 in range(8):
                        sl, lv = swe.get(ee * 16 + 8 + m2)
                        wtd("tensor", "gpsimd", lv)
                        for n in range(NCH):
                            bi, bk = pool.acquire()
                            for kk in range(8):
                                mm(bk[:, :], ws_tile(sl, kk),
                                   h16[:, kk * TL + n * CW:kk * TL + (n + 1) * CW],
                                   kk == 0, kk == 7, inc=(kk == 7))
                            rec.wt("vector", pe, pe.n)
                            if ee == 0:
                                yv = dve(lambda v, b=bk, m_=m2, n_=n: v.tensor_copy(
                                    out=ylat[:, m_ * TL + n_ * CW:
                                             m_ * TL + (n_ + 1) * CW],
                                    in_=b[:, :]))
                            else:
                                yv = dve(lambda v, b=bk, m_=m2, n_=n: v.tensor_tensor(
                                    out=ylat[:, m_ * TL + n_ * CW:
                                             m_ * TL + (n_ + 1) * CW],
                                    in0=ylat[:, m_ * TL + n_ * CW:
                                             m_ * TL + (n_ + 1) * CW],
                                    in1=b[:, :], op=ALU.add))
                            y_done[(m2, n)] = yv
                            pool.release(bi, dv, yv)
                        ws_done(sl, pe.n)
                    h_free_pe = pe.n
                rec.wt("scalar", dv, dv.n)
                y16 = h16
                y16v = act(lambda s: s.copy(out=y16[:, :], in_=ylat[:, :]))
                rec.wt("tensor", ac, y16v)
                for g in range(KT):
                    sl, lv = swc.get(g)
                    wtd("tensor", "gpsimd", lv)
                    for n in range(NCH):
                        bi, bk = pool.acquire()
                        for kk in range(8):
                            mm(bk[:, :], ws_tile(sl, kk),
                               y16[:, kk * TL + n * CW:kk * TL + (n + 1) * CW],
                               kk == 0, kk == 7, inc=(kk == 7))
                        if dbg_split:
                            stage_acc(bk[:, :], pe.n,
                                      xdst_tile(dbg["dbg_ro"], g, n), accum=False)
                        else:
                            stage_acc(bk[:, :], pe.n, xdst_tile(xdst, g, n))
                        pool.release(bi, ac, ac.n)
                    ws_done(sl, pe.n)


        def emit_l1_routing(logits_sb, logits_done, combT):
            p3r = ExitStack()
            with p3r:
                pl = p3r.enter_context
                tmE = pl(nc.sbuf_tensor(_uname("tmE"), [P, KT * 8], F32))
                tmS = pl(nc.sbuf_tensor(_uname("tmS"), [P, KT * 8], F32))
                tmM = pl(nc.sbuf_tensor(_uname("tmM"), [P, KT * 8], F32))
                redE = pl(nc.sbuf_tensor(_uname("redE"), [P, 2 * KT], F32))
                comb16 = pl(nc.sbuf_tensor(_uname("comb16"), [P, KT * 8], BF16))
                rec.wt("tensor", ac, logits_done)
                prev_copy = 0
                for rnd in range(4):
                    bi_t, bk_t = pool.acquire()
                    rec.wt("tensor", ac, prev_copy)
                    for c in range(4):
                        cg = rnd * 4 + c
                        inc = c == 3
                        rec.em("tensor", lambda t, c_=c, cg_=cg, b=bk_t, i_=inc:
                               (t.transpose(b[:, c_ * 8:(c_ + 1) * 8],
                                            logits_sb[0:8, cg_ * P:(cg_ + 1) * P],
                                            eye32f[0:8, 0:8]).then_inc(pe.h, 1)
                                if i_ else
                                t.transpose(b[:, c_ * 8:(c_ + 1) * 8],
                                            logits_sb[0:8, cg_ * P:(cg_ + 1) * P],
                                            eye32f[0:8, 0:8])),
                               incs=((("pe", 1),) if inc else ()))
                    tv = pe.inc()
                    rec.wt("scalar", pe, tv)
                    prev_copy = act(lambda s, r=rnd, b=bk_t: s.copy(
                        out=tmE[:, r * 32:(r + 1) * 32], in_=b[:, 0:32]))
                    pool.release(bi_t, ac, prev_copy)
                rec.wt("vector", ac, prev_copy)
                tm38 = tmE[:, :].rearrange("p (c e) -> p c e", e=8)
                mE = redE[:, 0:KT].rearrange("p (c o) -> p c o", o=1)
                dve(lambda v: v.tensor_reduce(out=mE, in_=tm38, axis=AXX,
                                              op=ALU.max), inc=False)
                mEb = bass.broadcast_tensor_aps(tm38, mE)[1]
                e38 = tmS[:, :].rearrange("p (c e) -> p c e", e=8)
                sv = dve(lambda v: v.tensor_tensor(out=e38, in0=tm38, in1=mEb,
                                                   op=ALU.subtract))
                rec.wt("scalar", dv, sv)
                ev_ = act(lambda s: s.activation(out=tmS[:, :], in_=tmS[:, :],
                                                 func=AF.Exp))
                rec.wt("vector", ac, ev_)
                dve(lambda v: v.tensor_reduce(out=mE, in_=e38, axis=AXX,
                                              op=ALU.max), inc=False)
                m1b = bass.broadcast_tensor_aps(e38, mE)[1]
                msk = tmM[:, :].rearrange("p (c e) -> p c e", e=8)
                dve(lambda v: v.tensor_tensor(out=msk, in0=e38, in1=m1b,
                                              op=ALU.is_equal), inc=False)
                dve(lambda v: v.tensor_tensor(out=msk, in0=e38, in1=msk,
                                              op=ALU.mult), inc=False)
                dve(lambda v: v.tensor_tensor(out=msk, in0=e38, in1=msk,
                                              op=ALU.subtract), inc=False)
                e2 = redE[:, KT:2 * KT].rearrange("p (c o) -> p c o", o=1)
                dve(lambda v: v.tensor_reduce(out=e2, in_=msk, axis=AXX,
                                              op=ALU.max), inc=False)
                dve(lambda v: v.tensor_tensor(out=redE[:, 0:KT], in0=redE[:, 0:KT],
                                              in1=redE[:, KT:2 * KT], op=ALU.add),
                    inc=False)
                m2b = bass.broadcast_tensor_aps(e38, e2)[1]
                dve(lambda v: v.tensor_tensor(out=msk, in0=e38, in1=m2b,
                                              op=ALU.is_ge), inc=False)
                dve(lambda v: v.tensor_tensor(out=msk, in0=e38, in1=msk,
                                              op=ALU.mult), inc=False)
                dve(lambda v: v.reciprocal(out=redE[:, 0:KT], in_=redE[:, 0:KT]),
                    inc=False)
                dnb = bass.broadcast_tensor_aps(msk, mE)[1]
                dve(lambda v: v.tensor_tensor(out=msk, in0=msk, in1=dnb,
                                              op=ALU.mult), inc=False)
                al3 = cons[:, C_ALPHA:C_ALPHA + 8].rearrange("p (o e) -> p o e", o=1)
                alb = bass.broadcast_tensor_aps(msk, al3)[1]
                cbv = dve(lambda v: v.tensor_tensor(out=msk, in0=msk, in1=alb,
                                                    op=ALU.mult))
                rec.wt("vector", dv, cbv)
                c16v = dve(lambda v: v.tensor_copy(out=comb16[:, :], in_=tmM[:, :]))
                prev_copy = 0
                for rnd in range(4):
                    rec.wt("tensor", dv, c16v)
                    rec.wt("tensor", ac, prev_copy)
                    for c in range(4):
                        cg = rnd * 4 + c
                        src = bass.AP(comb16[:, :].tensor,
                                      comb16[:, :].offset + cg * 8,
                                      [comb16[:, :].ap[0], [1, 8]])
                        if c < 3:
                            rec.em("tensor", lambda t, c_=c, s_=src:
                                   t.transpose(ps16[0:8, c_ * 128:(c_ + 1) * 128],
                                               s_, cons16[:, B_EYE128:B_EYE128 + P]))
                        else:
                            rec.em("tensor", lambda t, c_=c, s_=src:
                                   t.transpose(ps16[0:8, c_ * 128:(c_ + 1) * 128],
                                               s_, cons16[:, B_EYE128:B_EYE128 + P])
                                   .then_inc(pe.h, 1), incs=(("pe", 1),))
                            tv = pe.inc()
                    rec.wt("scalar", pe, tv)
                    prev_copy = act(lambda s, r=rnd: s.copy(
                        out=combT[0:8, r * CW:(r + 1) * CW], in_=ps16[0:8, 0:CW]))
                cT_done = prev_copy
                state["cT_done"] = cT_done
                rec.wt("sync", ac, cT_done)
                state["comb_q"] = sdma(comb_d, combT[0:8, :])

        def emit_l1_p3(logits_sb, logits_done, combT, xdst, consume_carry,
                       xn_last_pe):
            TH = TL // 2
            p3L = ExitStack()
            with p3L:
                pl = p3L.enter_context
                o16H = [pl(nc.sbuf_tensor(_uname(f"o16H{i}"), [P, KT * TH], BF16))
                        for i in range(2)]
                arb = [pl(nc.sbuf_tensor(_uname(f"arb{i}"), [P, TH], BF16))
                       for i in range(2)]
                irb = [pl(nc.sbuf_tensor(_uname(f"irb{i}"), [P, TH], BF16))
                       for i in range(2)]
                grb = pl(nc.sbuf_tensor(_uname("grb"), [P, TH], BF16))
                hbuf = pl(nc.sbuf_tensor(_uname("hbuf"), [P, TH], F32))
                hs16 = pl(nc.sbuf_tensor(_uname("hs16"), [P, TH], BF16))
                rstdbc = [pl(nc.sbuf_tensor(_uname(f"rstdbc{i}"), [P, TH], F32))
                          for i in range(2)]
                tt = pl(nc.sbuf_tensor(_uname("tt"), [P, TH], F32))
                stI = pl(nc.sbuf_tensor(_uname("stI"), [P, KT], F32))
                stO = pl(nc.sbuf_tensor(_uname("stO"), [P, KT], F32))
                qfence()
                carry_v = consume_carry()
                wtd("gpsimd", "sync", state["x1cp_q"])
                swo1 = WSweep([(wi["wo1T"][g], KT * P) for g in range(KT)] * 2)
                o_done_H = [0, 0]
                ssq_pe_prev = 0
                for H in range(2):
                    o16 = o16H[H]
                    ssq = []
                    for n in range(2):
                        ssq.append(pool.acquire(hold=True))
                    rb2 = [[0, 0], [0, 0]]
                    for kk in range(KT):
                        j = kk % 2
                        rec.wt("sync", dv, rb2[0][j])
                        aq_ = sdma(arb[j][:, :],
                                   sp_w[kk * P:(kk + 1) * P, H * TH:(H + 1) * TH])
                        rec.wt("sync", dv, rb2[1][j])
                        iq_ = sdma(irb[j][:, :],
                                   sp_u[kk * P:(kk + 1) * P, H * TH:(H + 1) * TH])
                        wtd("vector", "sync", max(aq_, iq_))
                        rec.wt("vector", dv, carry_v)
                        rec.wt("vector", pe, xn_last_pe)
                        ini = (cons[:, C_CN + kk:C_CN + kk + 1] if H == 0
                               else stI[:, kk:kk + 1])
                        hv = dve(lambda v, a_=arb[j], i_=irb[j], ini_=ini:
                                 v.tensor_tensor_scan(
                                     out=hbuf[:, :], data0=a_[:, :], data1=i_[:, :],
                                     initial=ini_, op0=ALU.mult, op1=ALU.add))
                        if H == 0:
                            dve(lambda v, k=kk: v.tensor_copy(
                                out=stO[:, k:k + 1], in_=hbuf[:, TH - 1:TH]),
                                inc=False)
                        rb2[0][j] = hv
                        rb2[1][j] = hv
                        rec.wt("scalar", dv, hv)
                        rec.wt("scalar", pe, ssq_pe_prev)
                        sqv = act(lambda s: s.activation(out=hs16[:, :],
                                                         in_=hbuf[:, :],
                                                         func=AF.Square))
                        rec.wt("tensor", ac, sqv)
                        for n in range(2):
                            mm(ssq[n][1][0:1, :], cons16[:, B_ONES:B_ONES + 1],
                               hs16[:, n * CW:(n + 1) * CW], kk == 0, kk == KT - 1,
                               inc=(kk == KT - 1 and n == 1))
                    ssq_pe = pe.n
                    ssq_pe_prev = ssq_pe
                    rec.wt("scalar", pe, ssq_pe)
                    for n in range(2):
                        act(lambda s, b=ssq[n][1], n_=n: s.activation(
                            out=rows[0:1, n_ * CW:(n_ + 1) * CW], in_=b[0:1, :],
                            func=AF.Ln, bias=cons[0:1, C_EPS + 3:C_EPS + 4]),
                            inc=False)
                    for n in range(2):
                        act(lambda s, n_=n: s.activation(
                            out=rows[0:1, n_ * CW:(n_ + 1) * CW],
                            in_=rows[0:1, n_ * CW:(n_ + 1) * CW],
                            func=AF.Exp, scale=-0.5), inc=False)
                    rec.em("scalar", lambda s: s.copy(
                        out=rows[0:1, 2 * CW:2 * CW + 1],
                        in_=rows[0:1, 0:1]).then_inc(ac.h, 1),
                        incs=(("ac", 1),))
                    expv = ac.inc()
                    rec.wt("vector", ac, expv)
                    sqrtD = float(np.sqrt(D))
                    rowsv = dve(lambda v: v.tensor_scalar(
                        out=rows[0:1, 0:2 * CW], in0=rows[0:1, 0:2 * CW],
                        scalar1=sqrtD, scalar2=None, op0=ALU.mult))
                    for n in range(2):
                        bi, bk = pool.acquire()
                        rec.wt("tensor", dv, rowsv)
                        bv = mm(bk[:, :], onesrow[0:1, :],
                                rows[0:1, n * CW:(n + 1) * CW], True, True,
                                inc=True)
                        rec.wt("scalar", pe, bv)
                        rbv = act(lambda s, b=bk, n_=n, rb=rstdbc[H]: s.copy(
                            out=rb[:, n_ * CW:(n_ + 1) * CW], in_=b[:, :]))
                        pool.release(bi, ac, rbv)
                    rstd_done = ac.n
                    for n in range(2):
                        pool.release(ssq[n][0], ac, rstd_done)
                    g_free = 0
                    for kk in range(KT):
                        j = kk % 2
                        rec.wt("sync", dv, rb2[0][j])
                        aq_ = sdma(arb[j][:, :],
                                   sp_w[kk * P:(kk + 1) * P, H * TH:(H + 1) * TH])
                        rec.wt("sync", dv, rb2[1][j])
                        iq_ = sdma(irb[j][:, :],
                                   sp_u[kk * P:(kk + 1) * P, H * TH:(H + 1) * TH])
                        rec.wt("sync", dv, g_free)
                        gq_ = sdma(grb[:, :],
                                   sp_r[kk * P:(kk + 1) * P, H * TH:(H + 1) * TH])
                        wtd("vector", "sync", max(aq_, iq_))
                        ini = (cons[:, C_CN + kk:C_CN + kk + 1] if H == 0
                               else stI[:, kk:kk + 1])
                        hv = dve(lambda v, a_=arb[j], i_=irb[j], ini_=ini:
                                 v.tensor_tensor_scan(
                                     out=hbuf[:, :], data0=a_[:, :], data1=i_[:, :],
                                     initial=ini_, op0=ALU.mult, op1=ALU.add))
                        rb2[0][j] = hv
                        rb2[1][j] = hv
                        rec.wt("vector", ac, rstd_done)
                        dve(lambda v, k=kk: v.tensor_scalar(
                            out=tt[:, :], in0=hbuf[:, :],
                            scalar1=cons[:, C_GNW + k:C_GNW + k + 1], scalar2=None,
                            op0=ALU.mult), inc=False)
                        dve(lambda v, rb=rstdbc[H]: v.tensor_tensor(
                            out=tt[:, :], in0=tt[:, :], in1=rb[:, :],
                            op=ALU.mult), inc=False)
                        wtd("vector", "sync", gq_)
                        ov = dve(lambda v, k=kk, o_=o16: v.tensor_tensor(
                            out=o_[:, k * TH:(k + 1) * TH], in0=tt[:, :],
                            in1=grb[:, :], op=ALU.mult))
                        g_free = ov
                    if H == 0:
                        dve(lambda v: v.tensor_copy(out=stI[:, :], in_=stO[:, :]),
                            inc=False)
                    o_done_H[H] = dv.n
                for H in range(2):
                    o16 = o16H[H]
                    rec.wt("tensor", dv, o_done_H[H])
                    for g in range(KT):
                        sl, lv = swo1.get(H * KT + g)
                        wtd("tensor", "gpsimd", lv)
                        for n in range(2):
                            bi, bk = pool.acquire()
                            for kk in range(KT):
                                mm(bk[:, :], ws_tile(sl, kk),
                                   o16[:, kk * TH + n * CW:kk * TH + (n + 1) * CW],
                                   kk == 0, kk == KT - 1, inc=(kk == KT - 1))
                            if dbg_split:
                                stage_acc(bk[:, :], pe.n,
                                          xdst_tile(dbg["dbg_at"], g, H * 2 + n),
                                          accum=False)
                            else:
                                stage_acc(bk[:, :], pe.n,
                                          xdst_tile(xdst, g, H * 2 + n))
                            pool.release(bi, ac, ac.n)
                        ws_done(sl, pe.n)
            state["p3_last_pe"] = pe.n

        def emit_l1_p4(combT, xdst, xn_dump_q):
            p4L = ExitStack()
            with p4L:
                q4 = p4L.enter_context
                xnr = q4(nc.sbuf_tensor(_uname("xnr"), [P, KT * TL], BF16))
                hE = q4(nc.sbuf_tensor(_uname("hE"), [P, KT * TL], BF16))
                cbbc = [q4(nc.sbuf_tensor(_uname("cbbc0"), [P, TL], F32))] * 2
                cbrow = [q4(nc.sbuf_tensor(_uname("cbrow0"), [1, TL], BF16))] * 2
                gel_t = q4(nc.sbuf_tensor(_uname("gel_t"), [P, 2 * CW], F32))
                b2sb = [q4(nc.sbuf_tensor(_uname(f"b2sb{i}"), [8, P], BF16))
                        for i in range(2)]
                qfence()
                swg = WSweep([(wi[w][ee_, m_], KT * P) for ee_ in range(8)
                              for w in ("g1fc1", "g1fc2") for m_ in range(KT)])
                rec.wt("sync", qX, xn_dump_q)
                xnr_src = bass.AP(sp_xn.tensor, 0, [[TL, P], [P * TL, KT], [1, TL]])
                xq = sdma(xnr[:, :], xnr_src)
                cbrow_pe = 0
                cbbc_dv = 0
                hE_free_pe = 0
                hlast = 0
                for ee in range(8):
                    j = ee % 2
                    rec.wt("sync", qX, state["comb_q"])
                    rec.wt("sync", pe, cbrow_pe)
                    rq_ = sdma(cbrow[j][0:1, :], comb_d[ee:ee + 1, :])
                    for n in range(NCH):
                        bi, bk = pool.acquire()
                        wtd("tensor", "sync", rq_)
                        v_ = mm(bk[:, :], onesrow16[0:1, :],
                                cbrow[j][0:1, n * CW:(n + 1) * CW], True, True,
                                inc=True)
                        rec.wt("scalar", pe, v_)
                        rec.wt("scalar", dv, cbbc_dv)
                        cv_ = act(lambda s, b=bk, jj=j, n_=n: s.copy(
                            out=cbbc[jj][:, n_ * CW:(n_ + 1) * CW], in_=b[:, :]))
                        pool.release(bi, ac, cv_)
                    cbrow_pe = pe.n
                    cb_ready = ac.n
                    for m in range(KT):
                        sl, lv = swg.get(ee * 32 + m)
                        wtd("tensor", "gpsimd", lv)
                        if ee == 0 and m == 0:
                            wtd("tensor", "sync", xq)
                        for n in range(NCH):
                            bi, bk = pool.acquire()
                            for kk in range(KT):
                                mm(bk[:, :], ws_tile(sl, kk),
                                   xnr[:, kk * TL + n * CW:kk * TL + (n + 1) * CW],
                                   kk == 0, kk == KT - 1, inc=(kk == KT - 1))
                            gt = gel_t[:, (n % 2) * CW:(n % 2 + 1) * CW]
                            rec.wt("scalar", pe, pe.n)
                            rec.wt("scalar", dv, hlast)
                            wtd("scalar", "sync", cst_q)
                            gv_ = act(lambda s, b=bk, m_=m, e_=ee, g_=gt:
                                      s.activation(out=g_, in_=b[:, :], func=AF.Gelu,
                                                   bias=b1sb[:, m_ * 8 + e_:
                                                             m_ * 8 + e_ + 1]))
                            pool.release(bi, ac, gv_)
                            rec.wt("vector", ac, gv_)
                            rec.wt("vector", ac, cb_ready)
                            rec.wt("vector", pe, hE_free_pe)
                            hlast = dve(lambda v, jj=j, m_=m, n_=n, g_=gt:
                                        v.tensor_tensor(
                                            out=hE[:, m_ * TL + n_ * CW:
                                                   m_ * TL + (n_ + 1) * CW],
                                            in0=g_,
                                            in1=cbbc[jj][:, n_ * CW:(n_ + 1) * CW],
                                            op=ALU.mult))
                        ws_done(sl, pe.n)
                    hE_done = hlast
                    cbbc_dv = hlast
                    rec.wt("tensor", dv, hE_done)
                    for m2 in range(KT):
                        sl, lv = swg.get(ee * 32 + KT + m2)
                        wtd("tensor", "gpsimd", lv)
                        for n in range(NCH):
                            bi, bk = pool.acquire()
                            for kk in range(KT):
                                mm(bk[:, :], ws_tile(sl, kk),
                                   hE[:, kk * TL + n * CW:kk * TL + (n + 1) * CW],
                                   kk == 0, kk == KT - 1, inc=(kk == KT - 1))
                            if dbg_split:
                                stage_acc(bk[:, :], pe.n,
                                          xdst_tile(dbg["dbg_ro"], m2, n),
                                          accum=(ee > 0))
                            else:
                                stage_acc(bk[:, :], pe.n, xdst_tile(xdst, m2, n))
                            pool.release(bi, ac, ac.n)
                        ws_done(sl, pe.n)
                    hE_free_pe = pe.n
                b2_pe = [0, 0]
                for m2 in range(KT):
                    jb = m2 % 2
                    rec.wt("gpsimd", pe, b2_pe[jb])
                    bq_ = gdma(b2sb[jb][0:8, :],
                               wi["b2raw"][0:8, m2 * P:(m2 + 1) * P])
                    wtd("tensor", "gpsimd", bq_)
                    rec.wt("tensor", ac, state["cT_done"])
                    for n in range(NCH):
                        bi, bk = pool.acquire()
                        bv = mm(bk[:, :], b2sb[jb][0:8, :],
                                combT[0:8, n * CW:(n + 1) * CW], True, True,
                                inc=True)
                        if dbg_split:
                            stage_acc(bk[:, :], bv, xdst_tile(dbg["dbg_ro"], m2, n))
                        else:
                            stage_acc(bk[:, :], bv, xdst_tile(xdst, m2, n))
                        pool.release(bi, ac, ac.n)
                    b2_pe[jb] = pe.n
                    if not dbg_split:
                        rowap_o = bass.AP(out.tensor, m2 * P * TL, [[TL, P], [1, TL]])
                        rowap_x = bass.AP(xdst.tensor, m2 * P * TL, [[TL, P], [1, TL]])
                        wtd("sync", "gpsimd", qW.n)
                        sdma(rowap_o, rowap_x)

        state = {}
        emit_layer(0)
        if n_layers > 1:
            emit_layer(1)
            rec.wt("gpsimd", qW, qW.n)
            rec.wt("sync", qX, qX.n)
        else:
            rec.wt("gpsimd", qW, qW.n)
            gdma(out, xnext0)
            rec.wt("gpsimd", qW, qW.n)

        assert rec.check(), "semaphore schedule deadlock"

        with nc.Block() as block:
            @block.sync
            def _(sync):
                for fn in rec.ops["sync"]:
                    fn(sync)

            @block.gpsimd
            def _(g):
                for fn in rec.ops["gpsimd"]:
                    fn(g)

            @block.tensor
            def _(t):
                for fn in rec.ops["tensor"]:
                    fn(t)

            @block.vector
            def _(v):
                for fn in rec.ops["vector"]:
                    fn(v)

            @block.scalar
            def _(s):
                for fn in rec.ops["scalar"]:
                    fn(s)

    return nc


def host_pack(inputs, n_layers=2):
    f32 = np.float32
    g = {}
    g["wkT"] = pack_lhsT(np.asarray(inputs["L0_attn_Wk"]).T)
    g["wvT"] = pack_lhsT(np.asarray(inputs["L0_attn_Wv"]).T)
    g["wrT"] = pack_lhsT(np.asarray(inputs["L0_attn_Wr"]).T)
    g["woT"] = pack_lhsT(np.asarray(inputs["L0_attn_Wo"]).T)
    g["fc1T"] = pack_lhsT(np.asarray(inputs["L0_ffn_fc1_w"]).T)
    g["fc2T"] = pack_lhsT(np.asarray(inputs["L0_ffn_fc2_w"]).T)
    g["w1T"] = np.stack([pack_lhsT(np.asarray(w).T) for w in inputs["L0_ffn_w1"]])
    g["w2T"] = np.stack([pack_lhsT(np.asarray(w).T) for w in inputs["L0_ffn_w2"]])
    upT = pack_lhsT(np.asarray(inputs["L0_ffn_shared_up_w"]).T)
    g["upT"] = np.ascontiguousarray(upT.reshape(4, 8, P, KT * P))
    g["dnT"] = pack_lhsT(np.asarray(inputs["L0_ffn_shared_down_w"]).T)
    g["gate0T"] = pack_lhsT(np.asarray(inputs["L0_ffn_gate_w"]).T, msz=32)
    g["td_T"] = np.ascontiguousarray(
        np.asarray(inputs["L0_attn_time_decay"], dtype=f32).reshape(KT, P).T)
    g["tf_T"] = np.ascontiguousarray(
        np.asarray(inputs["L0_attn_time_first"], dtype=f32).reshape(KT, P).T)
    gb = np.asarray(inputs["L0_ffn_gate_bias"], dtype=f32)
    g["gb_col"] = np.ascontiguousarray(gb.reshape(32, 1))
    g["gb_rep"] = np.ascontiguousarray(np.tile(gb.reshape(1, 32), (P, 1)))
    if n_layers > 1:
        g["wiT"] = pack_lhsT(np.asarray(inputs["L1_attn_Wi"]))
        g["wfT"] = pack_lhsT(np.asarray(inputs["L1_attn_Wf"]))
        g["wgT"] = pack_lhsT(np.asarray(inputs["L1_attn_Wg"]))
        g["wo1T"] = pack_lhsT(np.asarray(inputs["L1_attn_Wo"]))
        g["g1fc1"] = np.stack([pack_lhsT(np.asarray(w))
                               for w in inputs["L1_ffn_fc1_w"]])
        g["g1fc2"] = np.stack([pack_lhsT(np.asarray(w))
                               for w in inputs["L1_ffn_fc2_w"]])
        g["gate1T"] = pack_lhsT(np.asarray(inputs["L1_ffn_gate_w"]), msz=8)
        b1 = np.asarray(inputs["L1_ffn_fc1_b"], dtype=f32)
        g["b1P"] = np.ascontiguousarray(
            b1.T.reshape(KT, P, 8).transpose(1, 0, 2).reshape(P, KT * 8))
        g["b2raw"] = np.ascontiguousarray(np.asarray(inputs["L1_ffn_fc2_b"],
                                                     dtype=f32))
        g["gnw"] = np.ascontiguousarray(
            np.asarray(inputs["L1_attn_g_norm_weight"], dtype=f32).reshape(KT, P).T)
        g["alpha_rep"] = np.ascontiguousarray(
            np.tile(np.asarray(inputs["L1_ffn_alpha"], dtype=f32).reshape(1, 8),
                    (P, 1)))
    g["ones_col"] = np.ones((P, 1), f32)
    g["ones_row"] = np.ones((1, P), f32)
    g["eye128"] = np.eye(P, dtype=f32)
    g["eye32"] = np.eye(32, dtype=f32)
    g["epsv"] = np.tile(np.array([[1.0e-5, 1.0e-6, 1.0e-30, 1.0e-5 * D]], f32), (P, 1))
    return g


def make_in_maps(inputs, n_layers=2):
    shared = host_pack(inputs, n_layers=n_layers)
    x = np.asarray(inputs["L0_attn_x"], dtype=np.float32)
    in_maps = []
    for c in range(8):
        b, h = c // 2, c % 2
        m = dict(shared)
        m["xres0"] = np.ascontiguousarray(
            x[b, h * TL:(h + 1) * TL, :].T)
        sel = np.zeros((P, 8), np.float32)
        if h == 1:
            sel[:, 2 * b] = 1.0
        m["sel_rep"] = sel
        m["fmask"] = np.full((P, 1), 1.0 if h == 0 else 0.0, np.float32)
        in_maps.append(m)
    return in_maps


def kernel(**inputs):
    n_layers = 2
    nc = build_program(n_layers=n_layers)
    in_maps = make_in_maps(inputs, n_layers=n_layers)
    res = run_bass_kernel_spmd(nc, in_maps, core_ids=list(range(8)))
    out = np.empty((4, 4096, D), np.float32)
    for c in range(8):
        b, h = c // 2, c % 2
        out[b, h * TL:(h + 1) * TL, :] = res.results[c]["out"].T
    return out


# version 12
# speedup vs baseline: 1.0207x; 1.0033x over previous
import numpy as np
from contextlib import ExitStack

import concourse.bass as bass
import concourse.mybir as mybir
from concourse.bass_utils import run_bass_kernel_spmd

F32 = mybir.dt.float32
BF16 = mybir.dt.bfloat16
AF = mybir.ActivationFunctionType
ALU = mybir.AluOpType
AXX = mybir.AxisListType.X

P = 128
D = 2048
KT = D // P
TL = 2048
NCH = 4
CW = 512
NC0 = 8
CW0 = 256
BIG = 1.0e30

ENGINES = ("sync", "gpsimd", "tensor", "vector", "scalar")


class Sem:
    def __init__(self, handle, name):
        self.h, self.name, self.n = handle, name, 0

    def inc(self, k=1):
        self.n += k
        return self.n


class Rec:
    def __init__(self):
        self.ops = {e: [] for e in ENGINES}
        self.meta = {e: [] for e in ENGINES}

    def em(self, eng, fn, incs=(), tag=""):
        self.ops[eng].append(fn)
        self.meta[eng].append(("op", tuple(incs), tag))

    def wt(self, eng, sem, val):
        if val and val > 0:
            self.ops[eng].append(lambda e, s=sem.h, v=val: e.wait_ge(s, v))
            self.meta[eng].append(("wait", (sem.name, val), ""))

    def check(self):
        heads = {e: 0 for e in ENGINES}
        sems = {}
        import collections
        progress = True
        while progress:
            progress = False
            for e in ENGINES:
                while heads[e] < len(self.meta[e]):
                    kind, data, tag = self.meta[e][heads[e]]
                    if kind == "wait":
                        name, val = data
                        if sems.get(name, 0) < val:
                            break
                    else:
                        for (name, amt) in data:
                            sems[name] = sems.get(name, 0) + amt
                    heads[e] += 1
                    progress = True
        stuck = {e: heads[e] for e in ENGINES if heads[e] < len(self.meta[e])}
        if stuck:
            print("DEADLOCK. engine heads:")
            for e, h in stuck.items():
                kind, data, tag = self.meta[e][h]
                print(f"  {e}@{h}/{len(self.meta[e])}: waiting {data}"
                      f" (sem now {sems.get(data[0], 0)}) {tag}")
            return False
        print("schedule check OK; final sems:", sems)
        return True


class Pool:

    def __init__(self, rec, tensors):
        self.rec, self.t = rec, tensors
        self.nb = len(tensors)
        self.free_at = [None] * self.nb
        self.held = [False] * self.nb
        self.i = 0

    def acquire(self, hold=False):
        for _ in range(self.nb):
            idx = self.i % self.nb
            self.i += 1
            if not self.held[idx]:
                break
        else:
            raise RuntimeError("no free psum bank")
        f = self.free_at[idx]
        if f is not None:
            self.rec.wt("tensor", f[0], f[1])
            self.free_at[idx] = None
        if hold:
            self.held[idx] = True
        return idx, self.t[idx]

    def release(self, idx, sem, val):
        self.held[idx] = False
        self.free_at[idx] = (sem, val)


def pack_lhsT(WT, msz=128):
    K, M = WT.shape
    kt = K // P
    G = M // msz
    a = np.asarray(WT, dtype=np.float32).reshape(kt, P, G, msz).transpose(2, 1, 0, 3)
    return np.ascontiguousarray(a.reshape(G, P, kt * msz))


def build_program(n_layers=2, taps=(), dbg_split=False):
    nc = bass.Bass("TRN2", target_bir_lowering=False, debug=False,
                   detect_race_conditions=False)
    dt = nc.dram_tensor

    xres0 = dt("xres0", [D, TL], F32, kind="ExternalInput").ap()
    out = dt("out", [D, TL], F32, kind="ExternalOutput").ap()

    wi = {}
    def win(name, shape):
        wi[name] = dt(name, list(shape), F32, kind="ExternalInput").ap()
        return wi[name]

    win("wkT", (KT, P, KT * P)); win("wvT", (KT, P, KT * P))
    win("wrT", (KT, P, KT * P)); win("woT", (KT, P, KT * P))
    win("fc1T", (8, P, KT * P)); win("fc2T", (KT, P, 8 * P))
    win("w1T", (32, 8, P, 8 * P)); win("w2T", (32, 8, P, 8 * P))
    win("upT", (4, 8, P, KT * P)); win("dnT", (KT, P, 32 * P))
    win("gate0T", (1, P, KT * 32))
    win("td_T", (P, KT)); win("tf_T", (P, KT))
    win("gb_col", (32, 1)); win("gb_rep", (P, 32))
    if n_layers > 1:
        win("wiT", (KT, P, KT * P)); win("wfT", (KT, P, KT * P))
        win("wgT", (KT, P, KT * P)); win("wo1T", (KT, P, KT * P))
        win("g1fc1", (8, KT, P, KT * P)); win("g1fc2", (8, KT, P, KT * P))
        win("gate1T", (1, P, KT * 8))
        win("b1P", (P, KT * 8)); win("b2raw", (8, D))
        win("gnw", (P, KT)); win("alpha_rep", (P, 8))
    win("ones_col", (P, 1)); win("ones_row", (1, P))
    win("eye128", (P, P)); win("eye32", (32, 32))
    win("sel_rep", (P, 8)); win("fmask", (P, 1)); win("epsv", (P, 4))

    xnext0 = dt("xnext0", [D, TL], F32, kind="Internal").ap()
    xnext1 = dt("xnext1", [D, TL], F32, kind="Internal").ap() if n_layers > 1 else None
    sp_w = dt("sp_w", [D, TL], BF16, kind="Internal").ap()
    sp_u = dt("sp_u", [D, TL], BF16, kind="Internal").ap()
    sp_r = dt("sp_r", [D, TL], BF16, kind="Internal").ap()
    sp_xn = dt("sp_xn", [D, TL], BF16, kind="Internal").ap() if n_layers > 1 else None
    scw_d = dt("scw_d", [32, TL], BF16, kind="Internal").ap()
    comb_d = dt("comb_d", [8, TL], BF16, kind="Internal").ap() if n_layers > 1 else None
    dbg = {}
    if dbg_split:
        for nm in ("dbg_at", "dbg_sh", "dbg_ro"):
            dbg[nm] = dt(nm, [D, TL], F32, kind="Internal").ap()
    cc_in = [dt(f"cc_in{l}", [P, 32], F32, kind="Internal").ap()
             for l in range(n_layers)]
    cc_out = [dt(f"cc_out{l}", [8 * P, 32], F32, kind="Internal").ap()
              for l in range(n_layers)]

    tap_out = {}
    for t in taps:
        tap_out[t] = dt("tap_" + t, [D, TL], F32, kind="ExternalOutput").ap()

    _names = {}

    def _uname(n):
        k = _names.get(n, 0)
        _names[n] = k + 1
        return n if k == 0 else f"{n}_{k}"

    rec = Rec()
    ctx = ExitStack()
    with ctx:
        e = ctx.enter_context
        qW = Sem(e(nc.semaphore("qW")), "qW")
        qX = Sem(e(nc.semaphore("qX")), "qX")
        pe = Sem(e(nc.semaphore("pe")), "pe")
        dv = Sem(e(nc.semaphore("dv")), "dv")
        ac = Sem(e(nc.semaphore("ac")), "ac")
        ccs = [Sem(e(nc.semaphore(f"cc{l}")), f"cc{l}") for l in range(n_layers)]
        sX = Sem(e(nc.semaphore("sX")), "sX")
        sW = Sem(e(nc.semaphore("sW")), "sW")

        psum = [e(nc.psum_tensor(f"ps{i}", [P, CW], F32)) for i in range(7)]
        ps16 = e(nc.psum_tensor("ps16", [P, CW], BF16))
        pool = Pool(rec, psum)

        WSLOT = 32 * P
        wring = e(nc.sbuf_tensor(_uname("wring"), [P, 3 * WSLOT], BF16))
        stage = e(nc.sbuf_tensor(_uname("stage"), [P, 4 * CW], F32))
        cons = e(nc.sbuf_tensor(_uname("cons"), [P, 256], F32))
        C_ONES, C_DEC, C_FF, C_TF, C_TD = 0, 1, 17, 33, 49
        C_GNW, C_SEL, C_CN, C_FIN, C_ALPHA, C_FM = 65, 81, 96, 128, 160, 255
        C_EPS = 168
        cons16 = e(nc.sbuf_tensor(_uname("cons16"), [P, 224], BF16))
        B_ONES, B_EYE128, B_EYE32 = 0, 64, 192
        onesrow = e(nc.sbuf_tensor(_uname("onesrow"), [1, P], F32))
        onesrow16 = e(nc.sbuf_tensor(_uname("onesrow16"), [1, P], BF16))
        gbcol = e(nc.sbuf_tensor(_uname("gbcol"), [32, 1], F32))
        gbrep = e(nc.sbuf_tensor(_uname("gbrep"), [P, 32], F32))
        g0sb = e(nc.sbuf_tensor(_uname("g0sb"), [P, KT * 32], BF16))
        rows = e(nc.sbuf_tensor(_uname("rows"), [1, NCH * CW], F32))
        abc = e(nc.sbuf_tensor(_uname("abc"), [P, 2 * CW0], F32))
        bbc = e(nc.sbuf_tensor(_uname("bbc"), [P, 2 * CW0], F32))
        agsb = e(nc.sbuf_tensor(_uname("agsb"), [P, 8 * 32], F32))
        eye32f = e(nc.sbuf_tensor(_uname("eye32f"), [32, 32], F32))
        if n_layers > 1:
            g1sb = e(nc.sbuf_tensor(_uname("g1sb"), [P, KT * 8], BF16))
            b1sb = e(nc.sbuf_tensor(_uname("b1sb"), [P, KT * 8], F32))

        ws_state = {"i": 0, "free": [0, 0, 0]}

        def ws_load(dram_ap, n_elems):
            s = ws_state["i"] % 3
            ws_state["i"] += 1
            rec.wt("gpsimd", pe, ws_state["free"][s])
            dst = wring[:, s * WSLOT: s * WSLOT + n_elems]
            rec.em("gpsimd", lambda g, d=dst, a=dram_ap:
                   g.dma_start(out=d, in_=a).then_inc(qW.h, 16),
                   incs=(("qW", 16),))
            return s, qW.inc(16)

        def ws_tile(slot, kk, msz=128):
            off = slot * WSLOT + kk * msz
            return wring[:, off: off + msz]

        def ws_done(slot, pe_val):
            ws_state["free"][slot] = pe_val

        class WSweep:

            def __init__(self, groups):
                self.groups = groups
                self.h = []
                self.i = 0

            def get(self, g):
                while self.i <= min(g + 2, len(self.groups) - 1):
                    ap_, ne_ = self.groups[self.i]
                    self.h.append(ws_load(ap_, ne_))
                    self.i += 1
                return self.h[g]

        def mm(bank, lhsT, rhs, start, stop, inc=False):
            if inc:
                rec.em("tensor", lambda t, b=bank, l=lhsT, r=rhs, s=start, p=stop:
                       t.matmul(b, l, r, start=s, stop=p).then_inc(pe.h, 1),
                       incs=(("pe", 1),))
                return pe.inc()
            rec.em("tensor", lambda t, b=bank, l=lhsT, r=rhs, s=start, p=stop:
                   t.matmul(b, l, r, start=s, stop=p))
            return None

        def dve(fn, inc=True):
            rec.wt("vector", dv, dv.n)
            rec.em("vector", lambda v, f=fn: f(v).then_inc(dv.h, 1),
                   incs=(("dv", 1),))
            return dv.inc()

        def act(fn, inc=True):
            rec.wt("scalar", ac, ac.n)
            rec.em("scalar", lambda s, f=fn: f(s).then_inc(ac.h, 1),
                   incs=(("ac", 1),))
            return ac.inc()

        def sdma(out_ap, in_ap, **kw):
            rec.em("sync", lambda s, o=out_ap, i=in_ap, k=kw:
                   s.dma_start(out=o, in_=i, **k).then_inc(qX.h, 16),
                   incs=(("qX", 16),))
            return qX.inc(16)

        def gdma(out_ap, in_ap, **kw):
            rec.em("gpsimd", lambda g, o=out_ap, i=in_ap, k=kw:
                   g.dma_start(out=o, in_=i, **k).then_inc(qW.h, 16),
                   incs=(("qW", 16),))
            return qW.inc(16)

        _sig = {"sync": (qX, sX, []), "gpsimd": (qW, sW, [])}

        def dsig(queue, upto):
            qsem, ssem, lst = _sig[queue]
            for cov, val in lst:
                if cov >= upto:
                    return ssem, val
            rec.wt(queue, qsem, upto)
            rec.em(queue, lambda e, h=ssem.h: e.nop().then_inc(h, 1),
                   incs=((ssem.name, 1),))
            v = ssem.inc()
            lst.append((upto, v))
            return ssem, v

        def wtd(eng, queue, upto):
            if upto and upto > 0:
                s_, v_ = dsig(queue, upto)
                rec.wt(eng, s_, v_)

        def wt_any(eng, s_, v_):
            if s_ == "dma":
                wtd(eng, "gpsimd", v_)
            elif s_ == "dmas":
                wtd(eng, "sync", v_)
            else:
                rec.wt(eng, s_, v_)

        stage_free = [0, 0, 0, 0]
        stage_i = [0]

        def stage_acc(ps_ap, pe_val, dst_dram_ap, accum=True):
            s = stage_i[0] % 4
            stage_i[0] += 1
            wtd("scalar", "gpsimd", stage_free[s])
            rec.wt("scalar", pe, pe_val)
            slot = stage[:, s * CW:(s + 1) * CW]
            v = act(lambda sc, o=slot, p=ps_ap: sc.copy(out=o, in_=p))
            rec.wt("gpsimd", ac, v)
            kw = dict(accum_op=ALU.add) if accum else {}
            q = gdma(dst_dram_ap, slot, **kw)
            stage_free[s] = q
            return v

        def xdst_tile(xdst, g, n, w=CW):
            return bass.AP(xdst.tensor, g * P * TL + n * w, [[TL, P], [1, w]])

        def qfence():
            rec.wt("gpsimd", qW, qW.n)
            rec.wt("sync", pe, pe.n)
            rec.wt("sync", dv, dv.n)
            rec.wt("sync", ac, ac.n)
            rec.wt("gpsimd", pe, pe.n)
            rec.wt("gpsimd", dv, dv.n)
            rec.wt("gpsimd", ac, ac.n)

        sdma(cons[:, C_ONES:C_ONES + 1], wi["ones_col"])
        sdma(cons[:, C_TD:C_TD + KT], wi["td_T"])
        sdma(cons[:, C_TF:C_TF + KT], wi["tf_T"])
        sdma(cons[:, C_SEL:C_SEL + 8], wi["sel_rep"])
        sdma(onesrow[0:1, :], wi["ones_row"])
        sdma(gbcol[0:32, :], wi["gb_col"])
        sdma(gbrep[:, :], wi["gb_rep"])
        sdma(cons[:, C_FM:C_FM + 1], wi["fmask"])
        sdma(cons[:, C_EPS:C_EPS + 4], wi["epsv"])
        sdma(eye32f[0:32, :], wi["eye32"])
        gdma(cons16[:, B_ONES:B_ONES + 1], wi["ones_col"])
        gdma(cons16[:, B_EYE128:B_EYE128 + P], wi["eye128"])
        gdma(cons16[0:32, B_EYE32:B_EYE32 + 32], wi["eye32"])
        gdma(onesrow16[0:1, :], wi["ones_row"])
        g0_q = gdma(g0sb[:, :], wi["gate0T"][0])
        if n_layers > 1:
            sdma(cons[:, C_GNW:C_GNW + KT], wi["gnw"])
            sdma(cons[:, C_ALPHA:C_ALPHA + 8], wi["alpha_rep"])
            sdma(b1sb[:, :], wi["b1P"])
            g1_q = gdma(g1sb[:, :], wi["gate1T"][0])
        cst_q, cst_w = qX.n, qW.n
        wtd("scalar", "sync", cst_q)
        act(lambda s: s.activation(out=cons[:, C_DEC:C_DEC + KT],
                                   in_=cons[:, C_TD:C_TD + KT], func=AF.Exp),
            inc=False)
        dec_ready = act(lambda s: s.activation(out=cons[:, C_DEC:C_DEC + KT],
                                               in_=cons[:, C_DEC:C_DEC + KT],
                                               func=AF.Exp, scale=-1.0))
        wtd("vector", "sync", cst_q)
        ffm = dve(lambda v: v.tensor_scalar(out=cons[:, C_FF:C_FF + KT],
                                            in0=cons[:, C_TF:C_TF + KT],
                                            scalar1=cons[:, C_FM:C_FM + 1],
                                            scalar2=None, op0=ALU.mult))
        rec.wt("scalar", dv, ffm)
        ff_ready = act(lambda s: s.activation(out=cons[:, C_FF:C_FF + KT],
                                              in_=cons[:, C_FF:C_FF + KT],
                                              func=AF.Exp))
        gdma(xnext0, xres0)

        finals = cons[:, C_FIN:C_FIN + 32]
        carry = cons[:, C_CN:C_CN + 32]

        def proj_sweep(wgroups, kt, rhs_fn, evac_fn, m_list=None, ready=None):
            for g in (m_list if m_list is not None else range(len_w(wgroups))):
                sl, lv = ws_load(wgroups(g) if callable(wgroups) else wgroups[g],
                                 kt * P)
                wtd("tensor", "gpsimd", lv)
                if ready is not None:
                    rec.wt("tensor", ready[0], ready[1])
                for n in range(NCH):
                    bi, bk = pool.acquire()
                    for kk in range(kt):
                        mm(bk[:, :], ws_tile(sl, kk), rhs_fn(kk, n),
                           kk == 0, kk == kt - 1, inc=(kk == kt - 1))
                    sem, val = evac_fn(g, n, bk, pe.n)
                    pool.release(bi, sem, val)
                ws_done(sl, pe.n)

        def len_w(x):
            raise RuntimeError

        def emit_layer(l):
            xsrc = xres0 if l == 0 else xnext0
            xdst = xnext0 if l == 0 else xnext1
            cin, cout, cc = cc_in[l], cc_out[l], ccs[l]

            lay = ExitStack()
            with lay:
                le = lay.enter_context
                if l == 0:
                    xlat = le(nc.sbuf_tensor(_uname("xlat"), [P, 8 * TL], BF16))
                else:
                    logits_sb = le(nc.sbuf_tensor(_uname("logits1"), [8, TL], F32))
                    combT = le(nc.sbuf_tensor(_uname("combT"), [8, TL], BF16))

                xn_ctx = ExitStack()
                with xn_ctx:
                    xn = xn_ctx.enter_context(
                        nc.sbuf_tensor(_uname("xn"), [P, KT * TL], BF16))

                    def xn_t(kk, n, w=CW):
                        return xn[:, kk * TL + n * w: kk * TL + n * w + w]

                    p0 = ExitStack()
                    with p0:
                        pe0 = p0.enter_context
                        xch = [pe0(nc.sbuf_tensor(_uname(f"xch{i}"),
                                                  [P, KT * CW0], F32))
                               for i in range(2)]
                        xb16 = pe0(nc.sbuf_tensor(_uname("xb16"), [P, KT * CW0],
                                                  BF16))
                        xs16 = pe0(nc.sbuf_tensor(_uname("xs16"), [P, KT * CW0],
                                                  BF16))
                        if l > 0:
                            qfence()
                            wtd("sync", "gpsimd", qW.n)
                        xn_wr = [0, 0]
                        sq_prev = 0
                        xb_prev = 0
                        stats_prev = 0
                        for n in range(NC0):
                            j = n % 2
                            xc = xch[j]
                            src = bass.AP(xsrc.tensor, n * CW0,
                                          [[TL, P], [P * TL, KT], [1, CW0]])
                            rec.wt("sync", dv, xn_wr[j])
                            rec.wt("sync", ac, sq_prev)
                            lq = sdma(xc[:, :], src)
                            wtd("scalar", "sync", lq)
                            rec.wt("scalar", pe, stats_prev)
                            sq_prev = act(lambda s, x=xc: s.activation(
                                out=xs16[:, :], in_=x[:, :], func=AF.Square))
                            xb_prev = act(lambda s, x=xc: s.copy(out=xb16[:, :],
                                                                 in_=x[:, :]))
                            i1, b1 = pool.acquire()
                            i2, b2 = pool.acquire()
                            rec.wt("tensor", ac, xb_prev)
                            wtd("tensor", "gpsimd", cst_w)
                            for kk in range(KT):
                                mm(b1[0:1, 0:CW0], cons16[:, B_ONES:B_ONES + 1],
                                   xb16[:, kk * CW0:(kk + 1) * CW0],
                                   kk == 0, kk == KT - 1)
                            for kk in range(KT):
                                mm(b2[0:1, 0:CW0], cons16[:, B_ONES:B_ONES + 1],
                                   xs16[:, kk * CW0:(kk + 1) * CW0],
                                   kk == 0, kk == KT - 1,
                                   inc=(kk == KT - 1))
                            stats_prev = pe.n
                            mrow = rows[0:1, 0:CW0]
                            vrow = rows[0:1, CW0:2 * CW0]
                            rrow = rows[0:1, 2 * CW0:3 * CW0]
                            brow = rows[0:1, 3 * CW0:4 * CW0]
                            rec.wt("vector", pe, stats_prev)
                            dve(lambda v, b=b1: v.tensor_scalar(
                                out=mrow, in0=b[0:1, 0:CW0], scalar1=1.0 / D,
                                scalar2=None, op0=ALU.mult), inc=False)
                            dve(lambda v, b=b2: v.tensor_scalar(
                                out=vrow, in0=b[0:1, 0:CW0], scalar1=1.0 / D,
                                scalar2=None, op0=ALU.mult), inc=False)
                            dve(lambda v: v.tensor_tensor(out=brow, in0=mrow,
                                                          in1=mrow,
                                                          op=ALU.mult), inc=False)
                            vv = dve(lambda v: v.tensor_tensor(out=vrow, in0=vrow,
                                                               in1=brow,
                                                               op=ALU.subtract))
                            pool.release(i1, dv, vv)
                            pool.release(i2, dv, vv)
                            rec.wt("scalar", dv, vv)
                            act(lambda s: s.activation(
                                out=rrow, in_=vrow, func=AF.Ln,
                                bias=cons[0:1, C_EPS:C_EPS + 1]), inc=False)
                            rw = act(lambda s: s.activation(out=rrow, in_=rrow,
                                                            func=AF.Exp, scale=-0.5))
                            rec.wt("vector", ac, rw)
                            bw = dve(lambda v: v.tensor_tensor(out=brow, in0=mrow,
                                                               in1=rrow,
                                                               op=ALU.mult))
                            i3, b3 = pool.acquire()
                            i4, b4 = pool.acquire()
                            rec.wt("tensor", ac, rw)
                            av = mm(b3[:, 0:CW0], onesrow[0:1, :], rrow, True, True,
                                    inc=True)
                            rec.wt("tensor", dv, bw)
                            bv = mm(b4[:, 0:CW0], onesrow[0:1, :], brow, True, True,
                                    inc=True)
                            abj = abc[:, (j) * CW0:(j + 1) * CW0]
                            bbj = bbc[:, (j) * CW0:(j + 1) * CW0]
                            rec.wt("scalar", pe, av)
                            rec.wt("scalar", dv, xn_wr[j])
                            a_c = act(lambda s, b=b3, o=abj: s.copy(
                                out=o, in_=b[:, 0:CW0]))
                            rec.wt("scalar", pe, bv)
                            b_c = act(lambda s, b=b4, o=bbj: s.copy(
                                out=o, in_=b[:, 0:CW0]))
                            pool.release(i3, ac, a_c)
                            pool.release(i4, ac, b_c)
                            rec.wt("vector", ac, b_c)
                            for kk in range(KT):
                                dve(lambda v, k=kk, x=xc, o=bbj: v.tensor_tensor(
                                    out=x[:, k * CW0:(k + 1) * CW0],
                                    in0=x[:, k * CW0:(k + 1) * CW0],
                                    in1=o, op=ALU.subtract), inc=False)
                                xn_wr[j] = dve(lambda v, k=kk, nn=n, x=xc, o=abj:
                                               v.tensor_tensor(
                                                   out=xn_t(k, nn, CW0),
                                                   in0=x[:, k * CW0:(k + 1) * CW0],
                                                   in1=o, op=ALU.mult))
                        xn_done = max(xn_wr)
                    if l == 0 and "xn" in tap_out:
                        rec.wt("gpsimd", dv, xn_done)
                        gdma(tap_out["xn"], xn[:, :])
                    if l > 0:
                        rec.wt("sync", dv, xn_done)
                        xn_dump_q = sdma(
                            bass.AP(sp_xn.tensor, 0,
                                    [[TL, P], [P * TL, KT], [1, TL]]),
                            xn[:, :])
                        wtd("sync", "gpsimd", state["l0_acc_q"])
                        x1cp = [sdma(bass.AP(xnext1.tensor,
                                             q_ * (D // 4) * TL,
                                             [[TL, D // 4], [1, TL]]),
                                     bass.AP(xnext0.tensor,
                                             q_ * (D // 4) * TL,
                                             [[TL, D // 4], [1, TL]]))
                                for q_ in range(4)]
                        state["x1cp_q"] = x1cp[-1]

                    p1 = ExitStack()
                    with p1:
                        pe1 = p1.enter_context
                        if l == 0:
                            wm = pe1(nc.sbuf_tensor(_uname("wm"), [P, TL], F32))
                            wex = [pe1(nc.sbuf_tensor(_uname(f"wex{i}"), [P, TL], BF16))
                                   for i in range(2)]
                            um = [pe1(nc.sbuf_tensor(_uname(f"um{i}"), [P, TL], BF16))
                                  for i in range(2)]
                            srm = pe1(nc.sbuf_tensor(_uname("srm"), [P, TL], BF16))
                            scr = pe1(nc.sbuf_tensor(_uname("scr"), [P, TL], F32))
                            sw1 = WSweep(
                                [(wi[w][m], KT * P) for m in range(KT)
                                 for w in ("wkT", "wvT", "wrT")])
                            wex_free = [[], []]
                            um_free = [[], []]
                            srm_free = []
                            wm_free = []
                            for m in range(KT):
                                j = m % 2
                                dcol = cons[:, C_DEC + m:C_DEC + m + 1]
                                sl, lv = sw1.get(3 * m)
                                wtd("tensor", "gpsimd", lv)
                                if m == 0:
                                    rec.wt("tensor", dv, xn_done)
                                banks = []
                                for n in range(NCH):
                                    bi, bk = pool.acquire()
                                    for kk in range(KT):
                                        mm(bk[:, :], ws_tile(sl, kk), xn_t(kk, n),
                                           kk == 0, kk == KT - 1, inc=(kk == KT - 1))
                                    banks.append((bi, bk, pe.n))
                                ws_done(sl, pe.n)
                                for s_, v_ in wm_free:
                                    wt_any("vector", s_, v_)
                                wm_free = []
                                for n, (bi, bk, pv) in enumerate(banks):
                                    rec.wt("vector", pe, pv)
                                    cv = dve(lambda v, b=bk, n_=n: v.tensor_scalar(
                                        out=wm[:, n_ * CW:(n_ + 1) * CW], in0=b[:, :],
                                        scalar1=-10.0, scalar2=10.0,
                                        op0=ALU.max, op1=ALU.min))
                                    pool.release(bi, dv, cv)
                                clip_v = dv.n
                                wb = wex[j]
                                for s_, v_ in wex_free[j]:
                                    wt_any("scalar", s_, v_)
                                wex_free[j] = []
                                rec.wt("scalar", dv, clip_v)
                                rec.wt("scalar", ac, ff_ready)
                                ev = act(lambda s, w=wb: s.activation(
                                    out=w[:, :], in_=wm[:, :], func=AF.Exp))
                                wm_free = [(ac, ev)]
                                rec.wt("vector", ac, ev)
                                fv = dve(lambda v, w=wb, m_=m: v.tensor_tensor(
                                    out=w[:, 0:1], in0=w[:, 0:1],
                                    in1=cons[:, C_FF + m_:C_FF + m_ + 1],
                                    op=ALU.mult))
                                rec.wt("sync", dv, fv)
                                wq = sdma(sp_w[m * P:(m + 1) * P, :], wb[:, :])
                                d_bc = bass.broadcast_tensor_aps(wb[:, :], dcol)[1]
                                rec.wt("vector", ac, dec_ready)
                                dve(lambda v, w=wb, db=d_bc: v.tensor_tensor_scan(
                                    out=scr[:, :], data0=db, data1=w[:, :],
                                    initial=0.0, op0=ALU.mult, op1=ALU.add),
                                    inc=False)
                                dsv = dve(lambda v, m_=m: v.tensor_copy(
                                    out=finals[:, 16 + m_:17 + m_],
                                    in_=scr[:, TL - 1:TL]))
                                sl2, lv2 = sw1.get(3 * m + 1)
                                wtd("tensor", "gpsimd", lv2)
                                banks = []
                                for n in range(NCH):
                                    bi, bk = pool.acquire()
                                    for kk in range(KT):
                                        mm(bk[:, :], ws_tile(sl2, kk), xn_t(kk, n),
                                           kk == 0, kk == KT - 1, inc=(kk == KT - 1))
                                    banks.append((bi, bk, pe.n))
                                ws_done(sl2, pe.n)
                                ub = um[j]
                                for s_, v_ in um_free[j]:
                                    wt_any("vector", s_, v_)
                                um_free[j] = []
                                for n, (bi, bk, pv) in enumerate(banks):
                                    rec.wt("vector", pe, pv)
                                    uv = dve(lambda v, b=bk, n_=n, w=wb, u=ub:
                                             v.tensor_tensor(
                                                 out=u[:, n_ * CW:(n_ + 1) * CW],
                                                 in0=w[:, n_ * CW:(n_ + 1) * CW],
                                                 in1=b[:, :], op=ALU.mult))
                                    pool.release(bi, dv, uv)
                                u_v = dv.n
                                rec.wt("sync", dv, u_v)
                                uq = sdma(sp_u[m * P:(m + 1) * P, :], ub[:, :])
                                wex_free[j] = [("dmas", wq), (dv, u_v)]
                                dve(lambda v, u=ub, db=d_bc: v.tensor_tensor_scan(
                                    out=scr[:, :], data0=db, data1=u[:, :],
                                    initial=0.0, op0=ALU.mult, op1=ALU.add),
                                    inc=False)
                                nsv = dve(lambda v, m_=m: v.tensor_copy(
                                    out=finals[:, m_:m_ + 1], in_=scr[:, TL - 1:TL]))
                                um_free[j] = [("dmas", uq), (dv, nsv)]
                                sl3, lv3 = sw1.get(3 * m + 2)
                                wtd("tensor", "gpsimd", lv3)
                                for s_, v_ in srm_free:
                                    wt_any("scalar", s_, v_)
                                srm_free = []
                                for n in range(NCH):
                                    bi, bk = pool.acquire()
                                    for kk in range(KT):
                                        mm(bk[:, :], ws_tile(sl3, kk), xn_t(kk, n),
                                           kk == 0, kk == KT - 1, inc=(kk == KT - 1))
                                    rv = act(lambda s, b=bk, n_=n, pv=pe.n:
                                             (s.wait_ge(pe.h, pv),
                                              s.activation(
                                                  out=srm[:, n_ * CW:(n_ + 1) * CW],
                                                  in_=b[:, :], func=AF.Sigmoid))[1]
                                             ) if False else None
                                    rec.wt("scalar", pe, pe.n)
                                    rv = act(lambda s, b=bk, n_=n: s.activation(
                                        out=srm[:, n_ * CW:(n_ + 1) * CW],
                                        in_=b[:, :], func=AF.Sigmoid))
                                    pool.release(bi, ac, rv)
                                ws_done(sl3, pe.n)
                                rec.wt("sync", ac, ac.n)
                                rq = sdma(sp_r[m * P:(m + 1) * P, :], srm[:, :])
                                srm_free = [("dmas", rq)]
                            scan1_done = dv.n
                        else:
                            am = [pe1(nc.sbuf_tensor(_uname(f"am{i}"), [P, TL], BF16))
                                  for i in range(2)]
                            sgm = pe1(nc.sbuf_tensor(_uname("sgm"), [P, TL], BF16))
                            iim = [pe1(nc.sbuf_tensor(_uname(f"iim{i}"), [P, TL], BF16))
                                   for i in range(2)]
                            scr = pe1(nc.sbuf_tensor(_uname("scr1"), [P, TL], F32))
                            sw1b = WSweep(
                                [(wi[w][m], KT * P) for m in range(KT)
                                 for w in ("wfT", "wiT", "wgT")])
                            am_free = [[], []]
                            iim_free = [[], []]
                            sgm_free = []
                            for m in range(KT):
                                j = m % 2
                                sl, lv = sw1b.get(3 * m)
                                wtd("tensor", "gpsimd", lv)
                                if m == 0:
                                    rec.wt("tensor", dv, xn_done)
                                ab = am[j]
                                for s_, v_ in am_free[j]:
                                    wt_any("scalar", s_, v_)
                                am_free[j] = []
                                for s_, v_ in sgm_free:
                                    wt_any("scalar", s_, v_)
                                sgm_free = []
                                for n in range(NCH):
                                    bi, bk = pool.acquire()
                                    for kk in range(KT):
                                        mm(bk[:, :], ws_tile(sl, kk), xn_t(kk, n),
                                           kk == 0, kk == KT - 1, inc=(kk == KT - 1))
                                    rec.wt("scalar", pe, pe.n)
                                    act(lambda s, b=bk, n_=n, a_=ab: s.activation(
                                        out=a_[:, n_ * CW:(n_ + 1) * CW],
                                        in_=b[:, :], func=AF.Sigmoid), inc=False)
                                    gv = act(lambda s, b=bk, n_=n: s.activation(
                                        out=sgm[:, n_ * CW:(n_ + 1) * CW],
                                        in_=b[:, :], func=AF.Sigmoid, scale=-1.0))
                                    pool.release(bi, ac, gv)
                                a_done = ac.n
                                ws_done(sl, pe.n)
                                rec.wt("sync", ac, a_done)
                                aq = sdma(sp_w[m * P:(m + 1) * P, :], ab[:, :])
                                sl2, lv2 = sw1b.get(3 * m + 1)
                                wtd("tensor", "gpsimd", lv2)
                                ib = iim[j]
                                for s_, v_ in iim_free[j]:
                                    wt_any("scalar", s_, v_)
                                iim_free[j] = []
                                for n in range(NCH):
                                    bi, bk = pool.acquire()
                                    for kk in range(KT):
                                        mm(bk[:, :], ws_tile(sl2, kk), xn_t(kk, n),
                                           kk == 0, kk == KT - 1, inc=(kk == KT - 1))
                                    rec.wt("scalar", pe, pe.n)
                                    sv_ = act(lambda s, b=bk, n_=n, i_=ib:
                                              s.activation(
                                                  out=i_[:, n_ * CW:(n_ + 1) * CW],
                                                  in_=b[:, :], func=AF.Silu))
                                    pool.release(bi, ac, sv_)
                                ws_done(sl2, pe.n)
                                rec.wt("vector", ac, ac.n)
                                iv = dve(lambda v, i_=ib: v.tensor_tensor(
                                    out=i_[:, :], in0=i_[:, :], in1=sgm[:, :],
                                    op=ALU.mult))
                                rec.wt("sync", dv, iv)
                                iq = sdma(sp_u[m * P:(m + 1) * P, :], ib[:, :])
                                dve(lambda v, a_=ab, i_=ib: v.tensor_tensor_scan(
                                    out=scr[:, :], data0=a_[:, :], data1=i_[:, :],
                                    initial=0.0, op0=ALU.mult, op1=ALU.add),
                                    inc=False)
                                fsv = dve(lambda v, m_=m: v.tensor_copy(
                                    out=finals[:, m_:m_ + 1], in_=scr[:, TL - 1:TL]))
                                am_free[j] = [("dmas", aq), (dv, fsv)]
                                iim_free[j] = [("dmas", iq), (dv, fsv)]
                                sl3, lv3 = sw1b.get(3 * m + 2)
                                wtd("tensor", "gpsimd", lv3)
                                rec.wt("scalar", dv, iv)
                                for n in range(NCH):
                                    bi, bk = pool.acquire()
                                    for kk in range(KT):
                                        mm(bk[:, :], ws_tile(sl3, kk), xn_t(kk, n),
                                           kk == 0, kk == KT - 1, inc=(kk == KT - 1))
                                    rec.wt("scalar", pe, pe.n)
                                    gv2 = act(lambda s, b=bk, n_=n: s.activation(
                                        out=sgm[:, n_ * CW:(n_ + 1) * CW],
                                        in_=b[:, :], func=AF.Silu))
                                    pool.release(bi, ac, gv2)
                                ws_done(sl3, pe.n)
                                rec.wt("sync", ac, ac.n)
                                gq = sdma(sp_r[m * P:(m + 1) * P, :], sgm[:, :])
                                sgm_free = [("dmas", gq)]
                            scan1_done = dv.n
                            wtd("tensor", "gpsimd", g1_q)
                            for n in range(NCH):
                                bi, bk = pool.acquire()
                                for kk in range(KT):
                                    mm(bk[0:8, :], g1sb[:, kk * 8:(kk + 1) * 8],
                                       xn_t(kk, n), kk == 0, kk == KT - 1,
                                       inc=(kk == KT - 1))
                                rec.wt("scalar", pe, pe.n)
                                lv_ = act(lambda s, b=bk, n_=n: s.copy(
                                    out=logits_sb[0:8, n_ * CW:(n_ + 1) * CW],
                                    in_=b[0:8, :]))
                                pool.release(bi, ac, lv_)
                            logits_done = ac.n
                            emit_l1_routing(logits_sb, logits_done, combT)

                    rec.wt("gpsimd", dv, scan1_done)
                    gdma(cin, finals)
                    rec.wt("gpsimd", qW, qW.n)
                    rec.em("gpsimd", lambda g, i=cin, o=cout, c=cc.h:
                           g.collective_compute(
                               "AllGather", ALU.bypass,
                               replica_groups=[list(range(8))],
                               ins=[i.opt()], outs=[o.opt()]).then_inc(c),
                           incs=((cc.name, 1),))
                    cc.inc()

                    def consume_carry():
                        rec.wt("gpsimd", cc, 1)
                        ag_ap = bass.AP(cout.tensor, 0,
                                        [[32, P], [P * 32, 8], [1, 32]])
                        agq = gdma(agsb[:, :], ag_ap)
                        wtd("vector", "gpsimd", agq)
                        sel3 = cons[:, C_SEL:C_SEL + 8].rearrange(
                            "p (r o) -> p r o", o=1)
                        ag3 = agsb[:, :].rearrange("p (r c) -> p r c", r=8)
                        sel_b = bass.broadcast_tensor_aps(ag3, sel3)[1]
                        dve(lambda v: v.tensor_tensor(out=ag3, in0=ag3, in1=sel_b,
                                                      op=ALU.mult), inc=False)
                        red_in = bass.AP(agsb[:, :].tensor, agsb[:, :].offset,
                                         [agsb[:, :].ap[0], [1, 32], [32, 8]])
                        carry3 = carry.rearrange("p (c o) -> p c o", o=1)
                        return dve(lambda v: v.tensor_reduce(
                            out=carry3, in_=red_in, axis=AXX, op=ALU.add))

                    if l == 0:
                        emit_l0_p3a(xn_t, xn_done, xlat)
                        xn_last_pe = pe.n
                    else:
                        xn_last_pe = pe.n
                if l == 0:
                    carry_v = consume_carry()
                    emit_l0_p3b(xlat, xdst, carry_v, xn_last_pe)
                    emit_l0_p4(xlat, xdst)
                    state["l0_acc_q"] = qW.n
                    if "x1" in tap_out:
                        rec.wt("gpsimd", qW, qW.n)
                        gdma(tap_out["x1"], xnext0)
                else:
                    emit_l1_p3(logits_sb, logits_done, combT, xdst, consume_carry,
                               xn_last_pe)
                    emit_l1_p4(combT, xdst, xn_dump_q)
            return None

        def emit_l0_p3a(xn_t, xn_done, xlat):
            p3 = ExitStack()
            with p3:
                pe3 = p3.enter_context
                scores = pe3(nc.sbuf_tensor(_uname("scores"), [32, TL], F32))
                sbbf = scores
                tm = pe3(nc.sbuf_tensor(_uname("tm"), [P, KT * 32], F32))
                tms = pe3(nc.sbuf_tensor(_uname("tms"), [P, KT * 32], F32))
                sel8 = pe3(nc.sbuf_tensor(_uname("sel8"), [P, KT * 32], F32))
                t4 = pe3(nc.sbuf_tensor(_uname("t4"), [P, KT * 32], F32))
                red1 = pe3(nc.sbuf_tensor(_uname("red1"), [P, KT * 4], F32))
                red2 = pe3(nc.sbuf_tensor(_uname("red2"), [P, KT * 4], F32))
                scw16 = pe3(nc.sbuf_tensor(_uname("scw16"), [P, KT * 32], BF16))
                scwT = pe3(nc.sbuf_tensor(_uname("scwT"), [32, TL], BF16))
                up16 = pe3(nc.sbuf_tensor(_uname("up16"), [P, 8 * TL], BF16))
                relu_t = pe3(nc.sbuf_tensor(_uname("relu_t"), [P, 2 * CW], F32))

                wtd("tensor", "gpsimd", g0_q)
                rec.wt("tensor", dv, xn_done)
                for n in range(NCH):
                    bi, bk = pool.acquire()
                    for kk in range(KT):
                        mm(bk[0:32, :], g0sb[:, kk * 32:(kk + 1) * 32],
                           xn_t(kk, n), kk == 0, kk == KT - 1, inc=(kk == KT - 1))
                    rec.wt("scalar", pe, pe.n)
                    sv_ = act(lambda s, b=bk, n_=n: s.activation(
                        out=scores[0:32, n_ * CW:(n_ + 1) * CW],
                        in_=b[0:32, :], func=AF.Sigmoid))
                    pool.release(bi, ac, sv_)
                rec.wt("vector", ac, ac.n)
                wtd("vector", "sync", cst_q)
                sb16v = dve(lambda v: v.tensor_scalar(
                    out=sbbf[0:32, :], in0=scores[0:32, :],
                    scalar1=gbcol[0:32, 0:1], scalar2=None, op0=ALU.add))
                rec.wt("gpsimd", qW, cst_w)
                prev_copy = 0
                for rnd in range(4):
                    bi_t, bk_t = pool.acquire()
                    rec.wt("tensor", dv, sb16v)
                    rec.wt("tensor", ac, prev_copy)
                    for c in range(4):
                        cg = rnd * 4 + c
                        inc = c == 3
                        rec.em("tensor", lambda t, c_=c, cg_=cg, b=bk_t, i_=inc:
                               (t.transpose(b[:, c_ * 32:(c_ + 1) * 32],
                                            sbbf[0:32, cg_ * P:(cg_ + 1) * P],
                                            eye32f[0:32, 0:32]).then_inc(pe.h, 1)
                                if i_ else
                                t.transpose(b[:, c_ * 32:(c_ + 1) * 32],
                                            sbbf[0:32, cg_ * P:(cg_ + 1) * P],
                                            eye32f[0:32, 0:32])),
                               incs=((("pe", 1),) if inc else ()))
                    tv = pe.inc()
                    rec.wt("scalar", pe, tv)
                    prev_copy = act(lambda s, r=rnd, b=bk_t: s.copy(
                        out=tm[:, r * 128:(r + 1) * 128], in_=b[:, 0:128]))
                    pool.release(bi_t, ac, prev_copy)
                rec.wt("vector", ac, prev_copy)
                tm3 = tm[:, :].rearrange("p (c e) -> p c e", e=32)
                gb3 = gbrep[:, :].rearrange("p (o e) -> p o e", o=1)
                gb_b = bass.broadcast_tensor_aps(tm3, gb3)[1]
                tms3 = tms[:, :].rearrange("p (c e) -> p c e", e=32)
                dve(lambda v: v.tensor_tensor(out=tms3, in0=tm3, in1=gb_b,
                                              op=ALU.subtract), inc=False)
                tm4 = tm[:, :].rearrange("p (c g e) -> p c g e", g=4, e=8)
                r14 = red1[:, :].rearrange("p (c g o) -> p c g o", g=4, o=1)
                dve(lambda v: v.tensor_reduce(out=r14, in_=tm4, axis=AXX,
                                              op=ALU.max), inc=False)
                r1b = bass.broadcast_tensor_aps(tm4, r14)[1]
                t44 = t4[:, :].rearrange("p (c g e) -> p c g e", g=4, e=8)
                dve(lambda v: v.tensor_tensor(out=t44, in0=tm4, in1=r1b,
                                              op=ALU.is_equal), inc=False)
                dve(lambda v: v.scalar_tensor_tensor(out=t44, in0=t44, scalar=-BIG,
                                                     in1=tm4, op0=ALU.mult,
                                                     op1=ALU.add), inc=False)
                r24 = red2[:, :].rearrange("p (c g o) -> p c g o", g=4, o=1)
                dve(lambda v: v.tensor_reduce(out=r24, in_=t44, axis=AXX,
                                              op=ALU.max), inc=False)
                dve(lambda v: v.tensor_tensor(out=red1[:, :], in0=red1[:, :],
                                              in1=red2[:, :], op=ALU.add), inc=False)
                gs3 = red1[:, :].rearrange("p (c g) -> p c g", g=4)
                g14 = red2[:, 0:KT].rearrange("p (c o) -> p c o", o=1)
                dve(lambda v: v.tensor_reduce(out=g14, in_=gs3, axis=AXX,
                                              op=ALU.max), inc=False)
                g1b = bass.broadcast_tensor_aps(gs3, g14)[1]
                e1g = t4[:, 0:64].rearrange("p (c g) -> p c g", g=4)
                dve(lambda v: v.tensor_tensor(out=e1g, in0=gs3, in1=g1b,
                                              op=ALU.is_equal), inc=False)
                dve(lambda v: v.scalar_tensor_tensor(out=gs3, in0=e1g, scalar=-BIG,
                                                     in1=gs3, op0=ALU.mult,
                                                     op1=ALU.add), inc=False)
                g24 = red2[:, KT:2 * KT].rearrange("p (c o) -> p c o", o=1)
                dve(lambda v: v.tensor_reduce(out=g24, in_=gs3, axis=AXX,
                                              op=ALU.max), inc=False)
                g2b = bass.broadcast_tensor_aps(gs3, g24)[1]
                dve(lambda v: v.tensor_tensor(out=gs3, in0=gs3, in1=g2b,
                                              op=ALU.is_equal), inc=False)
                dve(lambda v: v.tensor_tensor(out=gs3, in0=gs3, in1=e1g,
                                              op=ALU.add), inc=False)
                gm4 = red1[:, :].rearrange("p (c g o) -> p c g o", g=4, o=1)
                gm_b = bass.broadcast_tensor_aps(tm4, gm4)[1]
                dve(lambda v: v.tensor_scalar(out=t44, in0=gm_b, scalar1=BIG,
                                              scalar2=-BIG, op0=ALU.mult,
                                              op1=ALU.add), inc=False)
                dve(lambda v: v.tensor_tensor(out=t44, in0=t44, in1=tm4,
                                              op=ALU.add), inc=False)
                tmm = t4[:, :].rearrange("p (c e) -> p c e", e=32)
                mi = red2[:, 0:KT].rearrange("p (c o) -> p c o", o=1)
                sel3_ = sel8[:, :].rearrange("p (c e) -> p c e", e=32)
                eq3 = tm[:, :].rearrange("p (c e) -> p c e", e=32)
                for it in range(8):
                    dve(lambda v: v.tensor_reduce(out=mi, in_=tmm, axis=AXX,
                                                  op=ALU.max), inc=False)
                    mib = bass.broadcast_tensor_aps(tmm, mi)[1]
                    if it == 0:
                        dve(lambda v: v.tensor_tensor(out=sel3_, in0=tmm, in1=mib,
                                                      op=ALU.is_equal), inc=False)
                        dve(lambda v: v.scalar_tensor_tensor(
                            out=tmm, in0=sel3_, scalar=-BIG, in1=tmm,
                            op0=ALU.mult, op1=ALU.add), inc=False)
                    else:
                        dve(lambda v: v.tensor_tensor(out=eq3, in0=tmm, in1=mib,
                                                      op=ALU.is_equal), inc=False)
                        dve(lambda v: v.tensor_tensor(out=sel3_, in0=sel3_, in1=eq3,
                                                      op=ALU.add), inc=False)
                        dve(lambda v: v.scalar_tensor_tensor(
                            out=tmm, in0=eq3, scalar=-BIG, in1=tmm,
                            op0=ALU.mult, op1=ALU.add), inc=False)
                dve(lambda v: v.tensor_tensor(out=tms3, in0=tms3, in1=sel3_,
                                              op=ALU.mult), inc=False)
                ssum = red2[:, 0:KT].rearrange("p (c o) -> p c o", o=1)
                dve(lambda v: v.tensor_reduce(out=ssum, in_=tms3, axis=AXX,
                                              op=ALU.add), inc=False)
                dve(lambda v: v.reciprocal(out=red2[:, 0:KT], in_=red2[:, 0:KT]),
                    inc=False)
                sb_b = bass.broadcast_tensor_aps(tms3, ssum)[1]
                dve(lambda v: v.tensor_tensor(out=tms3, in0=tms3, in1=sb_b,
                                              op=ALU.mult), inc=False)
                cwv = dve(lambda v: v.tensor_scalar(out=tms[:, :], in0=tms[:, :],
                                                    scalar1=2.5, scalar2=None,
                                                    op0=ALU.mult))
                rec.wt("scalar", dv, cwv)
                act(lambda s: s.activation(out=tms[:, :], in_=tms[:, :], func=AF.Ln,
                                           bias=cons[:, C_EPS + 2:C_EPS + 3]), inc=False)
                scv = act(lambda s: s.activation(out=tms[:, :], in_=tms[:, :],
                                                 func=AF.Exp, scale=0.5))
                rec.wt("vector", ac, scv)
                s16v = dve(lambda v: v.tensor_copy(out=scw16[:, :], in_=tms[:, :]))
                prev_copy2 = 0
                for rnd in range(4):
                    rec.wt("tensor", dv, s16v)
                    rec.wt("tensor", ac, prev_copy2)
                    for c in range(4):
                        cg = rnd * 4 + c
                        src = bass.AP(scw16[:, :].tensor,
                                      scw16[:, :].offset + cg * 32,
                                      [scw16[:, :].ap[0], [1, 32]])
                        if c < 3:
                            rec.em("tensor", lambda t, c_=c, s_=src:
                                   t.transpose(ps16[0:32, c_ * 128:(c_ + 1) * 128],
                                               s_, cons16[:, B_EYE128:B_EYE128 + P]))
                        else:
                            rec.em("tensor", lambda t, c_=c, s_=src:
                                   t.transpose(ps16[0:32, c_ * 128:(c_ + 1) * 128],
                                               s_, cons16[:, B_EYE128:B_EYE128 + P])
                                   .then_inc(pe.h, 1), incs=(("pe", 1),))
                            tv = pe.inc()
                    rec.wt("scalar", pe, tv)
                    prev_copy2 = act(lambda s, r=rnd: s.copy(
                        out=scwT[0:32, r * CW:(r + 1) * CW], in_=ps16[0:32, 0:CW]))
                rec.wt("sync", ac, prev_copy2)
                scw_q = sdma(scw_d, scwT[0:32, :])
                state["scw_q"] = scw_q

                swf = WSweep([(wi["fc1T"][g], KT * P) for g in range(8)])
                for g in range(8):
                    sl, lv = swf.get(g)
                    wtd("tensor", "gpsimd", lv)
                    for n in range(NCH):
                        bi, bk = pool.acquire()
                        for kk in range(KT):
                            mm(bk[:, :], ws_tile(sl, kk), xn_t(kk, n),
                               kk == 0, kk == KT - 1, inc=(kk == KT - 1))
                        rec.wt("scalar", pe, pe.n)
                        xv = act(lambda s, b=bk, g_=g, n_=n: s.copy(
                            out=xlat[:, g_ * TL + n_ * CW:g_ * TL + (n_ + 1) * CW],
                            in_=b[:, :]))
                        pool.release(bi, ac, xv)
                    ws_done(sl, pe.n)
                state["xlat_done"] = ac.n

                shg = []
                for ih_ in range(4):
                    for g_ in range(8):
                        shg.append((wi["upT"][ih_, g_], KT * P))
                    for g2_ in range(KT):
                        shg.append((bass.AP(wi["dnT"].tensor,
                                            g2_ * P * 32 * P + ih_ * 8 * P,
                                            [[32 * P, P], [1, 8 * P]]), 8 * P))
                swsh = WSweep(shg)
                for ih in range(4):
                    up_last = 0
                    for g in range(8):
                        sl, lv = swsh.get(ih * 24 + g)
                        wtd("tensor", "gpsimd", lv)
                        for n in range(NCH):
                            bi, bk = pool.acquire()
                            for kk in range(KT):
                                mm(bk[:, :], ws_tile(sl, kk), xn_t(kk, n),
                                   kk == 0, kk == KT - 1, inc=(kk == KT - 1))
                            rec.wt("vector", pe, pe.n)
                            rec.wt("vector", ac, up_last)
                            rt = relu_t[:, (n % 2) * CW:(n % 2 + 1) * CW]
                            rv_ = dve(lambda v, b=bk, r=rt: v.tensor_scalar(
                                out=r, in0=b[:, :], scalar1=0.0, scalar2=None,
                                op0=ALU.max))
                            pool.release(bi, dv, rv_)
                            rec.wt("scalar", dv, rv_)
                            up_last = act(lambda s, g_=g, n_=n, r=rt: s.activation(
                                out=up16[:, g_ * TL + n_ * CW:
                                         g_ * TL + (n_ + 1) * CW],
                                in_=r, func=AF.Square))
                        ws_done(sl, pe.n)
                    rec.wt("tensor", ac, up_last)
                    for g2 in range(KT):
                        sl, lv = swsh.get(ih * 24 + 8 + g2)
                        wtd("tensor", "gpsimd", lv)
                        for n in range(NCH):
                            bi, bk = pool.acquire()
                            for kk in range(8):
                                mm(bk[:, :], ws_tile(sl, kk),
                                   up16[:, kk * TL + n * CW:kk * TL + (n + 1) * CW],
                                   kk == 0, kk == 7, inc=(kk == 7))
                            if dbg_split:
                                stage_acc(bk[:, :], pe.n,
                                          xdst_tile(dbg["dbg_sh"], g2, n),
                                          accum=(ih > 0))
                            else:
                                stage_acc(bk[:, :], pe.n, xdst_tile(xnext0, g2, n))
                            pool.release(bi, ac, ac.n)
                        ws_done(sl, pe.n)

        def emit_l0_p3b(xlat, xdst, carry_v, xn_last_pe):
            TH = TL // 2
            p3b = ExitStack()
            with p3b:
                pb = p3b.enter_context
                rwkvH = [pb(nc.sbuf_tensor(_uname(f"rwkvH{i}"), [P, KT * TH],
                                           BF16)) for i in range(2)]
                wrb = [pb(nc.sbuf_tensor(_uname(f"wrb{i}"), [P, TH], BF16))
                       for i in range(2)]
                urb = [pb(nc.sbuf_tensor(_uname(f"urb{i}"), [P, TH], BF16))
                       for i in range(2)]
                srb = [pb(nc.sbuf_tensor(_uname(f"srb{i}"), [P, TH], BF16))
                       for i in range(2)]
                numb = pb(nc.sbuf_tensor(_uname("numb"), [P, TH], F32))
                denb = pb(nc.sbuf_tensor(_uname("denb"), [P, TH], F32))
                tb = pb(nc.sbuf_tensor(_uname("tb"), [P, TH], F32))
                stH = pb(nc.sbuf_tensor(_uname("stH"), [P, 32], F32))
                qfence()
                swo = WSweep([(wi["woT"][g], KT * P) for g in range(KT)] * 2)
                rb_free = [[0, 0], [0, 0], [0, 0]]

                def pre_kk(H, kk):
                    rwkv = rwkvH[H]
                    j = kk % 2
                    rec.wt("sync", dv, rb_free[0][j])
                    wq_ = sdma(wrb[j][:, :],
                               sp_w[kk * P:(kk + 1) * P, H * TH:(H + 1) * TH])
                    rec.wt("sync", dv, rb_free[1][j])
                    uq_ = sdma(urb[j][:, :],
                               sp_u[kk * P:(kk + 1) * P, H * TH:(H + 1) * TH])
                    rec.wt("sync", dv, rb_free[2][j])
                    rq_ = sdma(srb[j][:, :],
                               sp_r[kk * P:(kk + 1) * P, H * TH:(H + 1) * TH])
                    dcol = cons[:, C_DEC + kk:C_DEC + kk + 1]
                    d_bc = bass.broadcast_tensor_aps(numb[:, :], dcol)[1]
                    wtd("vector", "sync", max(wq_, uq_))
                    rec.wt("vector", dv, carry_v)
                    rec.wt("vector", pe, xn_last_pe)
                    nini = (cons[:, C_CN + kk:C_CN + kk + 1] if H == 0
                            else stH[:, kk:kk + 1])
                    dini = (cons[:, C_CN + 16 + kk:C_CN + 17 + kk] if H == 0
                            else stH[:, 16 + kk:17 + kk])
                    nv = dve(lambda v, u=urb[j], db=d_bc, ini=nini:
                             v.tensor_tensor_scan(
                                 out=numb[:, :], data0=db, data1=u[:, :],
                                 initial=ini, op0=ALU.mult, op1=ALU.add))
                    if H == 0:
                        dve(lambda v, k=kk: v.tensor_copy(
                            out=stH[:, k:k + 1], in_=numb[:, TH - 1:TH]),
                            inc=False)
                    dnv = dve(lambda v, w=wrb[j], db=d_bc, ini=dini:
                              v.tensor_tensor_scan(
                                  out=denb[:, :], data0=db, data1=w[:, :],
                                  initial=ini, op0=ALU.mult, op1=ALU.add))
                    rb_free[0][j] = dnv
                    rb_free[1][j] = dnv
                    if H == 0:
                        dve(lambda v, k=kk: v.tensor_copy(
                            out=stH[:, 16 + k:17 + k], in_=denb[:, TH - 1:TH]),
                            inc=False)
                    rec.wt("scalar", dv, dnv)
                    act(lambda s: s.activation(
                        out=tb[:, :], in_=denb[:, :], func=AF.Ln,
                        bias=cons[:, C_EPS + 1:C_EPS + 2]), inc=False)
                    rcv = act(lambda s: s.activation(out=tb[:, :], in_=tb[:, :],
                                                     func=AF.Exp, scale=-1.0))
                    rec.wt("vector", ac, rcv)
                    dve(lambda v: v.tensor_tensor(out=numb[:, :], in0=numb[:, :],
                                                  in1=tb[:, :], op=ALU.mult),
                        inc=False)
                    wtd("vector", "sync", rq_)
                    rv2 = dve(lambda v, s_=srb[j], k=kk, r=rwkv:
                              v.tensor_tensor(
                                  out=r[:, k * TH:(k + 1) * TH],
                                  in0=numb[:, :], in1=s_[:, :], op=ALU.mult))
                    rb_free[2][j] = rv2

                for kk in range(KT):
                    pre_kk(0, kk)
                rwkv_done_A = dv.n
                rec.wt("tensor", dv, rwkv_done_A)
                for g in range(KT):
                    sl, lv = swo.get(g)
                    wtd("tensor", "gpsimd", lv)
                    for n in range(2):
                        bi, bk = pool.acquire()
                        for kk in range(KT):
                            mm(bk[:, :], ws_tile(sl, kk),
                               rwkvH[0][:, kk * TH + n * CW:
                                         kk * TH + (n + 1) * CW],
                               kk == 0, kk == KT - 1, inc=(kk == KT - 1))
                        stage_acc(bk[:, :], pe.n, xdst_tile(xdst, g, n))
                        pool.release(bi, ac, ac.n)
                    ws_done(sl, pe.n)
                    pre_kk(1, g)
                rwkv_done_B = dv.n
                rec.wt("tensor", dv, rwkv_done_B)
                for g in range(KT):
                    sl, lv = swo.get(KT + g)
                    wtd("tensor", "gpsimd", lv)
                    for n in range(2):
                        bi, bk = pool.acquire()
                        for kk in range(KT):
                            mm(bk[:, :], ws_tile(sl, kk),
                               rwkvH[1][:, kk * TH + n * CW:
                                         kk * TH + (n + 1) * CW],
                               kk == 0, kk == KT - 1, inc=(kk == KT - 1))
                        stage_acc(bk[:, :], pe.n, xdst_tile(xdst, g, 2 + n))
                        pool.release(bi, ac, ac.n)
                    ws_done(sl, pe.n)

        def emit_l0_p4(xlat, xdst):
            p4 = ExitStack()
            with p4:
                p4e = p4.enter_context
                ylat = p4e(nc.sbuf_tensor(_uname("ylat"), [P, 8 * TL], F32))
                h16 = p4e(nc.sbuf_tensor(_uname("h16"), [P, 8 * TL], BF16))
                scwbc = [p4e(nc.sbuf_tensor(_uname(f"scwbc{i}"), [P, TL], F32))
                         for i in range(2)]
                scwrow = [p4e(nc.sbuf_tensor(_uname(f"scwrow{i}"), [1, TL], BF16))
                          for i in range(2)]
                relu2 = p4e(nc.sbuf_tensor(_uname("relu2"), [P, 2 * CW], F32))
                qfence()
                swe = WSweep([(wi[w][ee_, m_], 8 * P) for ee_ in range(32)
                              for w in ("w1T", "w2T") for m_ in range(8)])
                swc = WSweep([(wi["fc2T"][g], 8 * P) for g in range(KT)])
                scwrow_pe = [0, 0]
                scwbc_dv = [0, 0]
                h_free_pe = 0
                y_done = {}
                sq_last = 0
                for ee in range(32):
                    j = ee % 2
                    rec.wt("sync", qX, state["scw_q"])
                    rec.wt("sync", pe, scwrow_pe[j])
                    rq_ = sdma(scwrow[j][0:1, :], scw_d[ee:ee + 1, :])
                    for n in range(NCH):
                        bi, bk = pool.acquire()
                        wtd("tensor", "sync", rq_)
                        v_ = mm(bk[:, :], onesrow16[0:1, :],
                                scwrow[j][0:1, n * CW:(n + 1) * CW], True, True,
                                inc=True)
                        rec.wt("scalar", pe, v_)
                        rec.wt("scalar", dv, scwbc_dv[j])
                        cv_ = act(lambda s, b=bk, jj=j, n_=n: s.copy(
                            out=scwbc[jj][:, n_ * CW:(n_ + 1) * CW], in_=b[:, :]))
                        pool.release(bi, ac, cv_)
                    scwrow_pe[j] = pe.n
                    scw_ready = ac.n
                    for m in range(8):
                        sl, lv = swe.get(ee * 16 + m)
                        wtd("tensor", "gpsimd", lv)
                        if ee == 0 and m == 0:
                            rec.wt("tensor", ac, state["xlat_done"])
                        for n in range(NCH):
                            bi, bk = pool.acquire()
                            for kk in range(8):
                                mm(bk[:, :], ws_tile(sl, kk),
                                   xlat[:, kk * TL + n * CW:kk * TL + (n + 1) * CW],
                                   kk == 0, kk == 7, inc=(kk == 7))
                            rt = relu2[:, (n % 2) * CW:(n % 2 + 1) * CW]
                            rec.wt("vector", pe, pe.n)
                            rec.wt("vector", ac, scw_ready)
                            rec.wt("vector", ac, sq_last)
                            hv = dve(lambda v, b=bk, jj=j, n_=n, r=rt:
                                     v.scalar_tensor_tensor(
                                         out=r, in0=b[:, :], scalar=0.0,
                                         in1=scwbc[jj][:, n_ * CW:(n_ + 1) * CW],
                                         op0=ALU.max, op1=ALU.mult))
                            pool.release(bi, dv, hv)
                            rec.wt("scalar", dv, hv)
                            rec.wt("scalar", pe, h_free_pe)
                            sq_last = act(lambda s, m_=m, n_=n, r=rt: s.activation(
                                out=h16[:, m_ * TL + n_ * CW:m_ * TL + (n_ + 1) * CW],
                                in_=r, func=AF.Square))
                        ws_done(sl, pe.n)
                    h_done = sq_last
                    scwbc_dv[j] = dv.n
                    rec.wt("tensor", ac, h_done)
                    for m2 in range(8):
                        sl, lv = swe.get(ee * 16 + 8 + m2)
                        wtd("tensor", "gpsimd", lv)
                        for n in range(NCH):
                            bi, bk = pool.acquire()
                            for kk in range(8):
                                mm(bk[:, :], ws_tile(sl, kk),
                                   h16[:, kk * TL + n * CW:kk * TL + (n + 1) * CW],
                                   kk == 0, kk == 7, inc=(kk == 7))
                            rec.wt("vector", pe, pe.n)
                            if ee == 0:
                                yv = dve(lambda v, b=bk, m_=m2, n_=n: v.tensor_copy(
                                    out=ylat[:, m_ * TL + n_ * CW:
                                             m_ * TL + (n_ + 1) * CW],
                                    in_=b[:, :]))
                            else:
                                yv = dve(lambda v, b=bk, m_=m2, n_=n: v.tensor_tensor(
                                    out=ylat[:, m_ * TL + n_ * CW:
                                             m_ * TL + (n_ + 1) * CW],
                                    in0=ylat[:, m_ * TL + n_ * CW:
                                             m_ * TL + (n_ + 1) * CW],
                                    in1=b[:, :], op=ALU.add))
                            y_done[(m2, n)] = yv
                            pool.release(bi, dv, yv)
                        ws_done(sl, pe.n)
                    h_free_pe = pe.n
                rec.wt("scalar", dv, dv.n)
                y16 = h16
                y16v = act(lambda s: s.copy(out=y16[:, :], in_=ylat[:, :]))
                rec.wt("tensor", ac, y16v)
                for g in range(KT):
                    sl, lv = swc.get(g)
                    wtd("tensor", "gpsimd", lv)
                    for n in range(NCH):
                        bi, bk = pool.acquire()
                        for kk in range(8):
                            mm(bk[:, :], ws_tile(sl, kk),
                               y16[:, kk * TL + n * CW:kk * TL + (n + 1) * CW],
                               kk == 0, kk == 7, inc=(kk == 7))
                        if dbg_split:
                            stage_acc(bk[:, :], pe.n,
                                      xdst_tile(dbg["dbg_ro"], g, n), accum=False)
                        else:
                            stage_acc(bk[:, :], pe.n, xdst_tile(xdst, g, n))
                        pool.release(bi, ac, ac.n)
                    ws_done(sl, pe.n)


        def emit_l1_routing(logits_sb, logits_done, combT):
            p3r = ExitStack()
            with p3r:
                pl = p3r.enter_context
                tmE = pl(nc.sbuf_tensor(_uname("tmE"), [P, KT * 8], F32))
                tmS = pl(nc.sbuf_tensor(_uname("tmS"), [P, KT * 8], F32))
                tmM = pl(nc.sbuf_tensor(_uname("tmM"), [P, KT * 8], F32))
                redE = pl(nc.sbuf_tensor(_uname("redE"), [P, 2 * KT], F32))
                comb16 = pl(nc.sbuf_tensor(_uname("comb16"), [P, KT * 8], BF16))
                rec.wt("tensor", ac, logits_done)
                prev_copy = 0
                for rnd in range(4):
                    bi_t, bk_t = pool.acquire()
                    rec.wt("tensor", ac, prev_copy)
                    for c in range(4):
                        cg = rnd * 4 + c
                        inc = c == 3
                        rec.em("tensor", lambda t, c_=c, cg_=cg, b=bk_t, i_=inc:
                               (t.transpose(b[:, c_ * 8:(c_ + 1) * 8],
                                            logits_sb[0:8, cg_ * P:(cg_ + 1) * P],
                                            eye32f[0:8, 0:8]).then_inc(pe.h, 1)
                                if i_ else
                                t.transpose(b[:, c_ * 8:(c_ + 1) * 8],
                                            logits_sb[0:8, cg_ * P:(cg_ + 1) * P],
                                            eye32f[0:8, 0:8])),
                               incs=((("pe", 1),) if inc else ()))
                    tv = pe.inc()
                    rec.wt("scalar", pe, tv)
                    prev_copy = act(lambda s, r=rnd, b=bk_t: s.copy(
                        out=tmE[:, r * 32:(r + 1) * 32], in_=b[:, 0:32]))
                    pool.release(bi_t, ac, prev_copy)
                rec.wt("vector", ac, prev_copy)
                tm38 = tmE[:, :].rearrange("p (c e) -> p c e", e=8)
                mE = redE[:, 0:KT].rearrange("p (c o) -> p c o", o=1)
                dve(lambda v: v.tensor_reduce(out=mE, in_=tm38, axis=AXX,
                                              op=ALU.max), inc=False)
                mEb = bass.broadcast_tensor_aps(tm38, mE)[1]
                e38 = tmS[:, :].rearrange("p (c e) -> p c e", e=8)
                sv = dve(lambda v: v.tensor_tensor(out=e38, in0=tm38, in1=mEb,
                                                   op=ALU.subtract))
                rec.wt("scalar", dv, sv)
                ev_ = act(lambda s: s.activation(out=tmS[:, :], in_=tmS[:, :],
                                                 func=AF.Exp))
                rec.wt("vector", ac, ev_)
                dve(lambda v: v.tensor_reduce(out=mE, in_=e38, axis=AXX,
                                              op=ALU.max), inc=False)
                m1b = bass.broadcast_tensor_aps(e38, mE)[1]
                msk = tmM[:, :].rearrange("p (c e) -> p c e", e=8)
                dve(lambda v: v.tensor_tensor(out=msk, in0=e38, in1=m1b,
                                              op=ALU.is_equal), inc=False)
                dve(lambda v: v.tensor_tensor(out=msk, in0=e38, in1=msk,
                                              op=ALU.mult), inc=False)
                dve(lambda v: v.tensor_tensor(out=msk, in0=e38, in1=msk,
                                              op=ALU.subtract), inc=False)
                e2 = redE[:, KT:2 * KT].rearrange("p (c o) -> p c o", o=1)
                dve(lambda v: v.tensor_reduce(out=e2, in_=msk, axis=AXX,
                                              op=ALU.max), inc=False)
                dve(lambda v: v.tensor_tensor(out=redE[:, 0:KT], in0=redE[:, 0:KT],
                                              in1=redE[:, KT:2 * KT], op=ALU.add),
                    inc=False)
                m2b = bass.broadcast_tensor_aps(e38, e2)[1]
                dve(lambda v: v.tensor_tensor(out=msk, in0=e38, in1=m2b,
                                              op=ALU.is_ge), inc=False)
                dve(lambda v: v.tensor_tensor(out=msk, in0=e38, in1=msk,
                                              op=ALU.mult), inc=False)
                dve(lambda v: v.reciprocal(out=redE[:, 0:KT], in_=redE[:, 0:KT]),
                    inc=False)
                dnb = bass.broadcast_tensor_aps(msk, mE)[1]
                dve(lambda v: v.tensor_tensor(out=msk, in0=msk, in1=dnb,
                                              op=ALU.mult), inc=False)
                al3 = cons[:, C_ALPHA:C_ALPHA + 8].rearrange("p (o e) -> p o e", o=1)
                alb = bass.broadcast_tensor_aps(msk, al3)[1]
                cbv = dve(lambda v: v.tensor_tensor(out=msk, in0=msk, in1=alb,
                                                    op=ALU.mult))
                rec.wt("vector", dv, cbv)
                c16v = dve(lambda v: v.tensor_copy(out=comb16[:, :], in_=tmM[:, :]))
                prev_copy = 0
                for rnd in range(4):
                    rec.wt("tensor", dv, c16v)
                    rec.wt("tensor", ac, prev_copy)
                    for c in range(4):
                        cg = rnd * 4 + c
                        src = bass.AP(comb16[:, :].tensor,
                                      comb16[:, :].offset + cg * 8,
                                      [comb16[:, :].ap[0], [1, 8]])
                        if c < 3:
                            rec.em("tensor", lambda t, c_=c, s_=src:
                                   t.transpose(ps16[0:8, c_ * 128:(c_ + 1) * 128],
                                               s_, cons16[:, B_EYE128:B_EYE128 + P]))
                        else:
                            rec.em("tensor", lambda t, c_=c, s_=src:
                                   t.transpose(ps16[0:8, c_ * 128:(c_ + 1) * 128],
                                               s_, cons16[:, B_EYE128:B_EYE128 + P])
                                   .then_inc(pe.h, 1), incs=(("pe", 1),))
                            tv = pe.inc()
                    rec.wt("scalar", pe, tv)
                    prev_copy = act(lambda s, r=rnd: s.copy(
                        out=combT[0:8, r * CW:(r + 1) * CW], in_=ps16[0:8, 0:CW]))
                cT_done = prev_copy
                state["cT_done"] = cT_done
                rec.wt("sync", ac, cT_done)
                state["comb_q"] = sdma(comb_d, combT[0:8, :])

        def emit_l1_p3(logits_sb, logits_done, combT, xdst, consume_carry,
                       xn_last_pe):
            TH = TL // 2
            p3L = ExitStack()
            with p3L:
                pl = p3L.enter_context
                o16H = [pl(nc.sbuf_tensor(_uname(f"o16H{i}"), [P, KT * TH], BF16))
                        for i in range(2)]
                arb = [pl(nc.sbuf_tensor(_uname(f"arb{i}"), [P, TH], BF16))
                       for i in range(2)]
                irb = [pl(nc.sbuf_tensor(_uname(f"irb{i}"), [P, TH], BF16))
                       for i in range(2)]
                grb = pl(nc.sbuf_tensor(_uname("grb"), [P, TH], BF16))
                hbuf = pl(nc.sbuf_tensor(_uname("hbuf"), [P, TH], F32))
                hs16 = pl(nc.sbuf_tensor(_uname("hs16"), [P, TH], BF16))
                rstdbc = [pl(nc.sbuf_tensor(_uname(f"rstdbc{i}"), [P, TH], F32))
                          for i in range(2)]
                tt = pl(nc.sbuf_tensor(_uname("tt"), [P, TH], F32))
                stI = pl(nc.sbuf_tensor(_uname("stI"), [P, KT], F32))
                stO = pl(nc.sbuf_tensor(_uname("stO"), [P, KT], F32))
                qfence()
                carry_v = consume_carry()
                wtd("gpsimd", "sync", state["x1cp_q"])
                swo1 = WSweep([(wi["wo1T"][g], KT * P) for g in range(KT)] * 2)
                o_done_H = [0, 0]
                pz = {"rb2": [[0, 0], [0, 0]], "g_free": 0, "ssq": None,
                      "ssq_pe": 0, "ssq_pe_prev": 0, "rstd_done": 0}

                def pass1_kk(H, kk):
                    o16 = o16H[H]
                    if kk == 0:
                        pz["ssq"] = [pool.acquire(hold=True) for _ in range(2)]
                    rb2 = pz["rb2"]
                    j = kk % 2
                    rec.wt("sync", dv, rb2[0][j])
                    aq_ = sdma(arb[j][:, :],
                               sp_w[kk * P:(kk + 1) * P, H * TH:(H + 1) * TH])
                    rec.wt("sync", dv, rb2[1][j])
                    iq_ = sdma(irb[j][:, :],
                               sp_u[kk * P:(kk + 1) * P, H * TH:(H + 1) * TH])
                    wtd("vector", "sync", max(aq_, iq_))
                    rec.wt("vector", dv, carry_v)
                    rec.wt("vector", pe, xn_last_pe)
                    ini = (cons[:, C_CN + kk:C_CN + kk + 1] if H == 0
                           else stI[:, kk:kk + 1])
                    hv = dve(lambda v, a_=arb[j], i_=irb[j], ini_=ini:
                             v.tensor_tensor_scan(
                                 out=hbuf[:, :], data0=a_[:, :], data1=i_[:, :],
                                 initial=ini_, op0=ALU.mult, op1=ALU.add))
                    if H == 0:
                        dve(lambda v, k=kk: v.tensor_copy(
                            out=stO[:, k:k + 1], in_=hbuf[:, TH - 1:TH]),
                            inc=False)
                    rb2[0][j] = hv
                    rb2[1][j] = hv
                    rec.wt("scalar", dv, hv)
                    rec.wt("scalar", pe, pz["ssq_pe_prev"])
                    sqv = act(lambda s: s.activation(out=hs16[:, :],
                                                     in_=hbuf[:, :],
                                                     func=AF.Square))
                    rec.wt("tensor", ac, sqv)
                    for n in range(2):
                        mm(pz["ssq"][n][1][0:1, :], cons16[:, B_ONES:B_ONES + 1],
                           hs16[:, n * CW:(n + 1) * CW], kk == 0, kk == KT - 1,
                           inc=(kk == KT - 1 and n == 1))
                    if kk == KT - 1:
                        pz["ssq_pe"] = pe.n
                        pz["ssq_pe_prev"] = pe.n

                def rstd_block(H):
                    ssq = pz["ssq"]
                    rec.wt("scalar", pe, pz["ssq_pe"])
                    for n in range(2):
                        act(lambda s, b=ssq[n][1], n_=n: s.activation(
                            out=rows[0:1, n_ * CW:(n_ + 1) * CW], in_=b[0:1, :],
                            func=AF.Ln, bias=cons[0:1, C_EPS + 3:C_EPS + 4]),
                            inc=False)
                    for n in range(2):
                        act(lambda s, n_=n: s.activation(
                            out=rows[0:1, n_ * CW:(n_ + 1) * CW],
                            in_=rows[0:1, n_ * CW:(n_ + 1) * CW],
                            func=AF.Exp, scale=-0.5), inc=False)
                    rec.em("scalar", lambda s: s.copy(
                        out=rows[0:1, 2 * CW:2 * CW + 1],
                        in_=rows[0:1, 0:1]).then_inc(ac.h, 1),
                        incs=(("ac", 1),))
                    expv = ac.inc()
                    rec.wt("vector", ac, expv)
                    sqrtD = float(np.sqrt(D))
                    rowsv = dve(lambda v: v.tensor_scalar(
                        out=rows[0:1, 0:2 * CW], in0=rows[0:1, 0:2 * CW],
                        scalar1=sqrtD, scalar2=None, op0=ALU.mult))
                    for n in range(2):
                        bi, bk = pool.acquire()
                        rec.wt("tensor", dv, rowsv)
                        bv = mm(bk[:, :], onesrow[0:1, :],
                                rows[0:1, n * CW:(n + 1) * CW], True, True,
                                inc=True)
                        rec.wt("scalar", pe, bv)
                        rbv = act(lambda s, b=bk, n_=n, rb=rstdbc[H]: s.copy(
                            out=rb[:, n_ * CW:(n_ + 1) * CW], in_=b[:, :]))
                        pool.release(bi, ac, rbv)
                    pz["rstd_done"] = ac.n
                    for n in range(2):
                        pool.release(ssq[n][0], ac, pz["rstd_done"])

                def pass2_kk(H, kk):
                    o16 = o16H[H]
                    rb2 = pz["rb2"]
                    j = kk % 2
                    rec.wt("sync", dv, rb2[0][j])
                    aq_ = sdma(arb[j][:, :],
                               sp_w[kk * P:(kk + 1) * P, H * TH:(H + 1) * TH])
                    rec.wt("sync", dv, rb2[1][j])
                    iq_ = sdma(irb[j][:, :],
                               sp_u[kk * P:(kk + 1) * P, H * TH:(H + 1) * TH])
                    rec.wt("sync", dv, pz["g_free"])
                    gq_ = sdma(grb[:, :],
                               sp_r[kk * P:(kk + 1) * P, H * TH:(H + 1) * TH])
                    wtd("vector", "sync", max(aq_, iq_))
                    ini = (cons[:, C_CN + kk:C_CN + kk + 1] if H == 0
                           else stI[:, kk:kk + 1])
                    hv = dve(lambda v, a_=arb[j], i_=irb[j], ini_=ini:
                             v.tensor_tensor_scan(
                                 out=hbuf[:, :], data0=a_[:, :], data1=i_[:, :],
                                 initial=ini_, op0=ALU.mult, op1=ALU.add))
                    rb2[0][j] = hv
                    rb2[1][j] = hv
                    rec.wt("vector", ac, pz["rstd_done"])
                    dve(lambda v, k=kk: v.tensor_scalar(
                        out=tt[:, :], in0=hbuf[:, :],
                        scalar1=cons[:, C_GNW + k:C_GNW + k + 1], scalar2=None,
                        op0=ALU.mult), inc=False)
                    dve(lambda v, rb=rstdbc[H]: v.tensor_tensor(
                        out=tt[:, :], in0=tt[:, :], in1=rb[:, :],
                        op=ALU.mult), inc=False)
                    wtd("vector", "sync", gq_)
                    ov = dve(lambda v, k=kk, o_=o16: v.tensor_tensor(
                        out=o_[:, k * TH:(k + 1) * TH], in0=tt[:, :],
                        in1=grb[:, :], op=ALU.mult))
                    pz["g_free"] = ov
                    if H == 0 and kk == KT - 1:
                        dve(lambda v: v.tensor_copy(out=stI[:, :], in_=stO[:, :]),
                            inc=False)

                for kk in range(KT):
                    pass1_kk(0, kk)
                rstd_block(0)
                for kk in range(KT):
                    pass2_kk(0, kk)
                o_done_H[0] = dv.n
                rec.wt("tensor", dv, o_done_H[0])
                for g in range(KT):
                    sl, lv = swo1.get(g)
                    wtd("tensor", "gpsimd", lv)
                    for n in range(2):
                        bi, bk = pool.acquire()
                        for kk in range(KT):
                            mm(bk[:, :], ws_tile(sl, kk),
                               o16H[0][:, kk * TH + n * CW:
                                        kk * TH + (n + 1) * CW],
                               kk == 0, kk == KT - 1, inc=(kk == KT - 1))
                        if dbg_split:
                            stage_acc(bk[:, :], pe.n,
                                      xdst_tile(dbg["dbg_at"], g, n),
                                      accum=False)
                        else:
                            stage_acc(bk[:, :], pe.n, xdst_tile(xdst, g, n))
                        pool.release(bi, ac, ac.n)
                    ws_done(sl, pe.n)
                    if g < 8:
                        pass1_kk(1, 2 * g)
                        pass1_kk(1, 2 * g + 1)
                        if g == 7:
                            rstd_block(1)
                    else:
                        pass2_kk(1, 2 * (g - 8))
                        pass2_kk(1, 2 * (g - 8) + 1)
                o_done_H[1] = dv.n
                rec.wt("tensor", dv, o_done_H[1])
                for g in range(KT):
                    sl, lv = swo1.get(KT + g)
                    wtd("tensor", "gpsimd", lv)
                    for n in range(2):
                        bi, bk = pool.acquire()
                        for kk in range(KT):
                            mm(bk[:, :], ws_tile(sl, kk),
                               o16H[1][:, kk * TH + n * CW:
                                        kk * TH + (n + 1) * CW],
                               kk == 0, kk == KT - 1, inc=(kk == KT - 1))
                        if dbg_split:
                            stage_acc(bk[:, :], pe.n,
                                      xdst_tile(dbg["dbg_at"], g, 2 + n),
                                      accum=False)
                        else:
                            stage_acc(bk[:, :], pe.n, xdst_tile(xdst, g, 2 + n))
                        pool.release(bi, ac, ac.n)
                    ws_done(sl, pe.n)
            state["p3_last_pe"] = pe.n


        def emit_l1_p4(combT, xdst, xn_dump_q):
            p4L = ExitStack()
            with p4L:
                q4 = p4L.enter_context
                xnr = q4(nc.sbuf_tensor(_uname("xnr"), [P, KT * TL], BF16))
                hE = q4(nc.sbuf_tensor(_uname("hE"), [P, KT * TL], BF16))
                cbbc = [q4(nc.sbuf_tensor(_uname("cbbc0"), [P, TL], F32))] * 2
                cbrow = [q4(nc.sbuf_tensor(_uname("cbrow0"), [1, TL], BF16))] * 2
                gel_t = q4(nc.sbuf_tensor(_uname("gel_t"), [P, 2 * CW], F32))
                b2sb = [q4(nc.sbuf_tensor(_uname(f"b2sb{i}"), [8, P], BF16))
                        for i in range(2)]
                qfence()
                swg = WSweep([(wi[w][ee_, m_], KT * P) for ee_ in range(8)
                              for w in ("g1fc1", "g1fc2") for m_ in range(KT)])
                rec.wt("sync", qX, xn_dump_q)
                xnr_src = bass.AP(sp_xn.tensor, 0, [[TL, P], [P * TL, KT], [1, TL]])
                xq = sdma(xnr[:, :], xnr_src)
                cbrow_pe = 0
                cbbc_dv = 0
                hE_free_pe = 0
                hlast = 0
                for ee in range(8):
                    j = ee % 2
                    rec.wt("sync", qX, state["comb_q"])
                    rec.wt("sync", pe, cbrow_pe)
                    rq_ = sdma(cbrow[j][0:1, :], comb_d[ee:ee + 1, :])
                    for n in range(NCH):
                        bi, bk = pool.acquire()
                        wtd("tensor", "sync", rq_)
                        v_ = mm(bk[:, :], onesrow16[0:1, :],
                                cbrow[j][0:1, n * CW:(n + 1) * CW], True, True,
                                inc=True)
                        rec.wt("scalar", pe, v_)
                        rec.wt("scalar", dv, cbbc_dv)
                        cv_ = act(lambda s, b=bk, jj=j, n_=n: s.copy(
                            out=cbbc[jj][:, n_ * CW:(n_ + 1) * CW], in_=b[:, :]))
                        pool.release(bi, ac, cv_)
                    cbrow_pe = pe.n
                    cb_ready = ac.n
                    for m in range(KT):
                        sl, lv = swg.get(ee * 32 + m)
                        wtd("tensor", "gpsimd", lv)
                        if ee == 0 and m == 0:
                            wtd("tensor", "sync", xq)
                        for n in range(NCH):
                            bi, bk = pool.acquire()
                            for kk in range(KT):
                                mm(bk[:, :], ws_tile(sl, kk),
                                   xnr[:, kk * TL + n * CW:kk * TL + (n + 1) * CW],
                                   kk == 0, kk == KT - 1, inc=(kk == KT - 1))
                            gt = gel_t[:, (n % 2) * CW:(n % 2 + 1) * CW]
                            rec.wt("scalar", pe, pe.n)
                            rec.wt("scalar", dv, hlast)
                            wtd("scalar", "sync", cst_q)
                            gv_ = act(lambda s, b=bk, m_=m, e_=ee, g_=gt:
                                      s.activation(out=g_, in_=b[:, :], func=AF.Gelu,
                                                   bias=b1sb[:, m_ * 8 + e_:
                                                             m_ * 8 + e_ + 1]))
                            pool.release(bi, ac, gv_)
                            rec.wt("vector", ac, gv_)
                            rec.wt("vector", ac, cb_ready)
                            rec.wt("vector", pe, hE_free_pe)
                            hlast = dve(lambda v, jj=j, m_=m, n_=n, g_=gt:
                                        v.tensor_tensor(
                                            out=hE[:, m_ * TL + n_ * CW:
                                                   m_ * TL + (n_ + 1) * CW],
                                            in0=g_,
                                            in1=cbbc[jj][:, n_ * CW:(n_ + 1) * CW],
                                            op=ALU.mult))
                        ws_done(sl, pe.n)
                    hE_done = hlast
                    cbbc_dv = hlast
                    rec.wt("tensor", dv, hE_done)
                    for m2 in range(KT):
                        sl, lv = swg.get(ee * 32 + KT + m2)
                        wtd("tensor", "gpsimd", lv)
                        for n in range(NCH):
                            bi, bk = pool.acquire()
                            for kk in range(KT):
                                mm(bk[:, :], ws_tile(sl, kk),
                                   hE[:, kk * TL + n * CW:kk * TL + (n + 1) * CW],
                                   kk == 0, kk == KT - 1, inc=(kk == KT - 1))
                            if dbg_split:
                                stage_acc(bk[:, :], pe.n,
                                          xdst_tile(dbg["dbg_ro"], m2, n),
                                          accum=(ee > 0))
                            else:
                                stage_acc(bk[:, :], pe.n, xdst_tile(xdst, m2, n))
                            pool.release(bi, ac, ac.n)
                        ws_done(sl, pe.n)
                    hE_free_pe = pe.n
                b2_pe = [0, 0]
                for m2 in range(KT):
                    jb = m2 % 2
                    rec.wt("gpsimd", pe, b2_pe[jb])
                    bq_ = gdma(b2sb[jb][0:8, :],
                               wi["b2raw"][0:8, m2 * P:(m2 + 1) * P])
                    wtd("tensor", "gpsimd", bq_)
                    rec.wt("tensor", ac, state["cT_done"])
                    for n in range(NCH):
                        bi, bk = pool.acquire()
                        bv = mm(bk[:, :], b2sb[jb][0:8, :],
                                combT[0:8, n * CW:(n + 1) * CW], True, True,
                                inc=True)
                        if dbg_split:
                            stage_acc(bk[:, :], bv, xdst_tile(dbg["dbg_ro"], m2, n))
                        else:
                            stage_acc(bk[:, :], bv, xdst_tile(xdst, m2, n))
                        pool.release(bi, ac, ac.n)
                    b2_pe[jb] = pe.n
                    if not dbg_split:
                        rowap_o = bass.AP(out.tensor, m2 * P * TL, [[TL, P], [1, TL]])
                        rowap_x = bass.AP(xdst.tensor, m2 * P * TL, [[TL, P], [1, TL]])
                        wtd("sync", "gpsimd", qW.n)
                        sdma(rowap_o, rowap_x)

        state = {}
        emit_layer(0)
        if n_layers > 1:
            emit_layer(1)
            rec.wt("gpsimd", qW, qW.n)
            rec.wt("sync", qX, qX.n)
        else:
            rec.wt("gpsimd", qW, qW.n)
            gdma(out, xnext0)
            rec.wt("gpsimd", qW, qW.n)

        assert rec.check(), "semaphore schedule deadlock"

        with nc.Block() as block:
            @block.sync
            def _(sync):
                for fn in rec.ops["sync"]:
                    fn(sync)

            @block.gpsimd
            def _(g):
                for fn in rec.ops["gpsimd"]:
                    fn(g)

            @block.tensor
            def _(t):
                for fn in rec.ops["tensor"]:
                    fn(t)

            @block.vector
            def _(v):
                for fn in rec.ops["vector"]:
                    fn(v)

            @block.scalar
            def _(s):
                for fn in rec.ops["scalar"]:
                    fn(s)

    return nc


def host_pack(inputs, n_layers=2):
    f32 = np.float32
    g = {}
    g["wkT"] = pack_lhsT(np.asarray(inputs["L0_attn_Wk"]).T)
    g["wvT"] = pack_lhsT(np.asarray(inputs["L0_attn_Wv"]).T)
    g["wrT"] = pack_lhsT(np.asarray(inputs["L0_attn_Wr"]).T)
    g["woT"] = pack_lhsT(np.asarray(inputs["L0_attn_Wo"]).T)
    g["fc1T"] = pack_lhsT(np.asarray(inputs["L0_ffn_fc1_w"]).T)
    g["fc2T"] = pack_lhsT(np.asarray(inputs["L0_ffn_fc2_w"]).T)
    g["w1T"] = np.stack([pack_lhsT(np.asarray(w).T) for w in inputs["L0_ffn_w1"]])
    g["w2T"] = np.stack([pack_lhsT(np.asarray(w).T) for w in inputs["L0_ffn_w2"]])
    upT = pack_lhsT(np.asarray(inputs["L0_ffn_shared_up_w"]).T)
    g["upT"] = np.ascontiguousarray(upT.reshape(4, 8, P, KT * P))
    g["dnT"] = pack_lhsT(np.asarray(inputs["L0_ffn_shared_down_w"]).T)
    g["gate0T"] = pack_lhsT(np.asarray(inputs["L0_ffn_gate_w"]).T, msz=32)
    g["td_T"] = np.ascontiguousarray(
        np.asarray(inputs["L0_attn_time_decay"], dtype=f32).reshape(KT, P).T)
    g["tf_T"] = np.ascontiguousarray(
        np.asarray(inputs["L0_attn_time_first"], dtype=f32).reshape(KT, P).T)
    gb = np.asarray(inputs["L0_ffn_gate_bias"], dtype=f32)
    g["gb_col"] = np.ascontiguousarray(gb.reshape(32, 1))
    g["gb_rep"] = np.ascontiguousarray(np.tile(gb.reshape(1, 32), (P, 1)))
    if n_layers > 1:
        g["wiT"] = pack_lhsT(np.asarray(inputs["L1_attn_Wi"]))
        g["wfT"] = pack_lhsT(np.asarray(inputs["L1_attn_Wf"]))
        g["wgT"] = pack_lhsT(np.asarray(inputs["L1_attn_Wg"]))
        g["wo1T"] = pack_lhsT(np.asarray(inputs["L1_attn_Wo"]))
        g["g1fc1"] = np.stack([pack_lhsT(np.asarray(w))
                               for w in inputs["L1_ffn_fc1_w"]])
        g["g1fc2"] = np.stack([pack_lhsT(np.asarray(w))
                               for w in inputs["L1_ffn_fc2_w"]])
        g["gate1T"] = pack_lhsT(np.asarray(inputs["L1_ffn_gate_w"]), msz=8)
        b1 = np.asarray(inputs["L1_ffn_fc1_b"], dtype=f32)
        g["b1P"] = np.ascontiguousarray(
            b1.T.reshape(KT, P, 8).transpose(1, 0, 2).reshape(P, KT * 8))
        g["b2raw"] = np.ascontiguousarray(np.asarray(inputs["L1_ffn_fc2_b"],
                                                     dtype=f32))
        g["gnw"] = np.ascontiguousarray(
            np.asarray(inputs["L1_attn_g_norm_weight"], dtype=f32).reshape(KT, P).T)
        g["alpha_rep"] = np.ascontiguousarray(
            np.tile(np.asarray(inputs["L1_ffn_alpha"], dtype=f32).reshape(1, 8),
                    (P, 1)))
    g["ones_col"] = np.ones((P, 1), f32)
    g["ones_row"] = np.ones((1, P), f32)
    g["eye128"] = np.eye(P, dtype=f32)
    g["eye32"] = np.eye(32, dtype=f32)
    g["epsv"] = np.tile(np.array([[1.0e-5, 1.0e-6, 1.0e-30, 1.0e-5 * D]], f32), (P, 1))
    return g


def make_in_maps(inputs, n_layers=2):
    shared = host_pack(inputs, n_layers=n_layers)
    x = np.asarray(inputs["L0_attn_x"], dtype=np.float32)
    in_maps = []
    for c in range(8):
        b, h = c // 2, c % 2
        m = dict(shared)
        m["xres0"] = np.ascontiguousarray(
            x[b, h * TL:(h + 1) * TL, :].T)
        sel = np.zeros((P, 8), np.float32)
        if h == 1:
            sel[:, 2 * b] = 1.0
        m["sel_rep"] = sel
        m["fmask"] = np.full((P, 1), 1.0 if h == 0 else 0.0, np.float32)
        in_maps.append(m)
    return in_maps


def kernel(**inputs):
    n_layers = 2
    nc = build_program(n_layers=n_layers)
    in_maps = make_in_maps(inputs, n_layers=n_layers)
    res = run_bass_kernel_spmd(nc, in_maps, core_ids=list(range(8)))
    out = np.empty((4, 4096, D), np.float32)
    for c in range(8):
        b, h = c // 2, c % 2
        out[b, h * TL:(h + 1) * TL, :] = res.results[c]["out"].T
    return out
